# version 15
# baseline (speedup 1.0000x reference)
import math
import numpy as np
import ml_dtypes
import concourse.bass as bass
import concourse.mybir as mybir
from concourse.bass_utils import run_bass_kernel_spmd

BFNP = ml_dtypes.bfloat16
F32 = mybir.dt.float32
BF16 = mybir.dt.bfloat16
I32 = mybir.dt.int32
U8 = mybir.dt.uint8
AF = mybir.ActivationFunctionType
ALU = mybir.AluOpType
AX = mybir.AxisListType.X

D = 2048
S = 2048
NB = 4
NH = 16
DH = 128
HPC = 8
TOK = 1024
IFF = 8192
NCORE = 8
CSOFT = 1.0 / math.sqrt(DH)
EPS = 1e-5

PHASES = ("A", "B0", "BC", "D", "E", "DOWN", "FIN")
DBG = False
POOL_RS = False
POOL_DEQ = False

RG_PAIRS = [[0, 1], [2, 3], [4, 5], [6, 7]]

ENGS = ("sync", "scalar", "vector", "tensor", "gpsimd")


class Sched:

    def __init__(self):
        self.ops = []
        self.keys = {}
        self.fence_deps = []

    def emit(self, eng, fn, reads=(), writes=(), kind="compute", tag=""):
        opid = len(self.ops)
        deps = set(self.fence_deps)
        for k in reads:
            e = self.keys.get(k)
            if e is not None and e["w"] is not None:
                deps.add(e["w"])
        for k in writes:
            e = self.keys.get(k)
            if e is not None:
                if e["w"] is not None:
                    deps.add(e["w"])
                deps.update(e["r"])
        deps.discard(opid)
        self.ops.append({"id": opid, "eng": eng, "fn": fn, "kind": kind,
                         "deps": deps, "tag": tag})
        for k in reads:
            self.keys.setdefault(k, {"w": None, "r": []})["r"].append(opid)
        for k in writes:
            self.keys[k] = {"w": opid, "r": []}
        return opid

    def dma(self, eng, fn, reads=(), writes=(), tag=""):
        return self.emit(eng, fn, reads, writes, kind="dma", tag=tag)

    def fence(self):
        last = {}
        for op in self.ops:
            last[op["eng"]] = op["id"]
        self.fence_deps = list(last.values())

    def finalize(self):
        for op in self.ops:
            if op["eng"] == "tensor":
                op["deps"] = {d for d in op["deps"]
                              if self.ops[d]["eng"] != "tensor"}
        for op in self.ops:
            byeng = {}
            keep = set()
            for d in op["deps"]:
                dop = self.ops[d]
                if dop["kind"] == "compute":
                    k = dop["eng"]
                    if k not in byeng or byeng[k] < d:
                        byeng[k] = d
                else:
                    keep.add(d)
            op["deps"] = keep | set(byeng.values())
        need = set()
        for op in self.ops:
            need.update(op["deps"])
        NP = 8
        cnt = {e: 0 for e in ENGS}
        dmacnt = {e: 0 for e in ENGS}
        ccn = 0
        for op in self.ops:
            if op["kind"] == "dma":
                i = dmacnt[op["eng"]]
                dmacnt[op["eng"]] += 1
                op["tok"] = ("q%s%d" % (op["eng"], i % NP), 16 * (i // NP + 1))
                op["inc"] = 16
            elif op["kind"] == "cc":
                ccn += 1
                op["tok"] = ("cc", ccn)
                op["inc"] = 1
            else:
                if op["id"] in need:
                    cnt[op["eng"]] += 1
                    op["tok"] = ("c" + op["eng"], cnt[op["eng"]])
                    op["inc"] = 1
                else:
                    op["tok"] = None
                    op["inc"] = 0
        self.finals = {}
        for e in ENGS:
            for s in range(NP):
                n = (dmacnt[e] - s + NP - 1) // NP if dmacnt[e] > s else 0
                if n:
                    self.finals["q%s%d" % (e, s)] = 16 * n
            if cnt[e]:
                self.finals["c" + e] = cnt[e]
        if ccn:
            self.finals["cc"] = ccn

    def replay(self, engname, eng, sems, vclock):
        for op in self.ops:
            if op["eng"] != engname:
                continue
            waits = {}
            for d in sorted(op["deps"]):
                tok = self.ops[d]["tok"]
                assert tok is not None, (op["tag"], self.ops[d]["tag"])
                sem, val = tok
                if waits.get(sem, 0) < val:
                    waits[sem] = val
            for sem, val in waits.items():
                if vclock.get(sem, 0) < val:
                    eng.wait_ge(sems[sem], val)
                    vclock[sem] = val
            inst = op["fn"](eng)
            if op["inc"]:
                sem, _ = op["tok"]
                inst.then_inc(sems[sem], op["inc"])
                if sem == "c" + engname:
                    pass


def build_nc():
    nc = bass.Bass("TRN2", target_bir_lowering=False, debug=False)

    xb_d = nc.dram_tensor("xb", [S, D], F32, kind="ExternalInput").ap()
    wq_d = nc.dram_tensor("wq", [D, TOK], F32, kind="ExternalInput").ap()
    wk_d = nc.dram_tensor("wk", [D, TOK], F32, kind="ExternalInput").ap()
    wv_d = nc.dram_tensor("wv", [D, TOK], F32, kind="ExternalInput").ap()
    bq_d = nc.dram_tensor("bq", [TOK], F32, kind="ExternalInput").ap()
    bk_d = nc.dram_tensor("bk", [TOK], F32, kind="ExternalInput").ap()
    bv_d = nc.dram_tensor("bv", [TOK], F32, kind="ExternalInput").ap()
    wo_d = nc.dram_tensor("wo", [D, D], F32, kind="ExternalInput").ap()
    bo_d = nc.dram_tensor("bo", [D], F32, kind="ExternalInput").ap()
    gw_d = nc.dram_tensor("gw", [IFF, D], I32, kind="ExternalInput").ap()
    gs_d = nc.dram_tensor("gs", [IFF], F32, kind="ExternalInput").ap()
    uw_d = nc.dram_tensor("uw", [IFF, D], I32, kind="ExternalInput").ap()
    us_d = nc.dram_tensor("us", [IFF], F32, kind="ExternalInput").ap()
    dw_d = nc.dram_tensor("dw", [D, IFF], I32, kind="ExternalInput").ap()
    ds_d = nc.dram_tensor("ds", [D], F32, kind="ExternalInput").ap()
    xres_d = nc.dram_tensor("xres", [TOK, D], F32, kind="ExternalInput").ap()
    ident_d = nc.dram_tensor("ident", [128, 128], BF16, kind="ExternalInput").ap()
    maskm_d = nc.dram_tensor("maskm", [128, 1024], BF16, kind="ExternalInput").ap()
    msku8_d = nc.dram_tensor("msku8", [128, 1024], U8, kind="ExternalInput").ap()
    onesrowf_d = nc.dram_tensor("onesrowf", [1, 128], F32, kind="ExternalInput").ap()
    onesrowb_d = nc.dram_tensor("onesrowb", [1, 128], BF16, kind="ExternalInput").ap()
    epscol_d = nc.dram_tensor("epscol", [128, 1], F32, kind="ExternalInput").ap()
    zerorow_d = nc.dram_tensor("zerorow", [1, 128], BF16, kind="ExternalInput").ap()
    onescolb_d = nc.dram_tensor("onescolb", [128, 1], BF16, kind="ExternalInput").ap()

    yout_d = nc.dram_tensor("yout", [TOK, D], F32, kind="ExternalOutput").ap()

    h1s_d = nc.dram_tensor("h1s", [TOK, D], F32, kind="Internal").ap()
    zs_d = nc.dram_tensor("zs", [TOK, D], F32, kind="Internal").ap()
    ccA_in = nc.dram_tensor("ccA_in", [4, 128, S], BF16, kind="Internal")
    ccB_in = nc.dram_tensor("ccB_in", [4, 128, S], BF16, kind="Internal")
    ccA_out = nc.dram_tensor("ccA_out", [2, 4, 128, S], BF16, kind="Internal")
    ccB_out = nc.dram_tensor("ccB_out", [2, 4, 128, S], BF16, kind="Internal")

    SBB = 16512
    off = [SBB]

    def sb(name, shape, dtype, at=None):
        nonlocal_off = off[0] if at is None else at
        t = nc.alloc_sbuf_tensor_at(name, shape, dtype, offset=nonlocal_off)
        if at is None:
            nbytes = shape[1] * {F32: 4, BF16: 2, I32: 4, U8: 1}[dtype]
            off[0] = (nonlocal_off + nbytes + 31) & ~31
        return t

    ident = sb("ident", [128, 128], BF16)
    maskm = sb("maskm", [128, 1024], BF16)
    msku8 = sb("msku8", [128, 1024], U8)
    onesrowf = sb("onesrowf", [1, 128], F32)
    onesrowb = sb("onesrowb", [1, 128], BF16)
    epscol = sb("epscol", [128, 1], F32)
    zerorow = sb("zerorow", [1, 128], BF16)
    onescolb = sb("onescolb", [128, 1], BF16)
    bqcol = [sb(f"bqcol{i}", [128, 1], F32) for i in range(2)]
    bkcol = [sb(f"bkcol{i}", [128, 1], F32) for i in range(2)]
    gscol = [sb(f"gscol{i}", [128, 1], F32) for i in range(2)]
    uscol = [sb(f"uscol{i}", [128, 1], F32) for i in range(2)]
    dscol = [sb(f"dscol{i}", [128, 1], F32) for i in range(2)]
    bvhi = sb("bvhi", [1, 1024], BF16)
    bvlo = sb("bvlo", [1, 1024], BF16)
    c_s1 = [sb(f"cs1_{i}", [128, 1], F32) for i in range(2)]
    c_m = [sb(f"cm_{i}", [128, 1], F32) for i in range(2)]
    c_v = [sb(f"cv_{i}", [128, 1], F32) for i in range(2)]
    c_t = [sb(f"ct_{i}", [128, 1], F32) for i in range(2)]
    c_r = [sb(f"cr_{i}", [128, 1], F32) for i in range(2)]
    c_s2 = [sb(f"cs2_{i}", [128, 1], F32) for i in range(2)]
    assert off[0] <= SBB + 9216, off[0]

    SCR1 = sb("SCR1", [128, 2048], F32, at=SBB + 9216)
    SCR1_BF = nc.alloc_sbuf_tensor_at("SCR1bf", [128, 4096], BF16, offset=SBB + 9216)
    SCR2 = sb("SCR2", [128, 2048], F32, at=SBB + 17408)
    SCR2_BF = nc.alloc_sbuf_tensor_at("SCR2bf", [128, 4096], BF16, offset=SBB + 17408)
    SCR3 = sb("SCR3", [128, 2048], F32, at=SBB + 25600)
    SCR3_BF = nc.alloc_sbuf_tensor_at("SCR3bf", [128, 4096], BF16, offset=SBB + 25600)
    STRM_BF = nc.alloc_sbuf_tensor_at("STRMbf", [128, 8192], BF16, offset=SBB + 33792)
    STRM_F32 = nc.alloc_sbuf_tensor_at("STRMf32", [128, 4096], F32, offset=SBB + 33792)
    XBUF = nc.alloc_sbuf_tensor_at("XBUF", [128, 32768], BF16, offset=SBB + 50176)
    VBUF = nc.alloc_sbuf_tensor_at("VBUF", [128, 16384], BF16, offset=SBB + 115712)
    HBUF = nc.alloc_sbuf_tensor_at("HBUF", [128, 16384], BF16, offset=SBB + 148480)
    HBUF_F32 = nc.alloc_sbuf_tensor_at("HBUFf32", [128, 8192], F32, offset=SBB + 148480)
    yoff = [SBB + 181248]

    def yb(name, shape, dtype):
        t = nc.alloc_sbuf_tensor_at(name, shape, dtype, offset=yoff[0])
        nbytes = shape[1] * {F32: 4, BF16: 2}[dtype] * (1 if shape[0] > 1 else 1)
        yoff[0] = (yoff[0] + nbytes + 31) & ~31
        return t

    pT = [yb(f"pT{i}", [128, 512], BF16) for i in range(5)]
    rs_part = yb("rs", [128, 512], F32)
    bc_sb = yb("bc", [128, 512], F32)
    rr_sb = yb("rr", [1, 512], F32)
    otr = [yb(f"otr{i}", [128, 512], BF16) for i in range(2)]
    bvf32 = yb("bvf32", [1, 1024], F32)
    ccraw = [nc.alloc_sbuf_tensor_at(f"ccraw{i}", [128, 2048], BF16,
                                     offset=SBB + 181248 + i * 4096) for i in range(3)]
    boutf32 = nc.alloc_sbuf_tensor_at("boutf32", [1, 2048], F32, offset=SBB + 181248 + 12288)
    h1bf = [nc.alloc_sbuf_tensor_at(f"h1bf{i}", [128, 2048], BF16,
                                    offset=SBB + 181248 + 12288 + 8192 + i * 4096) for i in range(2)]
    YAT = nc.alloc_sbuf_tensor_at("YAT", [128, 14336], BF16, offset=SBB + 181248)

    P = [nc.alloc_psum_tensor(f"P{i}", [128, 512], F32) for i in range(8)]
    PB0, PB1, PS0, PS1, PO0, PO1, P6, P7 = P

    def aT_view(ic):
        if ic < 32:
            return XBUF[:, ic * 1024:(ic + 1) * 1024]
        if ic < 48:
            return VBUF[:, (ic - 32) * 1024:(ic - 31) * 1024]
        if ic < 62:
            return YAT[:, (ic - 48) * 1024:(ic - 47) * 1024]
        return SCR1_BF[:, 2048 + (ic - 62) * 1024: 2048 + (ic - 61) * 1024]

    def aT_key(ic):
        return f"aT.{ic}"

    sch = Sched()
    E = sch.emit
    DMA = sch.dma

    DMA("sync", lambda e: e.dma_start(out=ident[:, :], in_=ident_d[:, :]),
        writes=["ident"], tag="c.ident")
    DMA("sync", lambda e: e.dma_start(out=maskm[:, :], in_=maskm_d[:, :]),
        writes=["maskm"], tag="c.maskm")
    DMA("sync", lambda e: e.dma_start(out=msku8[:, :], in_=msku8_d[:, :]),
        writes=["msku8"], tag="c.msk")
    DMA("sync", lambda e: e.dma_start(out=onesrowf[:, :], in_=onesrowf_d[:, :]),
        writes=["onesrowf"], tag="c.orf")
    DMA("sync", lambda e: e.dma_start(out=onesrowb[:, :], in_=onesrowb_d[:, :]),
        writes=["onesrowb"], tag="c.orb")
    DMA("sync", lambda e: e.dma_start(out=epscol[:, :], in_=epscol_d[:, :]),
        writes=["epscol"], tag="c.eps")
    DMA("sync", lambda e: e.dma_start(out=zerorow[:, :], in_=zerorow_d[:, :]),
        writes=["zerorow"], tag="c.zr")
    DMA("sync", lambda e: e.dma_start(out=onescolb[:, :], in_=onescolb_d[:, :]),
        writes=["onescolb"], tag="c.ocb")

    def xrow(par):
        return STRM_BF[:, par * 2048:(par + 1) * 2048]

    if "A" in PHASES:
        for tt in range(16):
            par = tt % 2
            DMA("gpsimd",
                lambda e, par=par, tt=tt: e.dma_start(
                    out=xrow(par)[:, :], in_=xb_d[tt * 128:(tt + 1) * 128, :]),
                reads=[], writes=[f"xrow.{par}"], tag=f"A.dma{tt}")
            for g in range(4):
                pk = ("PB0", "PB1")[g % 2]
                pt = (PB0, PB1)[g % 2]
                for k4 in range(4):
                    k = g * 4 + k4
                    E("tensor",
                      lambda e, par=par, k=k, k4=k4, pt=pt: e.matmul(
                          pt[:, k4 * 128:(k4 + 1) * 128],
                          xrow(par)[:, k * 128:(k + 1) * 128],
                          ident[:, :], start=True, stop=True),
                      reads=[f"xrow.{par}", "ident"], writes=[pk],
                      tag=f"A.T{tt}.{k}")
                for k4 in range(4):
                    k = g * 4 + k4
                    E("vector",
                      lambda e, k=k, k4=k4, tt=tt, pt=pt: e.tensor_copy(
                          out=XBUF[:, k * 2048 + tt * 128: k * 2048 + (tt + 1) * 128],
                          in_=pt[:, k4 * 128:(k4 + 1) * 128]),
                      reads=[pk], writes=[f"xT.{k}.{tt}"], tag=f"A.E{tt}.{k}")

    def xT(k):
        return XBUF[:, k * 2048:(k + 1) * 2048]

    def xT_keys(rng=range(16)):
        return [f"xT.{k}.{t}" for k in rng for t in range(16)]

    if "B0" in PHASES:
        DMA("sync", lambda e: e.dma_start(
            out=bvf32[:, :], in_=bv_d.rearrange("(p j) -> p j", p=1)),
            writes=["bvf32"], tag="B0.bv")
        E("scalar", lambda e: e.copy(out=bvhi[:, :], in_=bvf32[:, :]),
          reads=["bvf32"], writes=["bvhi"], tag="B0.bvhi")
        E("vector", lambda e: e.tensor_tensor(
            out=bvf32[:, :], in0=bvf32[:, :], in1=bvhi[:, :], op=ALU.subtract),
          reads=["bvf32", "bvhi"], writes=["bvf32"], tag="B0.bvsub")
        E("scalar", lambda e: e.copy(out=bvlo[:, :], in_=bvf32[:, :]),
          reads=["bvf32"], writes=["bvlo"], tag="B0.bvlo")
        DMA("gpsimd", lambda e: e.dma_start(
            out=HBUF[:, :].rearrange("p (k m) -> p k m", k=16),
            in_=wv_d.rearrange("(k p) m -> p k m", p=128)),
            writes=["wvbf"], tag="B0.wv")
        for tt in range(16):
            for vc in range(2):
                pk = ("P6", "P7")[(tt * 2 + vc) % 2]
                pt = (P6, P7)[(tt * 2 + vc) % 2]
                for k in range(16):
                    E("tensor",
                      lambda e, k=k, tt=tt, vc=vc, pt=pt: e.matmul(
                          pt[:, :],
                          XBUF[:, k * 2048 + tt * 128: k * 2048 + (tt + 1) * 128],
                          HBUF[:, k * 1024 + vc * 512: k * 1024 + (vc + 1) * 512],
                          start=(k == 0), stop=False),
                      reads=[f"xT.{k}.{tt}", "wvbf"], writes=[pk],
                      tag=f"B0.mm{tt}.{vc}.{k}")
                E("tensor", lambda e, vc=vc, pt=pt: e.matmul(
                    pt[:, :], onesrowb[:, :],
                    bvhi[:, vc * 512:(vc + 1) * 512], start=False, stop=False),
                  reads=["onesrowb", "bvhi"], writes=[pk], tag=f"B0.bh{tt}.{vc}")
                E("tensor", lambda e, vc=vc, pt=pt: e.matmul(
                    pt[:, :], onesrowb[:, :],
                    bvlo[:, vc * 512:(vc + 1) * 512], start=False, stop=True),
                  reads=["onesrowb", "bvlo"], writes=[pk], tag=f"B0.bl{tt}.{vc}")
                E("vector", lambda e, tt=tt, vc=vc, pt=pt: e.tensor_copy(
                    out=VBUF[:, tt * 1024 + vc * 512: tt * 1024 + (vc + 1) * 512],
                    in_=pt[:, :]),
                  reads=[pk], writes=[f"V.{tt}.{vc}"], tag=f"B0.ev{tt}.{vc}")

    def QT(par):
        return HBUF[:, par * 2048:(par + 1) * 2048]

    def KT(par):
        return HBUF[:, 4096 + par * 2048: 4096 + (par + 1) * 2048]

    def wqh(par):
        return STRM_BF[:, par * 2048:(par + 1) * 2048]

    def wkh(par):
        return STRM_BF[:, 4096 + par * 2048: 4096 + (par + 1) * 2048]

    if "BC" in PHASES:
        sch.fence()
        ptc = [0]

        def emit_wdma(h):
            par = h % 2
            DMA("gpsimd", lambda e, h=h, par=par: e.dma_start(
                out=wqh(par)[:, :].rearrange("p (k j) -> p k j", k=16),
                in_=wq_d.rearrange("(k p) m -> p k m", p=128)[:, :, h * 128:(h + 1) * 128]),
                writes=[f"wqh.{par}"], tag=f"B.wq{h}")
            DMA("gpsimd", lambda e, h=h, par=par: e.dma_start(
                out=wkh(par)[:, :].rearrange("p (k j) -> p k j", k=16),
                in_=wk_d.rearrange("(k p) m -> p k m", p=128)[:, :, h * 128:(h + 1) * 128]),
                writes=[f"wkh.{par}"], tag=f"B.wk{h}")
            DMA("sync", lambda e, h=h, par=par: e.dma_start(
                out=bqcol[par][:, :],
                in_=bq_d.rearrange("(p j) -> p j", j=1)[h * 128:(h + 1) * 128, :]),
                writes=[f"bqc.{par}"], tag=f"B.bq{h}")
            DMA("sync", lambda e, h=h, par=par: e.dma_start(
                out=bkcol[par][:, :],
                in_=bk_d.rearrange("(p j) -> p j", j=1)[h * 128:(h + 1) * 128, :]),
                writes=[f"bkc.{par}"], tag=f"B.bk{h}")

        emit_wdma(0)
        pend_norm = [None]
        rspar = [0]
        for h in range(HPC):
            par = h % 2
            if h + 1 < HPC:
                emit_wdma(h + 1)
            for (wfn, wkey, dstfn, dkey, bcol, bkey) in (
                    (wqh, f"wqh.{par}", QT, f"QT.{par}", bqcol[par], f"bqc.{par}"),
                    (wkh, f"wkh.{par}", KT, f"KT.{par}", bkcol[par], f"bkc.{par}")):
                for tc in range(4):
                    pk = ("PB0", "PB1")[tc % 2]
                    pt = (PB0, PB1)[tc % 2]
                    for k in range(16):
                        E("tensor", lambda e, k=k, tc=tc, par=par, wfn=wfn, pt=pt: e.matmul(
                            pt[:, :], wfn(par)[:, k * 128:(k + 1) * 128],
                            XBUF[:, k * 2048 + tc * 512: k * 2048 + (tc + 1) * 512],
                            start=(k == 0), stop=(k == 15)),
                          reads=[wkey, f"xT.{k}.{4 * tc}", f"xT.{k}.{4 * tc + 1}",
                                 f"xT.{k}.{4 * tc + 2}", f"xT.{k}.{4 * tc + 3}"],
                          writes=[pk], tag=f"B.qk{h}.{tc}.{k}")
                    E("scalar", lambda e, tc=tc, par=par, dstfn=dstfn, bcol=bcol, pt=pt:
                      e.activation(out=dstfn(par)[:, tc * 512:(tc + 1) * 512],
                                   in_=pt[:, :], func=AF.Identity,
                                   bias=bcol[:, 0:1], scale=1.0),
                      reads=[pk, bkey], writes=[dkey], tag=f"B.ev{h}.{tc}")
                    if tc == 0 and pend_norm[0] is not None:
                        pend_norm[0]()
                        pend_norm[0] = None

            for qg in range(4):
                pok = ("PO0", "PO1")[qg % 2]
                pot = (PO0, PO1)[qg % 2]
                nkc = qg * 4 + 4
                psr = ("PS0", "PS1", "PB0", "PB1")
                pst_r = (PS0, PS1, PB0, PB1)
                ptis = [(ptc[0] + kc) % 5 for kc in range(nkc)]
                ptc[0] += nkc

                def emit_smm(kc, qg=qg, par=par, psr=psr, pst_r=pst_r):
                    E("tensor", lambda e, kc=kc, qg=qg, par=par,
                      pst=pst_r[kc % 4]: e.matmul(
                        pst[:, :], KT(par)[:, kc * 128:(kc + 1) * 128],
                        QT(par)[:, qg * 512:(qg + 1) * 512],
                        start=True, stop=True),
                      reads=[f"KT.{par}", f"QT.{par}"], writes=[psr[kc % 4]],
                      tag=f"C.s{h}.{qg}.{kc}")

                for kk in range(min(3, nkc)):
                    emit_smm(kk)
                for kc in range(nkc):
                    if kc == 3 and pend_norm[0] is not None:
                        pend_norm[0]()
                        pend_norm[0] = None
                    pti = ptis[kc]
                    psk = psr[kc % 4]
                    pst = pst_r[kc % 4]
                    E("scalar", lambda e, pti=pti, pst=pst: e.activation(
                        out=pT[pti][:, :], in_=pst[:, :], func=AF.Exp, scale=CSOFT),
                      reads=[psk], writes=[f"pT.{pti}"], tag=f"C.e{h}.{qg}.{kc}")
                    if kc >= qg * 4:
                        j = kc - qg * 4
                        E("vector", lambda e, pti=pti, j=j: e.tensor_tensor(
                            out=pT[pti][:, :], in0=pT[pti][:, :],
                            in1=maskm[:, 512 - j * 128: 1024 - j * 128], op=ALU.mult),
                          reads=[f"pT.{pti}", "maskm"], writes=[f"pT.{pti}"],
                          tag=f"C.m{h}.{qg}.{kc}")
                    if kc + 3 < nkc:
                        emit_smm(kc + 3)
                    E("tensor", lambda e, kc=kc, h=h, pti=pti, pot=pot, nkc=nkc: e.matmul(
                        pot[:, :], VBUF[:, kc * 1024 + h * 128: kc * 1024 + (h + 1) * 128],
                        pT[pti][:, :], start=(kc == 0), stop=(kc == nkc - 1)),
                      reads=[f"V.{kc}.0", f"V.{kc}.1", f"pT.{pti}"], writes=[pok],
                      tag=f"C.pv{h}.{qg}.{kc}")
                    E("tensor", lambda e, kc=kc, pti=pti, nkc=nkc,
                      rp=rspar[0]: e.matmul(
                        P6[32 * rp:32 * rp + 1, :], onescolb[:, :], pT[pti][:, :],
                        start=(kc == 0), stop=(kc == nkc - 1)),
                      reads=["onescolb", f"pT.{pti}"], writes=[f"P6r.{rspar[0]}"],
                      tag=f"C.rs{h}.{qg}.{kc}")

                def norm(h=h, qg=qg, pok=pok, pot=pot, rp=None):
                    rp = rspar[0]
                    E("vector", lambda e, rp=rp: e.reciprocal(
                        out=rr_sb[:, :], in_=P6[32 * rp:32 * rp + 1, :]),
                      reads=[f"P6r.{rp}"], writes=["rr"], tag=f"C.rcp{h}.{qg}")
                    E("tensor", lambda e: e.matmul(
                        P7[:, :], onesrowf[:, :], rr_sb[:, :], start=True, stop=True),
                      reads=["onesrowf", "rr"], writes=["P7b"], tag=f"C.bc{h}.{qg}")
                    E("vector", lambda e: e.tensor_copy(out=bc_sb[:, :], in_=P7[:, :]),
                      reads=["P7b"], writes=["bcs"], tag=f"C.bce{h}.{qg}")
                    E("vector", lambda e, qg=qg, pot=pot: e.tensor_tensor(
                        out=otr[qg % 2][:, :], in0=pot[:, :], in1=bc_sb[:, :], op=ALU.mult),
                      reads=[pok, "bcs"], writes=[f"otr.{qg % 2}"], tag=f"C.ot{h}.{qg}")
                    cc_in = (ccA_in if h < 4 else ccB_in)
                    DMA("scalar", lambda e, h=h, qg=qg, cc_in=cc_in: e.dma_start(
                        out=cc_in.ap()[h % 4, :, qg * 512:(qg + 1) * 512],
                        in_=otr[qg % 2][:, :]),
                        reads=[f"otr.{qg % 2}"],
                        writes=[f"d:cc{'A' if h < 4 else 'B'}.{h % 4}.{qg}"],
                        tag=f"C.cw{h}.{qg}")

                def norm_fix(norm=norm, rp=rspar[0]):
                    def g():
                        sv = rspar[0]
                        rspar[0] = rp
                        norm()
                        rspar[0] = sv
                    return g
                pend_norm[0] = norm_fix()
                rspar[0] ^= 1
            if h == 3:
                pend_norm[0]()
                pend_norm[0] = None
            if h == 3:
                E("gpsimd", lambda e: e.collective_compute(
                    "AllGather", ALU.bypass, replica_groups=RG_PAIRS,
                    ins=[ccA_in.ap().opt()], outs=[ccA_out.ap().opt()]),
                    reads=[f"d:ccA.{i}.{q}" for i in range(4) for q in range(4)],
                    writes=["d:ccoA"], kind="cc", tag="CCA")
            if h == 7:
                pend_norm[0]()
                pend_norm[0] = None
                E("gpsimd", lambda e: e.collective_compute(
                    "AllGather", ALU.bypass, replica_groups=RG_PAIRS,
                    ins=[ccB_in.ap().opt()], outs=[ccB_out.ap().opt()]),
                    reads=[f"d:ccB.{i}.{q}" for i in range(4) for q in range(4)],
                    writes=["d:ccoB"], kind="cc", tag="CCB")

    def h1T_view():
        return HBUF

    def wob(g):
        return XBUF[:, g * 2048:(g + 1) * 2048]

    def olh(g):
        return VBUF[:, g * 1024:(g + 1) * 1024]

    def xresb(par):
        return STRM_F32[:, par * 2048:(par + 1) * 2048]

    if "D" in PHASES:
        sch.fence()
        DMA("sync", lambda e: e.dma_start(
            out=boutf32[:, :], in_=bo_d.rearrange("(p j) -> p j", p=1)),
            writes=["bout"], tag="D.bo")
        for g in range(16):
            DMA("gpsimd", lambda e, g=g: e.dma_start(
                out=XBUF[:, g * 2048:(g + 1) * 2048],
                in_=wo_d[g * 128:(g + 1) * 128, :]),
                writes=[f"wob.{g}"], tag=f"D.wo{g}")
        GORD = [0, 1, 2, 3, 8, 9, 10, 11, 4, 5, 6, 7, 12, 13, 14, 15]
        for g in GORD:
            blk, hh = g // 8, g % 8
            cco = ccA_out if hh < 4 else ccB_out
            DMA("sync", lambda e, blk=blk, hh=hh, cco=cco, g=g: e.dma_start(
                out=ccraw[g % 3][:, :], in_=cco.ap()[blk, hh % 4, :, :]),
                reads=["d:ccoA" if hh < 4 else "d:ccoB"],
                writes=[f"ccraw.{g % 3}"], tag=f"D.cr{g}")
            E("vector", lambda e, g=g: e.select(
                out=olh(g)[:, :], mask=msku8[:, :],
                on_true=ccraw[g % 3][:, 1024:2048],
                on_false=ccraw[g % 3][:, 0:1024]),
              reads=[f"ccraw.{g % 3}", "msku8"], writes=[f"olh.{g}"],
              tag=f"D.sel{g}")
        def emit_h1T(tc):
            for g4 in range(4):
                pk = ("PB0", "PB1")[g4 % 2]
                pt = (PB0, PB1)[g4 % 2]
                for k4 in range(4):
                    k = g4 * 4 + k4
                    E("tensor", lambda e, k=k, k4=k4, tc=tc, pt=pt: e.matmul(
                        pt[:, k4 * 128:(k4 + 1) * 128],
                        h1bf[tc % 2][:, k * 128:(k + 1) * 128], ident[:, :],
                        start=True, stop=True),
                      reads=[f"h1bfk.{tc % 2}", "ident"], writes=[pk],
                      tag=f"D.hT{tc}.{k}")
                for k4 in range(4):
                    k = g4 * 4 + k4
                    E("vector", lambda e, k=k, k4=k4, tc=tc, pt=pt: e.tensor_copy(
                        out=HBUF[:, k * 1024 + tc * 128: k * 1024 + (tc + 1) * 128],
                        in_=pt[:, k4 * 128:(k4 + 1) * 128]),
                      reads=[pk], writes=[f"h1T.{k}.{tc}"], tag=f"D.hE{tc}.{k}")

        for tc in range(8):
            par = tc % 2
            DMA("sync", lambda e, tc=tc, par=par: e.dma_start(
                out=xresb(par)[:, :], in_=xres_d[tc * 128:(tc + 1) * 128, :]),
                writes=[f"xresb.{par}"], tag=f"D.xr{tc}")
            for ng in range(4):
                pk = ("PS0", "PS1", "PO0", "PO1")[ng]
                pt = (PS0, PS1, PO0, PO1)[ng]
                for gx, g in enumerate(GORD):
                    E("tensor", lambda e, g=g, tc=tc, ng=ng, pt=pt, gx=gx: e.matmul(
                        pt[:, :], olh(g)[:, tc * 128:(tc + 1) * 128],
                        wob(g)[:, ng * 512:(ng + 1) * 512],
                        start=(gx == 0), stop=False),
                      reads=[f"olh.{g}", f"wob.{g}"], writes=[pk],
                      tag=f"D.mm{tc}.{ng}.{g}")
                E("tensor", lambda e, ng=ng, pt=pt: e.matmul(
                    pt[:, :], onesrowf[:, :], boutf32[:, ng * 512:(ng + 1) * 512],
                    start=False, stop=True),
                  reads=["onesrowf", "bout"], writes=[pk], tag=f"D.bo{tc}.{ng}")
                E("vector", lambda e, ng=ng, par=par, pt=pt: e.tensor_tensor(
                    out=SCR1[:, ng * 512:(ng + 1) * 512], in0=pt[:, :],
                    in1=xresb(par)[:, ng * 512:(ng + 1) * 512], op=ALU.add),
                  reads=[pk, f"xresb.{par}"], writes=[f"hpre.{ng}"],
                  tag=f"D.hp{tc}.{ng}")
            hprek = [f"hpre.{i}" for i in range(4)]
            _ln(sch, par, SCR1, SCR3, SCR2, hprek, ["sq1"], ["h1f"],
                c_s1, c_m, c_v, c_t, c_r, c_s2, f"D.ln{tc}", epscol)
            E("scalar", lambda e, tc=tc: e.copy(out=h1bf[tc % 2][:, :], in_=SCR2[:, :]),
              reads=["h1f"], writes=[f"h1bfk.{tc % 2}"], tag=f"D.h1b{tc}")
            DMA("scalar", lambda e, tc=tc: e.dma_start(
                out=h1s_d[tc * 128:(tc + 1) * 128, :], in_=SCR2[:, :]),
                reads=["h1f"], writes=[f"d:h1s.{tc}"], tag=f"D.h1s{tc}")
            if tc > 0:
                emit_h1T(tc - 1)
        emit_h1T(7)

    def wgnat(par):
        return STRM_BF[:, par * 2048:(par + 1) * 2048]

    def wunat(par):
        return STRM_BF[:, 4096 + par * 2048: 4096 + (par + 1) * 2048]

    def wgt_sb():
        return SCR3_BF[:, 0:2048]

    def wut_sb():
        return SCR3_BF[:, 2048:4096]

    def silu_scr():
        return SCR1[:, 0:1024]

    if "E" in PHASES:
        sch.fence()
        h1Tkeys = [f"h1T.{k}.{t}" for k in range(16) for t in range(8)] + ["reg:H"]

        def wgt_pb(par):
            return SCR3_BF[:, par * 2048:(par + 1) * 2048]

        def wut_pb(par):
            return SCR2_BF[:, par * 2048:(par + 1) * 2048]

        def e_dma(ic):
            par = ic % 2
            DMA("gpsimd", lambda e, ic=ic, par=par: e.dma_start(
                out=wgnat(par)[:, :], in_=gw_d[ic * 128:(ic + 1) * 128, :]),
                writes=[f"wgn.{par}"], tag=f"E.gw{ic}")
            DMA("gpsimd", lambda e, ic=ic, par=par: e.dma_start(
                out=wunat(par)[:, :], in_=uw_d[ic * 128:(ic + 1) * 128, :]),
                writes=[f"wun.{par}"], tag=f"E.uw{ic}")
            DMA("sync", lambda e, ic=ic, par=par: e.dma_start(
                out=gscol[par][:, :],
                in_=gs_d.rearrange("(p j) -> p j", j=1)[ic * 128:(ic + 1) * 128, :]),
                writes=[f"gsc.{par}"], tag=f"E.gs{ic}")
            DMA("sync", lambda e, ic=ic, par=par: e.dma_start(
                out=uscol[par][:, :],
                in_=us_d.rearrange("(p j) -> p j", j=1)[ic * 128:(ic + 1) * 128, :]),
                writes=[f"usc.{par}"], tag=f"E.us{ic}")

        def e_deq(ic):
            par = ic % 2
            deqeng = "gpsimd" if POOL_DEQ else "vector"
            E(deqeng, lambda e, par=par: e.tensor_scalar(
                out=wgnat(par)[:, :], in0=wgnat(par)[:, :], scalar1=128.0,
                scalar2=gscol[par][:, 0:1], op0=ALU.subtract, op1=ALU.mult),
              reads=[f"wgn.{par}", f"gsc.{par}"], writes=[f"wgn.{par}"],
              tag=f"E.dq{ic}")
            E(deqeng, lambda e, par=par: e.tensor_scalar(
                out=wunat(par)[:, :], in0=wunat(par)[:, :], scalar1=128.0,
                scalar2=uscol[par][:, 0:1], op0=ALU.subtract, op1=ALU.mult),
              reads=[f"wun.{par}", f"usc.{par}"], writes=[f"wun.{par}"],
              tag=f"E.du{ic}")

        def e_trans(ic):
            par = ic % 2
            gg = 0
            for (nat, natk, dst, dstk) in ((wgnat, f"wgn.{par}", wgt_pb, "wgt"),
                                           (wunat, f"wun.{par}", wut_pb, "wut")):
                for g4 in range(4):
                    pk = ("PB0", "PB1", "P6e", "P7e")[gg % 4]
                    pt = (PB0, PB1, P6, P7)[gg % 4]
                    for k4 in range(4):
                        k = g4 * 4 + k4
                        E("tensor", lambda e, k=k, k4=k4, par=par, nat=nat, pt=pt:
                          e.matmul(pt[:, k4 * 128:(k4 + 1) * 128],
                                   nat(par)[:, k * 128:(k + 1) * 128],
                                   ident[:, :], start=True, stop=True),
                          reads=[natk, "ident"], writes=[pk], tag=f"E.T{ic}.{k}")
                    if gg % 2 == 0:
                        E("vector", lambda e, g4=g4, dst=dst, par=par, pt=pt: e.tensor_copy(
                            out=dst(par)[:, g4 * 512:(g4 + 1) * 512], in_=pt[:, :]),
                          reads=[pk], writes=[f"{dstk}.{par}.{g4}"], tag=f"E.TE{ic}.{g4}")
                    else:
                        E("scalar", lambda e, g4=g4, dst=dst, par=par, pt=pt: e.copy(
                            out=dst(par)[:, g4 * 512:(g4 + 1) * 512], in_=pt[:, :]),
                          reads=[pk], writes=[f"{dstk}.{par}.{g4}"], tag=f"E.TEs{ic}.{g4}")
                    gg += 1

        e_dma(0)
        e_deq(0)
        e_trans(0)
        for ic in range(64):
            par = ic % 2
            if ic + 1 < 64:
                e_dma(ic + 1)
                e_deq(ic + 1)
                e_trans(ic + 1)
            wgtk = [f"wgt.{par}.{i}" for i in range(4)]
            wutk = [f"wut.{par}.{i}" for i in range(4)]
            for c in range(16):
                for th in range(2):
                    E("tensor", lambda e, c=c, th=th, par=par: e.matmul(
                        (PS0, PS1)[th][:, :], wgt_pb(par)[:, c * 128:(c + 1) * 128],
                        HBUF[:, c * 1024 + th * 512: c * 1024 + (th + 1) * 512],
                        start=(c == 0), stop=(c == 15)),
                      reads=wgtk + h1Tkeys, writes=[("PS0", "PS1")[th]],
                      tag=f"E.g{ic}.{c}.{th}")
                for th in range(2):
                    E("tensor", lambda e, c=c, th=th, par=par: e.matmul(
                        (PO0, PO1)[th][:, :], wut_pb(par)[:, c * 128:(c + 1) * 128],
                        HBUF[:, c * 1024 + th * 512: c * 1024 + (th + 1) * 512],
                        start=(c == 0), stop=(c == 15)),
                      reads=wutk + h1Tkeys, writes=[("PO0", "PO1")[th]],
                      tag=f"E.u{ic}.{c}.{th}")
            for th in range(2):
                E("scalar", lambda e, th=th: e.activation(
                    out=silu_scr()[:, th * 512:(th + 1) * 512],
                    in_=(PS0, PS1)[th][:, :], func=AF.Silu),
                  reads=[("PS0", "PS1")[th]], writes=[f"silu.{th}"],
                  tag=f"E.si{ic}.{th}")
                E("vector", lambda e, ic=ic, th=th: e.tensor_tensor(
                    out=aT_view(ic)[:, th * 512:(th + 1) * 512],
                    in0=silu_scr()[:, th * 512:(th + 1) * 512],
                    in1=(PO0, PO1)[th][:, :], op=ALU.mult),
                  reads=[f"silu.{th}", ("PO0", "PO1")[th]],
                  writes=[aT_key(ic)], tag=f"E.a{ic}.{th}")

    def wdnat(hslot, par):
        return STRM_BF[:, (hslot * 2 + par) * 2048:(hslot * 2 + par + 1) * 2048]

    def wdT(par):
        return SCR3_BF[:, par * 256:(par + 1) * 256]

    def ztile(par):
        return SCR2[:, par * 256:(par + 1) * 256]

    def h1sl(par):
        return SCR2[:, 1024 + par * 256: 1024 + (par + 1) * 256]

    if "DOWN" in PHASES:
        sch.fence()
        if DBG:
            for ic in range(64):
                DMA("scalar", lambda e, ic=ic: e.dma_start(
                    out=dbg_at[ic * 128:(ic + 1) * 128, :], in_=aT_view(ic)[:, :]),
                    reads=[aT_key(ic)], writes=[f"d:dbgat.{ic}"], tag=f"dbgat{ic}")

        def w_dma(hc, iq):
            par = iq % 2
            for hs in range(2):
                H = hc * 2 + hs
                DMA("gpsimd", lambda e, H=H, iq=iq, hs=hs, par=par: e.dma_start(
                    out=wdnat(hs, par)[:, :],
                    in_=dw_d[H * 128:(H + 1) * 128, iq * 2048:(iq + 1) * 2048]),
                    writes=[f"wdn.{hs}.{par}"], tag=f"W.dw{hc}.{iq}.{hs}")

        def w_deq(hc, iq):
            par = iq % 2
            for hs in range(2):
                E("gpsimd" if POOL_DEQ else "vector", lambda e, hs=hs, par=par: e.tensor_scalar(
                    out=wdnat(hs, par)[:, :], in0=wdnat(hs, par)[:, :],
                    scalar1=128.0, scalar2=dscol[hs][:, 0:1],
                    op0=ALU.subtract, op1=ALU.mult),
                  reads=[f"wdn.{hs}.{par}", f"dsc.{hs}"],
                  writes=[f"wdn.{hs}.{par}"], tag=f"W.dq{hc}.{iq}.{hs}")

        def w_trans(hc, gi):
            iq, c = gi // 16, gi % 16
            par, cp = iq % 2, gi % 2
            pbt = (PB0, PB1)[cp]
            for hs in range(2):
                E("tensor", lambda e, c=c, hs=hs, par=par, pbt=pbt: e.matmul(
                    pbt[:, hs * 128:(hs + 1) * 128],
                    wdnat(hs, par)[:, c * 128:(c + 1) * 128],
                    ident[:, :], start=True, stop=True),
                  reads=[f"wdn.{hs}.{par}", "ident"],
                  writes=[f"PBd.{cp}"], tag=f"W.T{hc}.{gi}.{hs}")
            E("vector", lambda e, cp=cp, pbt=pbt: e.tensor_copy(
                out=wdT(cp)[:, :], in_=pbt[:, 0:256]),
              reads=[f"PBd.{cp}"], writes=[f"wdT.{cp}"], tag=f"W.TE{hc}.{gi}")

        for hc in range(8):
            for bi in range(4):
                pt = (PS0, PS1, PO0, PO1)[bi]
                pn = ("PS0", "PS1", "PO0", "PO1")[bi]
                E("tensor", lambda e, pt=pt: e.matmul(
                    pt[:, :], zerorow[:, :], maskm[0:1, 0:512],
                    start=True, stop=True),
                  reads=["zerorow", "maskm"],
                  writes=[pn + ".0", pn + ".1"], tag=f"W.z0{hc}.{bi}")
            for hs in range(2):
                H = hc * 2 + hs
                DMA("sync", lambda e, H=H, hs=hs: e.dma_start(
                    out=dscol[hs][:, :],
                    in_=ds_d.rearrange("(p j) -> p j", j=1)[H * 128:(H + 1) * 128, :]),
                    writes=[f"dsc.{hs}"], tag=f"W.ds{hc}.{hs}")
            w_dma(hc, 0)
            w_deq(hc, 0)
            w_trans(hc, 0)
            for gi in range(64):
                iq, c = gi // 16, gi % 16
                cp = gi % 2
                if c == 0 and iq + 1 < 4:
                    w_dma(hc, iq + 1)
                if gi + 1 < 64:
                    if (gi + 1) % 16 == 0:
                        w_deq(hc, iq + 1)
                    w_trans(hc, gi + 1)
                for tc in range(8):
                    pt = (PS0, PS1, PO0, PO1)[tc // 2]
                    pk = ("PS0", "PS1", "PO0", "PO1")[tc // 2] + f".{tc % 2}"
                    E("tensor", lambda e, gi=gi, tc=tc, cp=cp, pt=pt: e.matmul(
                        pt[:, (tc % 2) * 256:(tc % 2 + 1) * 256],
                        aT_view(gi)[:, tc * 128:(tc + 1) * 128],
                        wdT(cp)[:, :], start=False, stop=(gi == 63)),
                      reads=[aT_key(gi), f"wdT.{cp}"], writes=[pk],
                      tag=f"W.y{hc}.{gi}.{tc}")
            for tc in range(8):
                par = tc % 2
                pt = (PS0, PS1, PO0, PO1)[tc // 2]
                pk = ("PS0", "PS1", "PO0", "PO1")[tc // 2] + f".{tc % 2}"
                DMA("sync", lambda e, tc=tc, hc=hc, par=par: e.dma_start(
                    out=h1sl(par)[:, :],
                    in_=h1s_d[tc * 128:(tc + 1) * 128, hc * 256:(hc + 1) * 256]),
                    writes=[f"h1sl.{par}"], tag=f"W.h1{hc}.{tc}")
                E("vector", lambda e, tc=tc, par=par, pt=pt: e.tensor_tensor(
                    out=ztile(par)[:, :], in0=pt[:, (tc % 2) * 256:(tc % 2 + 1) * 256],
                    in1=h1sl(par)[:, :], op=ALU.add),
                  reads=[pk, f"h1sl.{par}"], writes=[f"zt.{par}"],
                  tag=f"W.z{hc}.{tc}")
                DMA("scalar", lambda e, tc=tc, hc=hc, par=par: e.dma_start(
                    out=zs_d[tc * 128:(tc + 1) * 128, hc * 256:(hc + 1) * 256],
                    in_=ztile(par)[:, :]),
                    reads=[f"zt.{par}"], writes=[f"d:zs.{tc}.{hc}"],
                    tag=f"W.zs{hc}.{tc}")

    if DBG:
        dbg_at = nc.dram_tensor("dbg_at", [IFF, TOK], BF16, kind="ExternalOutput").ap()
        dbg_h1 = nc.dram_tensor("dbg_h1", [TOK, D], F32, kind="ExternalOutput").ap()
        dbg_z = nc.dram_tensor("dbg_z", [TOK, D], F32, kind="ExternalOutput").ap()
        dbg_ot = nc.dram_tensor("dbg_ot", [2, 4, 128, S], BF16, kind="ExternalOutput").ap()

    if "FIN" in PHASES:
        zrow_f = [HBUF_F32[:, i * 2048:(i + 1) * 2048] for i in range(2)]
        sq_f = HBUF_F32[:, 2 * 2048:3 * 2048]
        yrow_f = HBUF_F32[:, 3 * 2048:4 * 2048]
        for tc in range(8):
            par = tc % 2
            DMA("sync", lambda e, tc=tc, par=par: e.dma_start(
                out=zrow_f[par][:, :], in_=zs_d[tc * 128:(tc + 1) * 128, :]),
                reads=[f"d:zs.{tc}.{hcv}" for hcv in range(8)],
                writes=[f"zrow.{par}", "reg:H"], tag=f"F.z{tc}")
            _ln(sch, par, zrow_f[par], sq_f, yrow_f, [f"zrow.{par}"], ["sqf"], ["yrow"],
                c_s1, c_m, c_v, c_t, c_r, c_s2, f"F.ln{tc}", epscol)
            DMA("scalar", lambda e, tc=tc: e.dma_start(
                out=yout_d[tc * 128:(tc + 1) * 128, :], in_=yrow_f[:, :]),
                reads=["yrow"], writes=[f"d:y.{tc}"], tag=f"F.y{tc}")

    sch.finalize()

    import contextlib
    with contextlib.ExitStack() as ctx:
        sems = {}
        for name in sch.finals:
            sems[name] = ctx.enter_context(nc.semaphore(name))
        block = ctx.enter_context(nc.Block())
        vclocks = {e: {} for e in ENGS}

        @block.sync
        def _(eng):
            sch.replay("sync", eng, sems, vclocks["sync"])
            for name, val in sorted(sch.finals.items()):
                if vclocks["sync"].get(name, 0) < val:
                    eng.wait_ge(sems[name], val)

        @block.scalar
        def _(eng):
            sch.replay("scalar", eng, sems, vclocks["scalar"])

        @block.vector
        def _(eng):
            sch.replay("vector", eng, sems, vclocks["vector"])

        @block.tensor
        def _(eng):
            sch.replay("tensor", eng, sems, vclocks["tensor"])

        @block.gpsimd
        def _(eng):
            sch.replay("gpsimd", eng, sems, vclocks["gpsimd"])

    return nc


def _ln(sch, par, row, sq, out, rowkeys, sqkeys, outkeys,
        c_s1, c_m, c_v, c_t, c_r, c_s2, tag, epscol=None):
    E = sch.emit
    N = 2048
    E("vector", lambda e: e.reduce_sum(out=c_s1[par][:, :], in_=row[:, :], axis=AX),
      reads=rowkeys, writes=[f"cs1.{par}"], tag=tag + ".s1")
    E("vector", lambda e: e.tensor_scalar(
        out=c_m[par][:, :], in0=c_s1[par][:, :], scalar1=1.0 / N, scalar2=None,
        op0=ALU.mult),
      reads=[f"cs1.{par}"], writes=[f"cm.{par}"], tag=tag + ".m")
    E("vector", lambda e: e.tensor_tensor(
        out=sq[:, :], in0=row[:, :], in1=row[:, :], op=ALU.mult),
      reads=rowkeys, writes=sqkeys, tag=tag + ".sq")
    E("vector", lambda e: e.reduce_sum(out=c_s2[par][:, :], in_=sq[:, :], axis=AX),
      reads=sqkeys, writes=[f"cs2.{par}"], tag=tag + ".s2")
    E("vector", lambda e: e.tensor_scalar(
        out=c_v[par][:, :], in0=c_s2[par][:, :], scalar1=1.0 / N, scalar2=None,
        op0=ALU.mult),
      reads=[f"cs2.{par}"], writes=[f"cv.{par}"], tag=tag + ".v")
    E("vector", lambda e: e.tensor_tensor(
        out=c_t[par][:, :], in0=c_m[par][:, :], in1=c_m[par][:, :], op=ALU.mult),
      reads=[f"cm.{par}"], writes=[f"ct.{par}"], tag=tag + ".m2")
    E("vector", lambda e: e.tensor_tensor(
        out=c_v[par][:, :], in0=c_v[par][:, :], in1=c_t[par][:, :], op=ALU.subtract),
      reads=[f"cv.{par}", f"ct.{par}"], writes=[f"cv.{par}"], tag=tag + ".var")
    E("scalar", lambda e: e.activation(
        out=c_t[par][:, :], in_=c_v[par][:, :], func=AF.Ln,
        bias=epscol[:, 0:1]),
      reads=[f"cv.{par}", "epscol"], writes=[f"ct.{par}"], tag=tag + ".ln")
    E("scalar", lambda e: e.activation(
        out=c_r[par][:, :], in_=c_t[par][:, :], func=AF.Exp, scale=-0.5),
      reads=[f"ct.{par}"], writes=[f"cr.{par}"], tag=tag + ".ex")
    E("vector", lambda e: e.tensor_scalar(
        out=out[:, :], in0=row[:, :], scalar1=c_m[par][:, 0:1],
        scalar2=c_r[par][:, 0:1], op0=ALU.subtract, op1=ALU.mult),
      reads=rowkeys + [f"cm.{par}", f"cr.{par}"], writes=outkeys, tag=tag + ".o")


_NC_CACHE = [None]
_LAST_RES = [None]


def kernel(L0_attn_x, L0_attn_w_in, L0_attn_b_in, L0_attn_w_out, L0_attn_b_out,
           L0_ffn_gate_w, L0_ffn_gate_scale, L0_ffn_up_w, L0_ffn_up_scale,
           L0_ffn_down_w, L0_ffn_down_scale):
    x = np.ascontiguousarray(np.asarray(L0_attn_x, dtype=np.float32))
    w_in = np.asarray(L0_attn_w_in, dtype=np.float32)
    b_in = np.asarray(L0_attn_b_in, dtype=np.float32)
    w_out = np.ascontiguousarray(np.asarray(L0_attn_w_out, dtype=np.float32))
    b_out = np.ascontiguousarray(np.asarray(L0_attn_b_out, dtype=np.float32))
    gw = np.ascontiguousarray(np.asarray(L0_ffn_gate_w, dtype=np.int32))
    gs = np.ascontiguousarray(np.asarray(L0_ffn_gate_scale, dtype=np.float32))
    uw = np.ascontiguousarray(np.asarray(L0_ffn_up_w, dtype=np.int32))
    us = np.ascontiguousarray(np.asarray(L0_ffn_up_scale, dtype=np.float32))
    dw = np.ascontiguousarray(np.asarray(L0_ffn_down_w, dtype=np.int32))
    ds = np.ascontiguousarray(np.asarray(L0_ffn_down_scale, dtype=np.float32))

    ident = np.eye(128, dtype=np.float32).astype(BFNP)
    tt, pp = np.meshgrid(np.arange(1024), np.arange(128), indexing="xy")
    maskm = (np.arange(1024)[None, :] >= 512 + np.arange(128)[:, None]).astype(np.float32).astype(BFNP)
    onesrowf = np.ones((1, 128), np.float32)
    onesrowb = np.ones((1, 128), np.float32).astype(BFNP)

    in_maps = []
    for c in range(NCORE):
        b, hg = c // 2, c % 2
        cs = hg * 1024
        m = {
            "xb": x[b],
            "wq": np.ascontiguousarray(w_in[:, cs:cs + 1024]),
            "wk": np.ascontiguousarray(w_in[:, 2048 + cs:2048 + cs + 1024]),
            "wv": np.ascontiguousarray(w_in[:, 4096 + cs:4096 + cs + 1024]),
            "bq": np.ascontiguousarray(b_in[cs:cs + 1024]),
            "bk": np.ascontiguousarray(b_in[2048 + cs:2048 + cs + 1024]),
            "bv": np.ascontiguousarray(b_in[4096 + cs:4096 + cs + 1024]),
            "wo": w_out, "bo": b_out,
            "gw": gw, "gs": gs, "uw": uw, "us": us, "dw": dw, "ds": ds,
            "xres": np.ascontiguousarray(x[b, hg * 1024:(hg + 1) * 1024]),
            "ident": ident, "maskm": maskm,
            "msku8": np.full((128, 1024), hg, np.uint8),
            "onesrowf": onesrowf, "onesrowb": onesrowb,
            "epscol": np.full((128, 1), EPS, np.float32),
            "zerorow": np.zeros((1, 128), np.float32).astype(BFNP),
            "onescolb": np.ones((128, 1), np.float32).astype(BFNP),
        }
        in_maps.append(m)

    if _NC_CACHE[0] is None:
        _NC_CACHE[0] = build_nc()
    nc = _NC_CACHE[0]
    res = run_bass_kernel_spmd(nc, in_maps, core_ids=list(range(NCORE)))
    _LAST_RES[0] = res

    out = np.empty((NB, S, D), np.float32)
    for c in range(NCORE):
        b, hg = c // 2, c % 2
        out[b, hg * 1024:(hg + 1) * 1024] = res.results[c]["yout"]
    return out


# version 17
# speedup vs baseline: 1.0725x; 1.0725x over previous
import math
import numpy as np
import ml_dtypes
import concourse.bass as bass
import concourse.mybir as mybir
from concourse.bass_utils import run_bass_kernel_spmd

BFNP = ml_dtypes.bfloat16
F32 = mybir.dt.float32
BF16 = mybir.dt.bfloat16
I32 = mybir.dt.int32
U8 = mybir.dt.uint8
AF = mybir.ActivationFunctionType
ALU = mybir.AluOpType
AX = mybir.AxisListType.X

D = 2048
S = 2048
NB = 4
NH = 16
DH = 128
HPC = 8
TOK = 1024
IFF = 8192
NCORE = 8
CSOFT = 1.0 / math.sqrt(DH)
EPS = 1e-5

PHASES = ("A", "B0", "BC", "D", "E", "DOWN", "FIN")
DBG = False
POOL_RS = False
POOL_DEQ = False

RG_PAIRS = [[0, 1], [2, 3], [4, 5], [6, 7]]

ENGS = ("sync", "scalar", "vector", "tensor", "gpsimd")


class Sched:

    def __init__(self):
        self.ops = []
        self.keys = {}
        self.fence_deps = []

    def emit(self, eng, fn, reads=(), writes=(), kind="compute", tag=""):
        opid = len(self.ops)
        deps = set(self.fence_deps)
        for k in reads:
            e = self.keys.get(k)
            if e is not None and e["w"] is not None:
                deps.add(e["w"])
        for k in writes:
            e = self.keys.get(k)
            if e is not None:
                if e["w"] is not None:
                    deps.add(e["w"])
                deps.update(e["r"])
        deps.discard(opid)
        self.ops.append({"id": opid, "eng": eng, "fn": fn, "kind": kind,
                         "deps": deps, "tag": tag})
        for k in reads:
            self.keys.setdefault(k, {"w": None, "r": []})["r"].append(opid)
        for k in writes:
            self.keys[k] = {"w": opid, "r": []}
        return opid

    def dma(self, eng, fn, reads=(), writes=(), tag=""):
        return self.emit(eng, fn, reads, writes, kind="dma", tag=tag)

    def fence(self, exclude=()):
        last = {}
        for op in self.ops:
            if op["eng"] in exclude and op["kind"] == "cc":
                continue
            last[(op["eng"], op["kind"] == "cc")] = op["id"]
        self.fence_deps = list(last.values())

    def finalize(self):
        for op in self.ops:
            if op["eng"] == "tensor":
                op["deps"] = {d for d in op["deps"]
                              if self.ops[d]["eng"] != "tensor"}
        for op in self.ops:
            byeng = {}
            keep = set()
            for d in op["deps"]:
                dop = self.ops[d]
                if dop["kind"] == "compute":
                    k = dop["eng"]
                    if k not in byeng or byeng[k] < d:
                        byeng[k] = d
                else:
                    keep.add(d)
            op["deps"] = keep | set(byeng.values())
        need = set()
        for op in self.ops:
            need.update(op["deps"])
        NP = 8
        cnt = {e: 0 for e in ENGS}
        dmacnt = {e: 0 for e in ENGS}
        ccn = 0
        for op in self.ops:
            if op["kind"] == "dma":
                i = dmacnt[op["eng"]]
                dmacnt[op["eng"]] += 1
                op["tok"] = ("q%s%d" % (op["eng"], i % NP), 16 * (i // NP + 1))
                op["inc"] = 16
            elif op["kind"] == "cc":
                ccn += 1
                op["tok"] = ("cc", ccn)
                op["inc"] = 1
            else:
                if op["id"] in need:
                    cnt[op["eng"]] += 1
                    op["tok"] = ("c" + op["eng"], cnt[op["eng"]])
                    op["inc"] = 1
                else:
                    op["tok"] = None
                    op["inc"] = 0
        self.finals = {}
        for e in ENGS:
            for s in range(NP):
                n = (dmacnt[e] - s + NP - 1) // NP if dmacnt[e] > s else 0
                if n:
                    self.finals["q%s%d" % (e, s)] = 16 * n
            if cnt[e]:
                self.finals["c" + e] = cnt[e]
        if ccn:
            self.finals["cc"] = ccn

    def replay(self, engname, eng, sems, vclock):
        for op in self.ops:
            if op["eng"] != engname:
                continue
            waits = {}
            for d in sorted(op["deps"]):
                tok = self.ops[d]["tok"]
                assert tok is not None, (op["tag"], self.ops[d]["tag"])
                sem, val = tok
                if waits.get(sem, 0) < val:
                    waits[sem] = val
            for sem, val in waits.items():
                if vclock.get(sem, 0) < val:
                    eng.wait_ge(sems[sem], val)
                    vclock[sem] = val
            inst = op["fn"](eng)
            if op["inc"]:
                sem, _ = op["tok"]
                inst.then_inc(sems[sem], op["inc"])
                if sem == "c" + engname:
                    pass


def build_nc():
    nc = bass.Bass("TRN2", target_bir_lowering=False, debug=False)

    xb_d = nc.dram_tensor("xb", [S, D], F32, kind="ExternalInput").ap()
    wq_d = nc.dram_tensor("wq", [D, TOK], F32, kind="ExternalInput").ap()
    wk_d = nc.dram_tensor("wk", [D, TOK], F32, kind="ExternalInput").ap()
    wv_d = nc.dram_tensor("wv", [D, TOK], F32, kind="ExternalInput").ap()
    bq_d = nc.dram_tensor("bq", [TOK], F32, kind="ExternalInput").ap()
    bk_d = nc.dram_tensor("bk", [TOK], F32, kind="ExternalInput").ap()
    bv_d = nc.dram_tensor("bv", [TOK], F32, kind="ExternalInput").ap()
    wo_d = nc.dram_tensor("wo", [D, D], F32, kind="ExternalInput").ap()
    bo_d = nc.dram_tensor("bo", [D], F32, kind="ExternalInput").ap()
    gw_d = nc.dram_tensor("gw", [IFF, D], I32, kind="ExternalInput").ap()
    gs_d = nc.dram_tensor("gs", [IFF], F32, kind="ExternalInput").ap()
    uw_d = nc.dram_tensor("uw", [IFF, D], I32, kind="ExternalInput").ap()
    us_d = nc.dram_tensor("us", [IFF], F32, kind="ExternalInput").ap()
    dw_d = nc.dram_tensor("dw", [D, IFF], I32, kind="ExternalInput").ap()
    ds_d = nc.dram_tensor("ds", [D], F32, kind="ExternalInput").ap()
    xres_d = nc.dram_tensor("xres", [TOK, D], F32, kind="ExternalInput").ap()
    ident_d = nc.dram_tensor("ident", [128, 128], BF16, kind="ExternalInput").ap()
    maskm_d = nc.dram_tensor("maskm", [128, 1024], BF16, kind="ExternalInput").ap()
    msku8_d = nc.dram_tensor("msku8", [128, 1024], U8, kind="ExternalInput").ap()
    onesrowf_d = nc.dram_tensor("onesrowf", [1, 128], F32, kind="ExternalInput").ap()
    onesrowb_d = nc.dram_tensor("onesrowb", [1, 128], BF16, kind="ExternalInput").ap()
    epscol_d = nc.dram_tensor("epscol", [128, 1], F32, kind="ExternalInput").ap()
    zerorow_d = nc.dram_tensor("zerorow", [1, 128], BF16, kind="ExternalInput").ap()
    onescolb_d = nc.dram_tensor("onescolb", [128, 1], BF16, kind="ExternalInput").ap()

    yout_d = nc.dram_tensor("yout", [TOK, D], F32, kind="ExternalOutput").ap()

    h1s_d = nc.dram_tensor("h1s", [TOK, D], F32, kind="Internal").ap()
    zs_d = nc.dram_tensor("zs", [TOK, D], F32, kind="Internal").ap()
    ccA_in = nc.dram_tensor("ccA_in", [4, 128, S], BF16, kind="Internal")
    ccB_in = nc.dram_tensor("ccB_in", [4, 128, S], BF16, kind="Internal")
    ccA_out = nc.dram_tensor("ccA_out", [2, 4, 128, S], BF16, kind="Internal")
    ccB_out = nc.dram_tensor("ccB_out", [2, 4, 128, S], BF16, kind="Internal")

    SBB = 16512
    off = [SBB]

    def sb(name, shape, dtype, at=None):
        nonlocal_off = off[0] if at is None else at
        t = nc.alloc_sbuf_tensor_at(name, shape, dtype, offset=nonlocal_off)
        if at is None:
            nbytes = shape[1] * {F32: 4, BF16: 2, I32: 4, U8: 1}[dtype]
            off[0] = (nonlocal_off + nbytes + 31) & ~31
        return t

    ident = sb("ident", [128, 128], BF16)
    maskm = sb("maskm", [128, 1024], BF16)
    msku8 = sb("msku8", [128, 1024], U8)
    onesrowf = sb("onesrowf", [1, 128], F32)
    onesrowb = sb("onesrowb", [1, 128], BF16)
    epscol = sb("epscol", [128, 1], F32)
    zerorow = sb("zerorow", [1, 128], BF16)
    onescolb = sb("onescolb", [128, 1], BF16)
    bqcol = [sb(f"bqcol{i}", [128, 1], F32) for i in range(2)]
    bkcol = [sb(f"bkcol{i}", [128, 1], F32) for i in range(2)]
    gscol = [sb(f"gscol{i}", [128, 1], F32) for i in range(2)]
    uscol = [sb(f"uscol{i}", [128, 1], F32) for i in range(2)]
    dscol = [sb(f"dscol{i}", [128, 1], F32) for i in range(2)]
    bvhi = sb("bvhi", [1, 1024], BF16)
    bvlo = sb("bvlo", [1, 1024], BF16)
    c_s1 = [sb(f"cs1_{i}", [128, 1], F32) for i in range(2)]
    c_m = [sb(f"cm_{i}", [128, 1], F32) for i in range(2)]
    c_v = [sb(f"cv_{i}", [128, 1], F32) for i in range(2)]
    c_t = [sb(f"ct_{i}", [128, 1], F32) for i in range(2)]
    c_r = [sb(f"cr_{i}", [128, 1], F32) for i in range(2)]
    c_s2 = [sb(f"cs2_{i}", [128, 1], F32) for i in range(2)]
    assert off[0] <= SBB + 9216, off[0]

    SCR1 = sb("SCR1", [128, 2048], F32, at=SBB + 9216)
    SCR1_BF = nc.alloc_sbuf_tensor_at("SCR1bf", [128, 4096], BF16, offset=SBB + 9216)
    SCR2 = sb("SCR2", [128, 2048], F32, at=SBB + 17408)
    SCR2_BF = nc.alloc_sbuf_tensor_at("SCR2bf", [128, 4096], BF16, offset=SBB + 17408)
    SCR3 = sb("SCR3", [128, 2048], F32, at=SBB + 25600)
    SCR3_BF = nc.alloc_sbuf_tensor_at("SCR3bf", [128, 4096], BF16, offset=SBB + 25600)
    STRM_BF = nc.alloc_sbuf_tensor_at("STRMbf", [128, 8192], BF16, offset=SBB + 33792)
    STRM_F32 = nc.alloc_sbuf_tensor_at("STRMf32", [128, 4096], F32, offset=SBB + 33792)
    XBUF = nc.alloc_sbuf_tensor_at("XBUF", [128, 32768], BF16, offset=SBB + 50176)
    VBUF = nc.alloc_sbuf_tensor_at("VBUF", [128, 16384], BF16, offset=SBB + 115712)
    HBUF = nc.alloc_sbuf_tensor_at("HBUF", [128, 16384], BF16, offset=SBB + 148480)
    HBUF_F32 = nc.alloc_sbuf_tensor_at("HBUFf32", [128, 8192], F32, offset=SBB + 148480)
    yoff = [SBB + 181248]

    def yb(name, shape, dtype):
        t = nc.alloc_sbuf_tensor_at(name, shape, dtype, offset=yoff[0])
        nbytes = shape[1] * {F32: 4, BF16: 2}[dtype] * (1 if shape[0] > 1 else 1)
        yoff[0] = (yoff[0] + nbytes + 31) & ~31
        return t

    pT = [yb(f"pT{i}", [128, 512], BF16) for i in range(5)]
    rs_part = yb("rs", [128, 512], F32)
    bc_sb = yb("bc", [128, 512], F32)
    rr_sb = yb("rr", [1, 512], F32)
    otr = [yb(f"otr{i}", [128, 512], BF16) for i in range(2)]
    bvf32 = yb("bvf32", [1, 1024], F32)
    ccraw = [nc.alloc_sbuf_tensor_at(f"ccraw{i}", [128, 2048], BF16,
                                     offset=SBB + 181248 + i * 4096) for i in range(3)]
    boutf32 = nc.alloc_sbuf_tensor_at("boutf32", [1, 2048], F32, offset=SBB + 181248 + 12288)
    h1bf = [nc.alloc_sbuf_tensor_at(f"h1bf{i}", [128, 2048], BF16,
                                    offset=SBB + 181248 + 12288 + 8192 + i * 4096) for i in range(2)]
    YAT = nc.alloc_sbuf_tensor_at("YAT", [128, 14336], BF16, offset=SBB + 181248)

    P = [nc.alloc_psum_tensor(f"P{i}", [128, 512], F32) for i in range(8)]
    PB0, PB1, PS0, PS1, PO0, PO1, P6, P7 = P

    def aT_view(ic):
        if ic < 32:
            return XBUF[:, ic * 1024:(ic + 1) * 1024]
        if ic < 48:
            return VBUF[:, (ic - 32) * 1024:(ic - 31) * 1024]
        if ic < 62:
            return YAT[:, (ic - 48) * 1024:(ic - 47) * 1024]
        return SCR1_BF[:, 2048 + (ic - 62) * 1024: 2048 + (ic - 61) * 1024]

    def aT_key(ic):
        return f"aT.{ic}"

    sch = Sched()
    E = sch.emit
    DMA = sch.dma

    DMA("sync", lambda e: e.dma_start(out=ident[:, :], in_=ident_d[:, :]),
        writes=["ident"], tag="c.ident")
    DMA("sync", lambda e: e.dma_start(out=maskm[:, :], in_=maskm_d[:, :]),
        writes=["maskm"], tag="c.maskm")
    DMA("sync", lambda e: e.dma_start(out=msku8[:, :], in_=msku8_d[:, :]),
        writes=["msku8"], tag="c.msk")
    DMA("sync", lambda e: e.dma_start(out=onesrowf[:, :], in_=onesrowf_d[:, :]),
        writes=["onesrowf"], tag="c.orf")
    DMA("sync", lambda e: e.dma_start(out=onesrowb[:, :], in_=onesrowb_d[:, :]),
        writes=["onesrowb"], tag="c.orb")
    DMA("sync", lambda e: e.dma_start(out=epscol[:, :], in_=epscol_d[:, :]),
        writes=["epscol"], tag="c.eps")
    DMA("sync", lambda e: e.dma_start(out=zerorow[:, :], in_=zerorow_d[:, :]),
        writes=["zerorow"], tag="c.zr")
    DMA("sync", lambda e: e.dma_start(out=onescolb[:, :], in_=onescolb_d[:, :]),
        writes=["onescolb"], tag="c.ocb")

    def xrow(par):
        return STRM_BF[:, par * 2048:(par + 1) * 2048]

    if "A" in PHASES:
        for tt in range(16):
            par = tt % 2
            DMA("gpsimd",
                lambda e, par=par, tt=tt: e.dma_start(
                    out=xrow(par)[:, :], in_=xb_d[tt * 128:(tt + 1) * 128, :]),
                reads=[], writes=[f"xrow.{par}"], tag=f"A.dma{tt}")
            for g in range(4):
                pk = ("PB0", "PB1")[g % 2]
                pt = (PB0, PB1)[g % 2]
                for k4 in range(4):
                    k = g * 4 + k4
                    E("tensor",
                      lambda e, par=par, k=k, k4=k4, pt=pt: e.matmul(
                          pt[:, k4 * 128:(k4 + 1) * 128],
                          xrow(par)[:, k * 128:(k + 1) * 128],
                          ident[:, :], start=True, stop=True),
                      reads=[f"xrow.{par}", "ident"], writes=[pk],
                      tag=f"A.T{tt}.{k}")
                for k4 in range(4):
                    k = g * 4 + k4
                    E("vector",
                      lambda e, k=k, k4=k4, tt=tt, pt=pt: e.tensor_copy(
                          out=XBUF[:, k * 2048 + tt * 128: k * 2048 + (tt + 1) * 128],
                          in_=pt[:, k4 * 128:(k4 + 1) * 128]),
                      reads=[pk], writes=[f"xT.{k}.{tt}"], tag=f"A.E{tt}.{k}")

    def xT(k):
        return XBUF[:, k * 2048:(k + 1) * 2048]

    def xT_keys(rng=range(16)):
        return [f"xT.{k}.{t}" for k in rng for t in range(16)]

    if "B0" in PHASES:
        DMA("sync", lambda e: e.dma_start(
            out=bvf32[:, :], in_=bv_d.rearrange("(p j) -> p j", p=1)),
            writes=["bvf32"], tag="B0.bv")
        E("scalar", lambda e: e.copy(out=bvhi[:, :], in_=bvf32[:, :]),
          reads=["bvf32"], writes=["bvhi"], tag="B0.bvhi")
        E("vector", lambda e: e.tensor_tensor(
            out=bvf32[:, :], in0=bvf32[:, :], in1=bvhi[:, :], op=ALU.subtract),
          reads=["bvf32", "bvhi"], writes=["bvf32"], tag="B0.bvsub")
        E("scalar", lambda e: e.copy(out=bvlo[:, :], in_=bvf32[:, :]),
          reads=["bvf32"], writes=["bvlo"], tag="B0.bvlo")
        DMA("gpsimd", lambda e: e.dma_start(
            out=HBUF[:, :].rearrange("p (k m) -> p k m", k=16),
            in_=wv_d.rearrange("(k p) m -> p k m", p=128)),
            writes=["wvbf"], tag="B0.wv")
        for tt in range(16):
            for vc in range(2):
                pk = ("P6", "P7")[(tt * 2 + vc) % 2]
                pt = (P6, P7)[(tt * 2 + vc) % 2]
                for k in range(16):
                    E("tensor",
                      lambda e, k=k, tt=tt, vc=vc, pt=pt: e.matmul(
                          pt[:, :],
                          XBUF[:, k * 2048 + tt * 128: k * 2048 + (tt + 1) * 128],
                          HBUF[:, k * 1024 + vc * 512: k * 1024 + (vc + 1) * 512],
                          start=(k == 0), stop=False),
                      reads=[f"xT.{k}.{tt}", "wvbf"], writes=[pk],
                      tag=f"B0.mm{tt}.{vc}.{k}")
                E("tensor", lambda e, vc=vc, pt=pt: e.matmul(
                    pt[:, :], onesrowb[:, :],
                    bvhi[:, vc * 512:(vc + 1) * 512], start=False, stop=False),
                  reads=["onesrowb", "bvhi"], writes=[pk], tag=f"B0.bh{tt}.{vc}")
                E("tensor", lambda e, vc=vc, pt=pt: e.matmul(
                    pt[:, :], onesrowb[:, :],
                    bvlo[:, vc * 512:(vc + 1) * 512], start=False, stop=True),
                  reads=["onesrowb", "bvlo"], writes=[pk], tag=f"B0.bl{tt}.{vc}")
                E("vector", lambda e, tt=tt, vc=vc, pt=pt: e.tensor_copy(
                    out=VBUF[:, tt * 1024 + vc * 512: tt * 1024 + (vc + 1) * 512],
                    in_=pt[:, :]),
                  reads=[pk], writes=[f"V.{tt}.{vc}"], tag=f"B0.ev{tt}.{vc}")

    def QT(par):
        return HBUF[:, par * 2048:(par + 1) * 2048]

    def KT(par):
        return HBUF[:, 4096 + par * 2048: 4096 + (par + 1) * 2048]

    def wqh(par):
        return STRM_BF[:, par * 2048:(par + 1) * 2048]

    def wkh(par):
        return STRM_BF[:, 4096 + par * 2048: 4096 + (par + 1) * 2048]

    if "BC" in PHASES:
        sch.fence()
        ptc = [0]

        def emit_wdma(h):
            par = h % 2
            DMA("gpsimd", lambda e, h=h, par=par: e.dma_start(
                out=wqh(par)[:, :].rearrange("p (k j) -> p k j", k=16),
                in_=wq_d.rearrange("(k p) m -> p k m", p=128)[:, :, h * 128:(h + 1) * 128]),
                writes=[f"wqh.{par}"], tag=f"B.wq{h}")
            DMA("gpsimd", lambda e, h=h, par=par: e.dma_start(
                out=wkh(par)[:, :].rearrange("p (k j) -> p k j", k=16),
                in_=wk_d.rearrange("(k p) m -> p k m", p=128)[:, :, h * 128:(h + 1) * 128]),
                writes=[f"wkh.{par}"], tag=f"B.wk{h}")
            DMA("sync", lambda e, h=h, par=par: e.dma_start(
                out=bqcol[par][:, :],
                in_=bq_d.rearrange("(p j) -> p j", j=1)[h * 128:(h + 1) * 128, :]),
                writes=[f"bqc.{par}"], tag=f"B.bq{h}")
            DMA("sync", lambda e, h=h, par=par: e.dma_start(
                out=bkcol[par][:, :],
                in_=bk_d.rearrange("(p j) -> p j", j=1)[h * 128:(h + 1) * 128, :]),
                writes=[f"bkc.{par}"], tag=f"B.bk{h}")

        emit_wdma(0)
        pend_norm = [None]
        rspar = [0]
        for h in range(HPC):
            par = h % 2
            if h + 1 < HPC:
                emit_wdma(h + 1)
            for (wfn, wkey, dstfn, dkey, bcol, bkey) in (
                    (wqh, f"wqh.{par}", QT, f"QT.{par}", bqcol[par], f"bqc.{par}"),
                    (wkh, f"wkh.{par}", KT, f"KT.{par}", bkcol[par], f"bkc.{par}")):
                for tc in range(4):
                    pk = ("PB0", "PB1")[tc % 2]
                    pt = (PB0, PB1)[tc % 2]
                    for k in range(16):
                        E("tensor", lambda e, k=k, tc=tc, par=par, wfn=wfn, pt=pt: e.matmul(
                            pt[:, :], wfn(par)[:, k * 128:(k + 1) * 128],
                            XBUF[:, k * 2048 + tc * 512: k * 2048 + (tc + 1) * 512],
                            start=(k == 0), stop=(k == 15)),
                          reads=[wkey, f"xT.{k}.{4 * tc}", f"xT.{k}.{4 * tc + 1}",
                                 f"xT.{k}.{4 * tc + 2}", f"xT.{k}.{4 * tc + 3}"],
                          writes=[pk], tag=f"B.qk{h}.{tc}.{k}")
                    E("scalar", lambda e, tc=tc, par=par, dstfn=dstfn, bcol=bcol, pt=pt:
                      e.activation(out=dstfn(par)[:, tc * 512:(tc + 1) * 512],
                                   in_=pt[:, :], func=AF.Identity,
                                   bias=bcol[:, 0:1], scale=1.0),
                      reads=[pk, bkey], writes=[dkey], tag=f"B.ev{h}.{tc}")
                    if tc == 0 and pend_norm[0] is not None:
                        pend_norm[0]()
                        pend_norm[0] = None

            for qg in range(4):
                pok = ("PO0", "PO1")[qg % 2]
                pot = (PO0, PO1)[qg % 2]
                nkc = qg * 4 + 4
                psr = ("PS0", "PS1", "PB0", "PB1")
                pst_r = (PS0, PS1, PB0, PB1)
                ptis = [(ptc[0] + kc) % 5 for kc in range(nkc)]
                ptc[0] += nkc

                def emit_smm(kc, qg=qg, par=par, psr=psr, pst_r=pst_r):
                    E("tensor", lambda e, kc=kc, qg=qg, par=par,
                      pst=pst_r[kc % 4]: e.matmul(
                        pst[:, :], KT(par)[:, kc * 128:(kc + 1) * 128],
                        QT(par)[:, qg * 512:(qg + 1) * 512],
                        start=True, stop=True),
                      reads=[f"KT.{par}", f"QT.{par}"], writes=[psr[kc % 4]],
                      tag=f"C.s{h}.{qg}.{kc}")

                for kk in range(min(3, nkc)):
                    emit_smm(kk)
                for kc in range(nkc):
                    if kc == 3 and pend_norm[0] is not None:
                        pend_norm[0]()
                        pend_norm[0] = None
                    pti = ptis[kc]
                    psk = psr[kc % 4]
                    pst = pst_r[kc % 4]
                    E("scalar", lambda e, pti=pti, pst=pst: e.activation(
                        out=pT[pti][:, :], in_=pst[:, :], func=AF.Exp, scale=CSOFT),
                      reads=[psk], writes=[f"pT.{pti}"], tag=f"C.e{h}.{qg}.{kc}")
                    if kc >= qg * 4:
                        j = kc - qg * 4
                        E("vector", lambda e, pti=pti, j=j: e.tensor_tensor(
                            out=pT[pti][:, :], in0=pT[pti][:, :],
                            in1=maskm[:, 512 - j * 128: 1024 - j * 128], op=ALU.mult),
                          reads=[f"pT.{pti}", "maskm"], writes=[f"pT.{pti}"],
                          tag=f"C.m{h}.{qg}.{kc}")
                    if kc + 3 < nkc:
                        emit_smm(kc + 3)
                    E("tensor", lambda e, kc=kc, h=h, pti=pti, pot=pot, nkc=nkc: e.matmul(
                        pot[:, :], VBUF[:, kc * 1024 + h * 128: kc * 1024 + (h + 1) * 128],
                        pT[pti][:, :], start=(kc == 0), stop=(kc == nkc - 1)),
                      reads=[f"V.{kc}.0", f"V.{kc}.1", f"pT.{pti}"], writes=[pok],
                      tag=f"C.pv{h}.{qg}.{kc}")
                    E("tensor", lambda e, kc=kc, pti=pti, nkc=nkc,
                      rp=rspar[0]: e.matmul(
                        P6[32 * rp:32 * rp + 1, :], onescolb[:, :], pT[pti][:, :],
                        start=(kc == 0), stop=(kc == nkc - 1)),
                      reads=["onescolb", f"pT.{pti}"], writes=[f"P6r.{rspar[0]}"],
                      tag=f"C.rs{h}.{qg}.{kc}")

                def norm(h=h, qg=qg, pok=pok, pot=pot, rp=None):
                    rp = rspar[0]
                    E("vector", lambda e, rp=rp: e.reciprocal(
                        out=rr_sb[:, :], in_=P6[32 * rp:32 * rp + 1, :]),
                      reads=[f"P6r.{rp}"], writes=["rr"], tag=f"C.rcp{h}.{qg}")
                    E("tensor", lambda e: e.matmul(
                        P7[:, :], onesrowf[:, :], rr_sb[:, :], start=True, stop=True),
                      reads=["onesrowf", "rr"], writes=["P7b"], tag=f"C.bc{h}.{qg}")
                    E("vector", lambda e: e.tensor_copy(out=bc_sb[:, :], in_=P7[:, :]),
                      reads=["P7b"], writes=["bcs"], tag=f"C.bce{h}.{qg}")
                    E("vector", lambda e, qg=qg, pot=pot: e.tensor_tensor(
                        out=otr[qg % 2][:, :], in0=pot[:, :], in1=bc_sb[:, :], op=ALU.mult),
                      reads=[pok, "bcs"], writes=[f"otr.{qg % 2}"], tag=f"C.ot{h}.{qg}")
                    cc_in = (ccA_in if h < 4 else ccB_in)
                    DMA("scalar", lambda e, h=h, qg=qg, cc_in=cc_in: e.dma_start(
                        out=cc_in.ap()[h % 4, :, qg * 512:(qg + 1) * 512],
                        in_=otr[qg % 2][:, :]),
                        reads=[f"otr.{qg % 2}"],
                        writes=[f"d:cc{'A' if h < 4 else 'B'}.{h % 4}.{qg}"],
                        tag=f"C.cw{h}.{qg}")

                def norm_fix(norm=norm, rp=rspar[0]):
                    def g():
                        sv = rspar[0]
                        rspar[0] = rp
                        norm()
                        rspar[0] = sv
                    return g
                pend_norm[0] = norm_fix()
                rspar[0] ^= 1
            if h == 3:
                pend_norm[0]()
                pend_norm[0] = None
            if h == 3:
                E("gpsimd", lambda e: e.collective_compute(
                    "AllGather", ALU.bypass, replica_groups=RG_PAIRS,
                    ins=[ccA_in.ap().opt()], outs=[ccA_out.ap().opt()]),
                    reads=[f"d:ccA.{i}.{q}" for i in range(4) for q in range(4)],
                    writes=["d:ccoA"], kind="cc", tag="CCA")
            if h == 7:
                pend_norm[0]()
                pend_norm[0] = None
                E("gpsimd", lambda e: e.collective_compute(
                    "AllGather", ALU.bypass, replica_groups=RG_PAIRS,
                    ins=[ccB_in.ap().opt()], outs=[ccB_out.ap().opt()]),
                    reads=[f"d:ccB.{i}.{q}" for i in range(4) for q in range(4)],
                    writes=["d:ccoB"], kind="cc", tag="CCB")

    def h1T_view():
        return HBUF

    def wob(g):
        return XBUF[:, g * 2048:(g + 1) * 2048]

    def olh(g):
        return VBUF[:, g * 1024:(g + 1) * 1024]

    def xresb(par):
        return STRM_F32[:, par * 2048:(par + 1) * 2048]

    if "D" in PHASES:
        sch.fence()
        DMA("sync", lambda e: e.dma_start(
            out=boutf32[:, :], in_=bo_d.rearrange("(p j) -> p j", p=1)),
            writes=["bout"], tag="D.bo")
        for g in range(16):
            DMA("gpsimd", lambda e, g=g: e.dma_start(
                out=XBUF[:, g * 2048:(g + 1) * 2048],
                in_=wo_d[g * 128:(g + 1) * 128, :]),
                writes=[f"wob.{g}"], tag=f"D.wo{g}")
        GORD = [0, 1, 2, 3, 8, 9, 10, 11, 4, 5, 6, 7, 12, 13, 14, 15]
        for g in GORD:
            blk, hh = g // 8, g % 8
            cco = ccA_out if hh < 4 else ccB_out
            DMA("sync", lambda e, blk=blk, hh=hh, cco=cco, g=g: e.dma_start(
                out=ccraw[g % 3][:, :], in_=cco.ap()[blk, hh % 4, :, :]),
                reads=["d:ccoA" if hh < 4 else "d:ccoB"],
                writes=[f"ccraw.{g % 3}"], tag=f"D.cr{g}")
            E("vector", lambda e, g=g: e.select(
                out=olh(g)[:, :], mask=msku8[:, :],
                on_true=ccraw[g % 3][:, 1024:2048],
                on_false=ccraw[g % 3][:, 0:1024]),
              reads=[f"ccraw.{g % 3}", "msku8"], writes=[f"olh.{g}"],
              tag=f"D.sel{g}")
        def emit_h1T(tc):
            for g4 in range(4):
                pk = ("PB0", "PB1")[g4 % 2]
                pt = (PB0, PB1)[g4 % 2]
                for k4 in range(4):
                    k = g4 * 4 + k4
                    E("tensor", lambda e, k=k, k4=k4, tc=tc, pt=pt: e.matmul(
                        pt[:, k4 * 128:(k4 + 1) * 128],
                        h1bf[tc % 2][:, k * 128:(k + 1) * 128], ident[:, :],
                        start=True, stop=True),
                      reads=[f"h1bfk.{tc % 2}", "ident"], writes=[pk],
                      tag=f"D.hT{tc}.{k}")
                for k4 in range(4):
                    k = g4 * 4 + k4
                    E("vector", lambda e, k=k, k4=k4, tc=tc, pt=pt: e.tensor_copy(
                        out=HBUF[:, k * 1024 + tc * 128: k * 1024 + (tc + 1) * 128],
                        in_=pt[:, k4 * 128:(k4 + 1) * 128]),
                      reads=[pk], writes=[f"h1T.{k}.{tc}"], tag=f"D.hE{tc}.{k}")

        for tc in range(8):
            par = tc % 2
            DMA("sync", lambda e, tc=tc, par=par: e.dma_start(
                out=xresb(par)[:, :], in_=xres_d[tc * 128:(tc + 1) * 128, :]),
                writes=[f"xresb.{par}"], tag=f"D.xr{tc}")
            for ng in range(4):
                pk = ("PS0", "PS1", "PO0", "PO1")[ng]
                pt = (PS0, PS1, PO0, PO1)[ng]
                for gx, g in enumerate(GORD):
                    E("tensor", lambda e, g=g, tc=tc, ng=ng, pt=pt, gx=gx: e.matmul(
                        pt[:, :], olh(g)[:, tc * 128:(tc + 1) * 128],
                        wob(g)[:, ng * 512:(ng + 1) * 512],
                        start=(gx == 0), stop=False),
                      reads=[f"olh.{g}", f"wob.{g}"], writes=[pk],
                      tag=f"D.mm{tc}.{ng}.{g}")
                E("tensor", lambda e, ng=ng, pt=pt: e.matmul(
                    pt[:, :], onesrowf[:, :], boutf32[:, ng * 512:(ng + 1) * 512],
                    start=False, stop=True),
                  reads=["onesrowf", "bout"], writes=[pk], tag=f"D.bo{tc}.{ng}")
                E("vector", lambda e, ng=ng, par=par, pt=pt: e.tensor_tensor(
                    out=SCR1[:, ng * 512:(ng + 1) * 512], in0=pt[:, :],
                    in1=xresb(par)[:, ng * 512:(ng + 1) * 512], op=ALU.add),
                  reads=[pk, f"xresb.{par}"], writes=[f"hpre.{ng}"],
                  tag=f"D.hp{tc}.{ng}")
            hprek = [f"hpre.{i}" for i in range(4)]
            _ln(sch, par, SCR1, SCR3, SCR2, hprek, ["sq1"], ["h1f"],
                c_s1, c_m, c_v, c_t, c_r, c_s2, f"D.ln{tc}", epscol)
            E("scalar", lambda e, tc=tc: e.copy(out=h1bf[tc % 2][:, :], in_=SCR2[:, :]),
              reads=["h1f"], writes=[f"h1bfk.{tc % 2}"], tag=f"D.h1b{tc}")
            DMA("scalar", lambda e, tc=tc: e.dma_start(
                out=h1s_d[tc * 128:(tc + 1) * 128, :], in_=SCR2[:, :]),
                reads=["h1f"], writes=[f"d:h1s.{tc}"], tag=f"D.h1s{tc}")
            if tc > 0:
                emit_h1T(tc - 1)
        emit_h1T(7)

    def wgnat(par):
        return STRM_BF[:, par * 2048:(par + 1) * 2048]

    def wunat(par):
        return STRM_BF[:, 4096 + par * 2048: 4096 + (par + 1) * 2048]

    def wgt_sb():
        return SCR3_BF[:, 0:2048]

    def wut_sb():
        return SCR3_BF[:, 2048:4096]

    def silu_scr():
        return SCR1[:, 0:1024]

    if "E" in PHASES:
        sch.fence()
        h1Tkeys = [f"h1T.{k}.{t}" for k in range(16) for t in range(8)] + ["reg:H"]

        def wgt_pb(par):
            return SCR3_BF[:, par * 2048:(par + 1) * 2048]

        def wut_pb(par):
            return SCR2_BF[:, par * 2048:(par + 1) * 2048]

        def e_dma(ic):
            par = ic % 2
            DMA("gpsimd", lambda e, ic=ic, par=par: e.dma_start(
                out=wgnat(par)[:, :], in_=gw_d[ic * 128:(ic + 1) * 128, :]),
                writes=[f"wgn.{par}"], tag=f"E.gw{ic}")
            DMA("gpsimd", lambda e, ic=ic, par=par: e.dma_start(
                out=wunat(par)[:, :], in_=uw_d[ic * 128:(ic + 1) * 128, :]),
                writes=[f"wun.{par}"], tag=f"E.uw{ic}")
            DMA("sync", lambda e, ic=ic, par=par: e.dma_start(
                out=gscol[par][:, :],
                in_=gs_d.rearrange("(p j) -> p j", j=1)[ic * 128:(ic + 1) * 128, :]),
                writes=[f"gsc.{par}"], tag=f"E.gs{ic}")
            DMA("sync", lambda e, ic=ic, par=par: e.dma_start(
                out=uscol[par][:, :],
                in_=us_d.rearrange("(p j) -> p j", j=1)[ic * 128:(ic + 1) * 128, :]),
                writes=[f"usc.{par}"], tag=f"E.us{ic}")

        def e_deq(ic):
            par = ic % 2
            deqeng = "gpsimd" if POOL_DEQ else "vector"
            E(deqeng, lambda e, par=par: e.tensor_scalar(
                out=wgnat(par)[:, :], in0=wgnat(par)[:, :], scalar1=128.0,
                scalar2=gscol[par][:, 0:1], op0=ALU.subtract, op1=ALU.mult),
              reads=[f"wgn.{par}", f"gsc.{par}"], writes=[f"wgn.{par}"],
              tag=f"E.dq{ic}")
            E(deqeng, lambda e, par=par: e.tensor_scalar(
                out=wunat(par)[:, :], in0=wunat(par)[:, :], scalar1=128.0,
                scalar2=uscol[par][:, 0:1], op0=ALU.subtract, op1=ALU.mult),
              reads=[f"wun.{par}", f"usc.{par}"], writes=[f"wun.{par}"],
              tag=f"E.du{ic}")

        def e_trans(ic):
            par = ic % 2
            gg = 0
            for (nat, natk, dst, dstk) in ((wgnat, f"wgn.{par}", wgt_pb, "wgt"),
                                           (wunat, f"wun.{par}", wut_pb, "wut")):
                for g4 in range(4):
                    pk = ("PB0", "PB1", "P6e", "P7e")[gg % 4]
                    pt = (PB0, PB1, P6, P7)[gg % 4]
                    for k4 in range(4):
                        k = g4 * 4 + k4
                        E("tensor", lambda e, k=k, k4=k4, par=par, nat=nat, pt=pt:
                          e.matmul(pt[:, k4 * 128:(k4 + 1) * 128],
                                   nat(par)[:, k * 128:(k + 1) * 128],
                                   ident[:, :], start=True, stop=True),
                          reads=[natk, "ident"], writes=[pk], tag=f"E.T{ic}.{k}")
                    if gg % 2 == 0:
                        E("vector", lambda e, g4=g4, dst=dst, par=par, pt=pt: e.tensor_copy(
                            out=dst(par)[:, g4 * 512:(g4 + 1) * 512], in_=pt[:, :]),
                          reads=[pk], writes=[f"{dstk}.{par}.{g4}"], tag=f"E.TE{ic}.{g4}")
                    else:
                        E("scalar", lambda e, g4=g4, dst=dst, par=par, pt=pt: e.copy(
                            out=dst(par)[:, g4 * 512:(g4 + 1) * 512], in_=pt[:, :]),
                          reads=[pk], writes=[f"{dstk}.{par}.{g4}"], tag=f"E.TEs{ic}.{g4}")
                    gg += 1

        e_dma(0)
        e_deq(0)
        e_trans(0)
        for ic in range(64):
            par = ic % 2
            if ic + 1 < 64:
                e_dma(ic + 1)
                e_deq(ic + 1)
                e_trans(ic + 1)
            wgtk = [f"wgt.{par}.{i}" for i in range(4)]
            wutk = [f"wut.{par}.{i}" for i in range(4)]
            for c in range(16):
                for th in range(2):
                    E("tensor", lambda e, c=c, th=th, par=par: e.matmul(
                        (PS0, PS1)[th][:, :], wgt_pb(par)[:, c * 128:(c + 1) * 128],
                        HBUF[:, c * 1024 + th * 512: c * 1024 + (th + 1) * 512],
                        start=(c == 0), stop=(c == 15)),
                      reads=wgtk + h1Tkeys, writes=[("PS0", "PS1")[th]],
                      tag=f"E.g{ic}.{c}.{th}")
                for th in range(2):
                    E("tensor", lambda e, c=c, th=th, par=par: e.matmul(
                        (PO0, PO1)[th][:, :], wut_pb(par)[:, c * 128:(c + 1) * 128],
                        HBUF[:, c * 1024 + th * 512: c * 1024 + (th + 1) * 512],
                        start=(c == 0), stop=(c == 15)),
                      reads=wutk + h1Tkeys, writes=[("PO0", "PO1")[th]],
                      tag=f"E.u{ic}.{c}.{th}")
            for th in range(2):
                E("scalar", lambda e, th=th: e.activation(
                    out=silu_scr()[:, th * 512:(th + 1) * 512],
                    in_=(PS0, PS1)[th][:, :], func=AF.Silu),
                  reads=[("PS0", "PS1")[th]], writes=[f"silu.{th}"],
                  tag=f"E.si{ic}.{th}")
                E("vector", lambda e, ic=ic, th=th: e.tensor_tensor(
                    out=aT_view(ic)[:, th * 512:(th + 1) * 512],
                    in0=silu_scr()[:, th * 512:(th + 1) * 512],
                    in1=(PO0, PO1)[th][:, :], op=ALU.mult),
                  reads=[f"silu.{th}", ("PO0", "PO1")[th]],
                  writes=[aT_key(ic)], tag=f"E.a{ic}.{th}")

    def wdnat(hslot, par):
        return STRM_BF[:, (hslot * 2 + par) * 2048:(hslot * 2 + par + 1) * 2048]

    def wdT(par):
        return SCR3_BF[:, par * 256:(par + 1) * 256]

    def ztile(par):
        return SCR2[:, par * 256:(par + 1) * 256]

    def h1sl(par):
        return SCR2[:, 1024 + par * 256: 1024 + (par + 1) * 256]

    def ztile4(i):
        return SCR2[:, i * 256:(i + 1) * 256]

    def h1sl4(i):
        return SCR2[:, 1024 + i * 256: 1024 + (i + 1) * 256]

    if "DOWN" in PHASES:
        sch.fence()
        if DBG:
            for ic in range(64):
                DMA("scalar", lambda e, ic=ic: e.dma_start(
                    out=dbg_at[ic * 128:(ic + 1) * 128, :], in_=aT_view(ic)[:, :]),
                    reads=[aT_key(ic)], writes=[f"d:dbgat.{ic}"], tag=f"dbgat{ic}")

        def w_dma(hc, iq):
            par = iq % 2
            for hs in range(2):
                H = hc * 2 + hs
                DMA("gpsimd", lambda e, H=H, iq=iq, hs=hs, par=par: e.dma_start(
                    out=wdnat(hs, par)[:, :],
                    in_=dw_d[H * 128:(H + 1) * 128, iq * 2048:(iq + 1) * 2048]),
                    writes=[f"wdn.{hs}.{par}"], tag=f"W.dw{hc}.{iq}.{hs}")

        def w_deq(hc, iq):
            par = iq % 2
            for hs in range(2):
                E("gpsimd" if POOL_DEQ else "vector", lambda e, hs=hs, par=par: e.tensor_scalar(
                    out=wdnat(hs, par)[:, :], in0=wdnat(hs, par)[:, :],
                    scalar1=128.0, scalar2=dscol[hs][:, 0:1],
                    op0=ALU.subtract, op1=ALU.mult),
                  reads=[f"wdn.{hs}.{par}", f"dsc.{hs}"],
                  writes=[f"wdn.{hs}.{par}"], tag=f"W.dq{hc}.{iq}.{hs}")

        def w_trans(hc, gi):
            iq, c = gi // 16, gi % 16
            par, cp = iq % 2, gi % 2
            pbt = (PB0, PB1)[cp]
            for hs in range(2):
                E("tensor", lambda e, c=c, hs=hs, par=par, pbt=pbt: e.matmul(
                    pbt[:, hs * 128:(hs + 1) * 128],
                    wdnat(hs, par)[:, c * 128:(c + 1) * 128],
                    ident[:, :], start=True, stop=True),
                  reads=[f"wdn.{hs}.{par}", "ident"],
                  writes=[f"PBd.{cp}"], tag=f"W.T{hc}.{gi}.{hs}")
            E("vector", lambda e, cp=cp, pbt=pbt: e.tensor_copy(
                out=wdT(cp)[:, :], in_=pbt[:, 0:256]),
              reads=[f"PBd.{cp}"], writes=[f"wdT.{cp}"], tag=f"W.TE{hc}.{gi}")

        for hc in range(8):
            for hs in range(2):
                H = hc * 2 + hs
                DMA("sync", lambda e, H=H, hs=hs: e.dma_start(
                    out=dscol[hs][:, :],
                    in_=ds_d.rearrange("(p j) -> p j", j=1)[H * 128:(H + 1) * 128, :]),
                    writes=[f"dsc.{hs}"], tag=f"W.ds{hc}.{hs}")
            if hc == 0:
                w_dma(hc, 0)
            w_deq(hc, 0)
            w_trans(hc, 0)
            for bi in range(4):
                pt = (PS0, PS1, PO0, PO1)[bi]
                pn = ("PS0", "PS1", "PO0", "PO1")[bi]
                E("tensor", lambda e, pt=pt: e.matmul(
                    pt[:, :], zerorow[:, :], maskm[0:1, 0:512],
                    start=True, stop=True),
                  reads=["zerorow", "maskm"],
                  writes=[pn + ".0", pn + ".1"], tag=f"W.z0{hc}.{bi}")
            for gi in range(64):
                iq, c = gi // 16, gi % 16
                cp = gi % 2
                if c == 0 and iq + 1 < 4:
                    w_dma(hc, iq + 1)
                if c == 0 and iq == 3 and hc + 1 < 8:
                    w_dma(hc + 1, 0)
                if gi + 1 < 64:
                    if (gi + 1) % 16 == 0:
                        w_deq(hc, iq + 1)
                    w_trans(hc, gi + 1)
                for tc in range(8):
                    pt = (PS0, PS1, PO0, PO1)[tc // 2]
                    pk = ("PS0", "PS1", "PO0", "PO1")[tc // 2] + f".{tc % 2}"
                    E("tensor", lambda e, gi=gi, tc=tc, cp=cp, pt=pt: e.matmul(
                        pt[:, (tc % 2) * 256:(tc % 2 + 1) * 256],
                        aT_view(gi)[:, tc * 128:(tc + 1) * 128],
                        wdT(cp)[:, :], start=False, stop=(gi == 63)),
                      reads=[aT_key(gi), f"wdT.{cp}"], writes=[pk],
                      tag=f"W.y{hc}.{gi}.{tc}")
            for tc in range(8):
                DMA("sync", lambda e, tc=tc, hc=hc: e.dma_start(
                    out=h1sl4(tc % 4)[:, :],
                    in_=h1s_d[tc * 128:(tc + 1) * 128, hc * 256:(hc + 1) * 256]),
                    writes=[f"h1sl.{tc % 4}"], tag=f"W.h1{hc}.{tc}")
                pt = (PS0, PS1, PO0, PO1)[tc // 2]
                pk = ("PS0", "PS1", "PO0", "PO1")[tc // 2] + f".{tc % 2}"
                E("vector", lambda e, tc=tc, pt=pt: e.tensor_tensor(
                    out=ztile4(tc % 4)[:, :], in0=pt[:, (tc % 2) * 256:(tc % 2 + 1) * 256],
                    in1=h1sl4(tc % 4)[:, :], op=ALU.add),
                  reads=[pk, f"h1sl.{tc % 4}"], writes=[f"zt.{tc % 4}"],
                  tag=f"W.z{hc}.{tc}")
                DMA("scalar", lambda e, tc=tc, hc=hc: e.dma_start(
                    out=zs_d[tc * 128:(tc + 1) * 128, hc * 256:(hc + 1) * 256],
                    in_=ztile4(tc % 4)[:, :]),
                    reads=[f"zt.{tc % 4}"], writes=[f"d:zs.{tc}.{hc}"],
                    tag=f"W.zs{hc}.{tc}")

    if DBG:
        dbg_at = nc.dram_tensor("dbg_at", [IFF, TOK], BF16, kind="ExternalOutput").ap()
        dbg_h1 = nc.dram_tensor("dbg_h1", [TOK, D], F32, kind="ExternalOutput").ap()
        dbg_z = nc.dram_tensor("dbg_z", [TOK, D], F32, kind="ExternalOutput").ap()
        dbg_ot = nc.dram_tensor("dbg_ot", [2, 4, 128, S], BF16, kind="ExternalOutput").ap()

    if "FIN" in PHASES:
        zrow_f = [HBUF_F32[:, i * 2048:(i + 1) * 2048] for i in range(2)]
        sq_f = HBUF_F32[:, 2 * 2048:3 * 2048]
        yrow_f = HBUF_F32[:, 3 * 2048:4 * 2048]
        for tc in range(8):
            par = tc % 2
            DMA("sync", lambda e, tc=tc, par=par: e.dma_start(
                out=zrow_f[par][:, :], in_=zs_d[tc * 128:(tc + 1) * 128, :]),
                reads=[f"d:zs.{tc}.{hcv}" for hcv in range(8)],
                writes=[f"zrow.{par}", "reg:H"], tag=f"F.z{tc}")
            _ln(sch, par, zrow_f[par], sq_f, yrow_f, [f"zrow.{par}"], ["sqf"], ["yrow"],
                c_s1, c_m, c_v, c_t, c_r, c_s2, f"F.ln{tc}", epscol)
            DMA("scalar", lambda e, tc=tc: e.dma_start(
                out=yout_d[tc * 128:(tc + 1) * 128, :], in_=yrow_f[:, :]),
                reads=["yrow"], writes=[f"d:y.{tc}"], tag=f"F.y{tc}")

    sch.finalize()

    import contextlib
    with contextlib.ExitStack() as ctx:
        sems = {}
        for name in sch.finals:
            sems[name] = ctx.enter_context(nc.semaphore(name))
        block = ctx.enter_context(nc.Block())
        vclocks = {e: {} for e in ENGS}

        @block.sync
        def _(eng):
            sch.replay("sync", eng, sems, vclocks["sync"])
            for name, val in sorted(sch.finals.items()):
                if vclocks["sync"].get(name, 0) < val:
                    eng.wait_ge(sems[name], val)

        @block.scalar
        def _(eng):
            sch.replay("scalar", eng, sems, vclocks["scalar"])

        @block.vector
        def _(eng):
            sch.replay("vector", eng, sems, vclocks["vector"])

        @block.tensor
        def _(eng):
            sch.replay("tensor", eng, sems, vclocks["tensor"])

        @block.gpsimd
        def _(eng):
            sch.replay("gpsimd", eng, sems, vclocks["gpsimd"])

    return nc


def _ln(sch, par, row, sq, out, rowkeys, sqkeys, outkeys,
        c_s1, c_m, c_v, c_t, c_r, c_s2, tag, epscol=None):
    E = sch.emit
    N = 2048
    E("vector", lambda e: e.reduce_sum(out=c_s1[par][:, :], in_=row[:, :], axis=AX),
      reads=rowkeys, writes=[f"cs1.{par}"], tag=tag + ".s1")
    E("vector", lambda e: e.tensor_scalar(
        out=c_m[par][:, :], in0=c_s1[par][:, :], scalar1=1.0 / N, scalar2=None,
        op0=ALU.mult),
      reads=[f"cs1.{par}"], writes=[f"cm.{par}"], tag=tag + ".m")
    E("vector", lambda e: e.tensor_tensor(
        out=sq[:, :], in0=row[:, :], in1=row[:, :], op=ALU.mult),
      reads=rowkeys, writes=sqkeys, tag=tag + ".sq")
    E("vector", lambda e: e.reduce_sum(out=c_s2[par][:, :], in_=sq[:, :], axis=AX),
      reads=sqkeys, writes=[f"cs2.{par}"], tag=tag + ".s2")
    E("vector", lambda e: e.tensor_scalar(
        out=c_v[par][:, :], in0=c_s2[par][:, :], scalar1=1.0 / N, scalar2=None,
        op0=ALU.mult),
      reads=[f"cs2.{par}"], writes=[f"cv.{par}"], tag=tag + ".v")
    E("vector", lambda e: e.tensor_tensor(
        out=c_t[par][:, :], in0=c_m[par][:, :], in1=c_m[par][:, :], op=ALU.mult),
      reads=[f"cm.{par}"], writes=[f"ct.{par}"], tag=tag + ".m2")
    E("vector", lambda e: e.tensor_tensor(
        out=c_v[par][:, :], in0=c_v[par][:, :], in1=c_t[par][:, :], op=ALU.subtract),
      reads=[f"cv.{par}", f"ct.{par}"], writes=[f"cv.{par}"], tag=tag + ".var")
    E("scalar", lambda e: e.activation(
        out=c_t[par][:, :], in_=c_v[par][:, :], func=AF.Ln,
        bias=epscol[:, 0:1]),
      reads=[f"cv.{par}", "epscol"], writes=[f"ct.{par}"], tag=tag + ".ln")
    E("scalar", lambda e: e.activation(
        out=c_r[par][:, :], in_=c_t[par][:, :], func=AF.Exp, scale=-0.5),
      reads=[f"ct.{par}"], writes=[f"cr.{par}"], tag=tag + ".ex")
    E("vector", lambda e: e.tensor_scalar(
        out=out[:, :], in0=row[:, :], scalar1=c_m[par][:, 0:1],
        scalar2=c_r[par][:, 0:1], op0=ALU.subtract, op1=ALU.mult),
      reads=rowkeys + [f"cm.{par}", f"cr.{par}"], writes=outkeys, tag=tag + ".o")


_NC_CACHE = [None]
_LAST_RES = [None]


def kernel(L0_attn_x, L0_attn_w_in, L0_attn_b_in, L0_attn_w_out, L0_attn_b_out,
           L0_ffn_gate_w, L0_ffn_gate_scale, L0_ffn_up_w, L0_ffn_up_scale,
           L0_ffn_down_w, L0_ffn_down_scale):
    x = np.ascontiguousarray(np.asarray(L0_attn_x, dtype=np.float32))
    w_in = np.asarray(L0_attn_w_in, dtype=np.float32)
    b_in = np.asarray(L0_attn_b_in, dtype=np.float32)
    w_out = np.ascontiguousarray(np.asarray(L0_attn_w_out, dtype=np.float32))
    b_out = np.ascontiguousarray(np.asarray(L0_attn_b_out, dtype=np.float32))
    gw = np.ascontiguousarray(np.asarray(L0_ffn_gate_w, dtype=np.int32))
    gs = np.ascontiguousarray(np.asarray(L0_ffn_gate_scale, dtype=np.float32))
    uw = np.ascontiguousarray(np.asarray(L0_ffn_up_w, dtype=np.int32))
    us = np.ascontiguousarray(np.asarray(L0_ffn_up_scale, dtype=np.float32))
    dw = np.ascontiguousarray(np.asarray(L0_ffn_down_w, dtype=np.int32))
    ds = np.ascontiguousarray(np.asarray(L0_ffn_down_scale, dtype=np.float32))

    ident = np.eye(128, dtype=np.float32).astype(BFNP)
    tt, pp = np.meshgrid(np.arange(1024), np.arange(128), indexing="xy")
    maskm = (np.arange(1024)[None, :] >= 512 + np.arange(128)[:, None]).astype(np.float32).astype(BFNP)
    onesrowf = np.ones((1, 128), np.float32)
    onesrowb = np.ones((1, 128), np.float32).astype(BFNP)

    in_maps = []
    for c in range(NCORE):
        b, hg = c // 2, c % 2
        cs = hg * 1024
        m = {
            "xb": x[b],
            "wq": np.ascontiguousarray(w_in[:, cs:cs + 1024]),
            "wk": np.ascontiguousarray(w_in[:, 2048 + cs:2048 + cs + 1024]),
            "wv": np.ascontiguousarray(w_in[:, 4096 + cs:4096 + cs + 1024]),
            "bq": np.ascontiguousarray(b_in[cs:cs + 1024]),
            "bk": np.ascontiguousarray(b_in[2048 + cs:2048 + cs + 1024]),
            "bv": np.ascontiguousarray(b_in[4096 + cs:4096 + cs + 1024]),
            "wo": w_out, "bo": b_out,
            "gw": gw, "gs": gs, "uw": uw, "us": us, "dw": dw, "ds": ds,
            "xres": np.ascontiguousarray(x[b, hg * 1024:(hg + 1) * 1024]),
            "ident": ident, "maskm": maskm,
            "msku8": np.full((128, 1024), hg, np.uint8),
            "onesrowf": onesrowf, "onesrowb": onesrowb,
            "epscol": np.full((128, 1), EPS, np.float32),
            "zerorow": np.zeros((1, 128), np.float32).astype(BFNP),
            "onescolb": np.ones((128, 1), np.float32).astype(BFNP),
        }
        in_maps.append(m)

    if _NC_CACHE[0] is None:
        _NC_CACHE[0] = build_nc()
    nc = _NC_CACHE[0]
    res = run_bass_kernel_spmd(nc, in_maps, core_ids=list(range(NCORE)))
    _LAST_RES[0] = res

    out = np.empty((NB, S, D), np.float32)
    for c in range(NCORE):
        b, hg = c // 2, c % 2
        out[b, hg * 1024:(hg + 1) * 1024] = res.results[c]["yout"]
    return out


# version 18
# speedup vs baseline: 1.1040x; 1.0293x over previous
import math
import numpy as np
import ml_dtypes
import concourse.bass as bass
import concourse.mybir as mybir
from concourse.bass_utils import run_bass_kernel_spmd

BFNP = ml_dtypes.bfloat16
F32 = mybir.dt.float32
BF16 = mybir.dt.bfloat16
I32 = mybir.dt.int32
U8 = mybir.dt.uint8
AF = mybir.ActivationFunctionType
ALU = mybir.AluOpType
AX = mybir.AxisListType.X

D = 2048
S = 2048
NB = 4
NH = 16
DH = 128
HPC = 8
TOK = 1024
IFF = 8192
NCORE = 8
CSOFT = 1.0 / math.sqrt(DH)
EPS = 1e-5

PHASES = ("A", "B0", "BC", "D", "E", "DOWN", "FIN")
DBG = False
POOL_RS = False
POOL_DEQ = False

RG_PAIRS = [[0, 1], [2, 3], [4, 5], [6, 7]]

ENGS = ("sync", "scalar", "vector", "tensor", "gpsimd")


class Sched:

    def __init__(self):
        self.ops = []
        self.keys = {}
        self.fence_deps = []

    def emit(self, eng, fn, reads=(), writes=(), kind="compute", tag=""):
        opid = len(self.ops)
        deps = set(self.fence_deps)
        for k in reads:
            e = self.keys.get(k)
            if e is not None and e["w"] is not None:
                deps.add(e["w"])
        for k in writes:
            e = self.keys.get(k)
            if e is not None:
                if e["w"] is not None:
                    deps.add(e["w"])
                deps.update(e["r"])
        deps.discard(opid)
        self.ops.append({"id": opid, "eng": eng, "fn": fn, "kind": kind,
                         "deps": deps, "tag": tag})
        for k in reads:
            self.keys.setdefault(k, {"w": None, "r": []})["r"].append(opid)
        for k in writes:
            self.keys[k] = {"w": opid, "r": []}
        return opid

    def dma(self, eng, fn, reads=(), writes=(), tag=""):
        return self.emit(eng, fn, reads, writes, kind="dma", tag=tag)

    def fence(self, exclude=()):
        last = {}
        for op in self.ops:
            if op["eng"] in exclude and op["kind"] == "cc":
                continue
            last[(op["eng"], op["kind"] == "cc")] = op["id"]
        self.fence_deps = list(last.values())

    def finalize(self):
        for op in self.ops:
            if op["eng"] == "tensor":
                op["deps"] = {d for d in op["deps"]
                              if self.ops[d]["eng"] != "tensor"}
        for op in self.ops:
            byeng = {}
            keep = set()
            for d in op["deps"]:
                dop = self.ops[d]
                if dop["kind"] == "compute":
                    k = dop["eng"]
                    if k not in byeng or byeng[k] < d:
                        byeng[k] = d
                else:
                    keep.add(d)
            op["deps"] = keep | set(byeng.values())
        need = set()
        for op in self.ops:
            need.update(op["deps"])
        NP = 8
        cnt = {e: 0 for e in ENGS}
        dmacnt = {e: 0 for e in ENGS}
        ccn = 0
        for op in self.ops:
            if op["kind"] == "dma":
                i = dmacnt[op["eng"]]
                dmacnt[op["eng"]] += 1
                op["tok"] = ("q%s%d" % (op["eng"], i % NP), 16 * (i // NP + 1))
                op["inc"] = 16
            elif op["kind"] == "cc":
                ccn += 1
                op["tok"] = ("cc", ccn)
                op["inc"] = 1
            else:
                if op["id"] in need:
                    cnt[op["eng"]] += 1
                    op["tok"] = ("c" + op["eng"], cnt[op["eng"]])
                    op["inc"] = 1
                else:
                    op["tok"] = None
                    op["inc"] = 0
        self.finals = {}
        for e in ENGS:
            for s in range(NP):
                n = (dmacnt[e] - s + NP - 1) // NP if dmacnt[e] > s else 0
                if n:
                    self.finals["q%s%d" % (e, s)] = 16 * n
            if cnt[e]:
                self.finals["c" + e] = cnt[e]
        if ccn:
            self.finals["cc"] = ccn

    def replay(self, engname, eng, sems, vclock):
        for op in self.ops:
            if op["eng"] != engname:
                continue
            waits = {}
            for d in sorted(op["deps"]):
                tok = self.ops[d]["tok"]
                assert tok is not None, (op["tag"], self.ops[d]["tag"])
                sem, val = tok
                if waits.get(sem, 0) < val:
                    waits[sem] = val
            for sem, val in waits.items():
                if vclock.get(sem, 0) < val:
                    eng.wait_ge(sems[sem], val)
                    vclock[sem] = val
            inst = op["fn"](eng)
            if op["inc"]:
                sem, _ = op["tok"]
                inst.then_inc(sems[sem], op["inc"])
                if sem == "c" + engname:
                    pass


def build_nc():
    nc = bass.Bass("TRN2", target_bir_lowering=False, debug=False)

    xb_d = nc.dram_tensor("xb", [S, D], F32, kind="ExternalInput").ap()
    wq_d = nc.dram_tensor("wq", [D, TOK], F32, kind="ExternalInput").ap()
    wk_d = nc.dram_tensor("wk", [D, TOK], F32, kind="ExternalInput").ap()
    wv_d = nc.dram_tensor("wv", [D, TOK], F32, kind="ExternalInput").ap()
    bq_d = nc.dram_tensor("bq", [TOK], F32, kind="ExternalInput").ap()
    bk_d = nc.dram_tensor("bk", [TOK], F32, kind="ExternalInput").ap()
    bv_d = nc.dram_tensor("bv", [TOK], F32, kind="ExternalInput").ap()
    wo_d = nc.dram_tensor("wo", [D, D], F32, kind="ExternalInput").ap()
    bo_d = nc.dram_tensor("bo", [D], F32, kind="ExternalInput").ap()
    gw_d = nc.dram_tensor("gw", [IFF, D], I32, kind="ExternalInput").ap()
    gs_d = nc.dram_tensor("gs", [IFF], F32, kind="ExternalInput").ap()
    uw_d = nc.dram_tensor("uw", [IFF, D], I32, kind="ExternalInput").ap()
    us_d = nc.dram_tensor("us", [IFF], F32, kind="ExternalInput").ap()
    dw_d = nc.dram_tensor("dw", [D, IFF], I32, kind="ExternalInput").ap()
    ds_d = nc.dram_tensor("ds", [D], F32, kind="ExternalInput").ap()
    xres_d = nc.dram_tensor("xres", [TOK, D], F32, kind="ExternalInput").ap()
    ident_d = nc.dram_tensor("ident", [128, 128], BF16, kind="ExternalInput").ap()
    maskm_d = nc.dram_tensor("maskm", [128, 1024], BF16, kind="ExternalInput").ap()
    msku8_d = nc.dram_tensor("msku8", [128, 1024], U8, kind="ExternalInput").ap()
    onesrowf_d = nc.dram_tensor("onesrowf", [1, 128], F32, kind="ExternalInput").ap()
    onesrowb_d = nc.dram_tensor("onesrowb", [1, 128], BF16, kind="ExternalInput").ap()
    epscol_d = nc.dram_tensor("epscol", [128, 1], F32, kind="ExternalInput").ap()
    zerorow_d = nc.dram_tensor("zerorow", [1, 128], BF16, kind="ExternalInput").ap()
    onescolb_d = nc.dram_tensor("onescolb", [128, 1], BF16, kind="ExternalInput").ap()

    yout_d = nc.dram_tensor("yout", [TOK, D], F32, kind="ExternalOutput").ap()

    h1s_d = nc.dram_tensor("h1s", [TOK, D], F32, kind="Internal").ap()
    zs_d = nc.dram_tensor("zs", [TOK, D], F32, kind="Internal").ap()
    ccA_in = nc.dram_tensor("ccA_in", [4, 128, S], BF16, kind="Internal")
    ccB_in = nc.dram_tensor("ccB_in", [4, 128, S], BF16, kind="Internal")
    ccA_out = nc.dram_tensor("ccA_out", [2, 4, 128, S], BF16, kind="Internal")
    ccB_out = nc.dram_tensor("ccB_out", [2, 4, 128, S], BF16, kind="Internal")

    SBB = 16512
    off = [SBB]

    def sb(name, shape, dtype, at=None):
        nonlocal_off = off[0] if at is None else at
        t = nc.alloc_sbuf_tensor_at(name, shape, dtype, offset=nonlocal_off)
        if at is None:
            nbytes = shape[1] * {F32: 4, BF16: 2, I32: 4, U8: 1}[dtype]
            off[0] = (nonlocal_off + nbytes + 31) & ~31
        return t

    ident = sb("ident", [128, 128], BF16)
    maskm = sb("maskm", [128, 1024], BF16)
    msku8 = sb("msku8", [128, 1024], U8)
    onesrowf = sb("onesrowf", [1, 128], F32)
    onesrowb = sb("onesrowb", [1, 128], BF16)
    epscol = sb("epscol", [128, 1], F32)
    zerorow = sb("zerorow", [1, 128], BF16)
    onescolb = sb("onescolb", [128, 1], BF16)
    bqcol = [sb(f"bqcol{i}", [128, 1], F32) for i in range(2)]
    bkcol = [sb(f"bkcol{i}", [128, 1], F32) for i in range(2)]
    gscol = [sb(f"gscol{i}", [128, 1], F32) for i in range(2)]
    uscol = [sb(f"uscol{i}", [128, 1], F32) for i in range(2)]
    dscol = [sb(f"dscol{i}", [128, 1], F32) for i in range(2)]
    bvhi = sb("bvhi", [1, 1024], BF16)
    bvlo = sb("bvlo", [1, 1024], BF16)
    c_s1 = [sb(f"cs1_{i}", [128, 1], F32) for i in range(2)]
    c_m = [sb(f"cm_{i}", [128, 1], F32) for i in range(2)]
    c_v = [sb(f"cv_{i}", [128, 1], F32) for i in range(2)]
    c_t = [sb(f"ct_{i}", [128, 1], F32) for i in range(2)]
    c_r = [sb(f"cr_{i}", [128, 1], F32) for i in range(2)]
    c_s2 = [sb(f"cs2_{i}", [128, 1], F32) for i in range(2)]
    assert off[0] <= SBB + 9216, off[0]

    SCR1 = sb("SCR1", [128, 2048], F32, at=SBB + 9216)
    SCR1_BF = nc.alloc_sbuf_tensor_at("SCR1bf", [128, 4096], BF16, offset=SBB + 9216)
    SCR2 = sb("SCR2", [128, 2048], F32, at=SBB + 17408)
    SCR2_BF = nc.alloc_sbuf_tensor_at("SCR2bf", [128, 4096], BF16, offset=SBB + 17408)
    SCR3 = sb("SCR3", [128, 2048], F32, at=SBB + 25600)
    SCR3_BF = nc.alloc_sbuf_tensor_at("SCR3bf", [128, 4096], BF16, offset=SBB + 25600)
    STRM_BF = nc.alloc_sbuf_tensor_at("STRMbf", [128, 8192], BF16, offset=SBB + 33792)
    STRM_F32 = nc.alloc_sbuf_tensor_at("STRMf32", [128, 4096], F32, offset=SBB + 33792)
    XBUF = nc.alloc_sbuf_tensor_at("XBUF", [128, 32768], BF16, offset=SBB + 50176)
    VBUF = nc.alloc_sbuf_tensor_at("VBUF", [128, 16384], BF16, offset=SBB + 115712)
    HBUF = nc.alloc_sbuf_tensor_at("HBUF", [128, 16384], BF16, offset=SBB + 148480)
    HBUF_F32 = nc.alloc_sbuf_tensor_at("HBUFf32", [128, 8192], F32, offset=SBB + 148480)
    yoff = [SBB + 181248]

    def yb(name, shape, dtype):
        t = nc.alloc_sbuf_tensor_at(name, shape, dtype, offset=yoff[0])
        nbytes = shape[1] * {F32: 4, BF16: 2}[dtype] * (1 if shape[0] > 1 else 1)
        yoff[0] = (yoff[0] + nbytes + 31) & ~31
        return t

    pT = [yb(f"pT{i}", [128, 512], BF16) for i in range(5)]
    rs_part = yb("rs", [128, 512], F32)
    bc_sb = yb("bc", [128, 512], F32)
    rr_sb = yb("rr", [1, 512], F32)
    otr = [yb(f"otr{i}", [128, 512], BF16) for i in range(2)]
    bvf32 = yb("bvf32", [1, 1024], F32)
    ccraw = [nc.alloc_sbuf_tensor_at(f"ccraw{i}", [128, 2048], BF16,
                                     offset=SBB + 181248 + i * 4096) for i in range(3)]
    boutf32 = nc.alloc_sbuf_tensor_at("boutf32", [1, 2048], F32, offset=SBB + 181248 + 12288)
    h1bf = [nc.alloc_sbuf_tensor_at(f"h1bf{i}", [128, 2048], BF16,
                                    offset=SBB + 181248 + 12288 + 8192 + i * 4096) for i in range(2)]
    YAT = nc.alloc_sbuf_tensor_at("YAT", [128, 14336], BF16, offset=SBB + 181248)

    P = [nc.alloc_psum_tensor(f"P{i}", [128, 512], F32) for i in range(8)]
    PB0, PB1, PS0, PS1, PO0, PO1, P6, P7 = P

    def aT_view(ic):
        if ic < 32:
            return XBUF[:, ic * 1024:(ic + 1) * 1024]
        if ic < 48:
            return VBUF[:, (ic - 32) * 1024:(ic - 31) * 1024]
        if ic < 62:
            return YAT[:, (ic - 48) * 1024:(ic - 47) * 1024]
        return SCR1_BF[:, 2048 + (ic - 62) * 1024: 2048 + (ic - 61) * 1024]

    def aT_key(ic):
        return f"aT.{ic}"

    sch = Sched()
    E = sch.emit
    DMA = sch.dma

    DMA("sync", lambda e: e.dma_start(out=ident[:, :], in_=ident_d[:, :]),
        writes=["ident"], tag="c.ident")
    DMA("sync", lambda e: e.dma_start(out=maskm[:, :], in_=maskm_d[:, :]),
        writes=["maskm"], tag="c.maskm")
    DMA("sync", lambda e: e.dma_start(out=msku8[:, :], in_=msku8_d[:, :]),
        writes=["msku8"], tag="c.msk")
    DMA("sync", lambda e: e.dma_start(out=onesrowf[:, :], in_=onesrowf_d[:, :]),
        writes=["onesrowf"], tag="c.orf")
    DMA("sync", lambda e: e.dma_start(out=onesrowb[:, :], in_=onesrowb_d[:, :]),
        writes=["onesrowb"], tag="c.orb")
    DMA("sync", lambda e: e.dma_start(out=epscol[:, :], in_=epscol_d[:, :]),
        writes=["epscol"], tag="c.eps")
    DMA("sync", lambda e: e.dma_start(out=zerorow[:, :], in_=zerorow_d[:, :]),
        writes=["zerorow"], tag="c.zr")
    DMA("sync", lambda e: e.dma_start(out=onescolb[:, :], in_=onescolb_d[:, :]),
        writes=["onescolb"], tag="c.ocb")

    def xrow(par):
        return STRM_BF[:, par * 2048:(par + 1) * 2048]

    if "A" in PHASES:
        for tt in range(16):
            par = tt % 2
            DMA("gpsimd",
                lambda e, par=par, tt=tt: e.dma_start(
                    out=xrow(par)[:, :], in_=xb_d[tt * 128:(tt + 1) * 128, :]),
                reads=[], writes=[f"xrow.{par}"], tag=f"A.dma{tt}")
            for g in range(4):
                pk = ("PB0", "PB1")[g % 2]
                pt = (PB0, PB1)[g % 2]
                for k4 in range(4):
                    k = g * 4 + k4
                    E("tensor",
                      lambda e, par=par, k=k, k4=k4, pt=pt: e.matmul(
                          pt[:, k4 * 128:(k4 + 1) * 128],
                          xrow(par)[:, k * 128:(k + 1) * 128],
                          ident[:, :], start=True, stop=True),
                      reads=[f"xrow.{par}", "ident"], writes=[pk],
                      tag=f"A.T{tt}.{k}")
                for k4 in range(4):
                    k = g * 4 + k4
                    E("vector",
                      lambda e, k=k, k4=k4, tt=tt, pt=pt: e.tensor_copy(
                          out=XBUF[:, k * 2048 + tt * 128: k * 2048 + (tt + 1) * 128],
                          in_=pt[:, k4 * 128:(k4 + 1) * 128]),
                      reads=[pk], writes=[f"xT.{k}.{tt}"], tag=f"A.E{tt}.{k}")

    def xT(k):
        return XBUF[:, k * 2048:(k + 1) * 2048]

    def xT_keys(rng=range(16)):
        return [f"xT.{k}.{t}" for k in rng for t in range(16)]

    if "B0" in PHASES:
        DMA("sync", lambda e: e.dma_start(
            out=bvf32[:, :], in_=bv_d.rearrange("(p j) -> p j", p=1)),
            writes=["bvf32"], tag="B0.bv")
        E("scalar", lambda e: e.copy(out=bvhi[:, :], in_=bvf32[:, :]),
          reads=["bvf32"], writes=["bvhi"], tag="B0.bvhi")
        E("vector", lambda e: e.tensor_tensor(
            out=bvf32[:, :], in0=bvf32[:, :], in1=bvhi[:, :], op=ALU.subtract),
          reads=["bvf32", "bvhi"], writes=["bvf32"], tag="B0.bvsub")
        E("scalar", lambda e: e.copy(out=bvlo[:, :], in_=bvf32[:, :]),
          reads=["bvf32"], writes=["bvlo"], tag="B0.bvlo")
        DMA("gpsimd", lambda e: e.dma_start(
            out=HBUF[:, :].rearrange("p (k m) -> p k m", k=16),
            in_=wv_d.rearrange("(k p) m -> p k m", p=128)),
            writes=["wvbf"], tag="B0.wv")
        for tt in range(16):
            for vc in range(2):
                pk = ("P6", "P7")[(tt * 2 + vc) % 2]
                pt = (P6, P7)[(tt * 2 + vc) % 2]
                for k in range(16):
                    E("tensor",
                      lambda e, k=k, tt=tt, vc=vc, pt=pt: e.matmul(
                          pt[:, :],
                          XBUF[:, k * 2048 + tt * 128: k * 2048 + (tt + 1) * 128],
                          HBUF[:, k * 1024 + vc * 512: k * 1024 + (vc + 1) * 512],
                          start=(k == 0), stop=False),
                      reads=[f"xT.{k}.{tt}", "wvbf"], writes=[pk],
                      tag=f"B0.mm{tt}.{vc}.{k}")
                E("tensor", lambda e, vc=vc, pt=pt: e.matmul(
                    pt[:, :], onesrowb[:, :],
                    bvhi[:, vc * 512:(vc + 1) * 512], start=False, stop=False),
                  reads=["onesrowb", "bvhi"], writes=[pk], tag=f"B0.bh{tt}.{vc}")
                E("tensor", lambda e, vc=vc, pt=pt: e.matmul(
                    pt[:, :], onesrowb[:, :],
                    bvlo[:, vc * 512:(vc + 1) * 512], start=False, stop=True),
                  reads=["onesrowb", "bvlo"], writes=[pk], tag=f"B0.bl{tt}.{vc}")
                E("vector", lambda e, tt=tt, vc=vc, pt=pt: e.tensor_copy(
                    out=VBUF[:, tt * 1024 + vc * 512: tt * 1024 + (vc + 1) * 512],
                    in_=pt[:, :]),
                  reads=[pk], writes=[f"V.{tt}.{vc}"], tag=f"B0.ev{tt}.{vc}")

    def QT(par):
        return HBUF[:, par * 2048:(par + 1) * 2048]

    def KT(par):
        return HBUF[:, 4096 + par * 2048: 4096 + (par + 1) * 2048]

    def wqh(par):
        return STRM_BF[:, par * 2048:(par + 1) * 2048]

    def wkh(par):
        return STRM_BF[:, 4096 + par * 2048: 4096 + (par + 1) * 2048]

    if "BC" in PHASES:
        sch.fence()
        ptc = [0]

        def emit_wdma(h):
            par = h % 2
            DMA("gpsimd", lambda e, h=h, par=par: e.dma_start(
                out=wqh(par)[:, :].rearrange("p (k j) -> p k j", k=16),
                in_=wq_d.rearrange("(k p) m -> p k m", p=128)[:, :, h * 128:(h + 1) * 128]),
                writes=[f"wqh.{par}"], tag=f"B.wq{h}")
            DMA("gpsimd", lambda e, h=h, par=par: e.dma_start(
                out=wkh(par)[:, :].rearrange("p (k j) -> p k j", k=16),
                in_=wk_d.rearrange("(k p) m -> p k m", p=128)[:, :, h * 128:(h + 1) * 128]),
                writes=[f"wkh.{par}"], tag=f"B.wk{h}")
            DMA("sync", lambda e, h=h, par=par: e.dma_start(
                out=bqcol[par][:, :],
                in_=bq_d.rearrange("(p j) -> p j", j=1)[h * 128:(h + 1) * 128, :]),
                writes=[f"bqc.{par}"], tag=f"B.bq{h}")
            DMA("sync", lambda e, h=h, par=par: e.dma_start(
                out=bkcol[par][:, :],
                in_=bk_d.rearrange("(p j) -> p j", j=1)[h * 128:(h + 1) * 128, :]),
                writes=[f"bkc.{par}"], tag=f"B.bk{h}")

        emit_wdma(0)
        pend_norm = [None]
        rspar = [0]
        for h in range(HPC):
            par = h % 2
            if h + 1 < HPC and h + 1 not in (4, 5):
                emit_wdma(h + 1)
            for (wfn, wkey, dstfn, dkey, bcol, bkey) in (
                    (wqh, f"wqh.{par}", QT, f"QT.{par}", bqcol[par], f"bqc.{par}"),
                    (wkh, f"wkh.{par}", KT, f"KT.{par}", bkcol[par], f"bkc.{par}")):
                for tc in range(4):
                    pk = ("PB0", "PB1")[tc % 2]
                    pt = (PB0, PB1)[tc % 2]
                    for k in range(16):
                        E("tensor", lambda e, k=k, tc=tc, par=par, wfn=wfn, pt=pt: e.matmul(
                            pt[:, :], wfn(par)[:, k * 128:(k + 1) * 128],
                            XBUF[:, k * 2048 + tc * 512: k * 2048 + (tc + 1) * 512],
                            start=(k == 0), stop=(k == 15)),
                          reads=[wkey, f"xT.{k}.{4 * tc}", f"xT.{k}.{4 * tc + 1}",
                                 f"xT.{k}.{4 * tc + 2}", f"xT.{k}.{4 * tc + 3}"],
                          writes=[pk], tag=f"B.qk{h}.{tc}.{k}")
                    if tc % 2 == 0:
                        E("scalar", lambda e, tc=tc, par=par, dstfn=dstfn, bcol=bcol, pt=pt:
                          e.activation(out=dstfn(par)[:, tc * 512:(tc + 1) * 512],
                                       in_=pt[:, :], func=AF.Identity,
                                       bias=bcol[:, 0:1], scale=1.0),
                          reads=[pk, bkey], writes=[dkey], tag=f"B.ev{h}.{tc}")
                    else:
                        E("vector", lambda e, tc=tc, par=par, dstfn=dstfn, bcol=bcol, pt=pt:
                          e.tensor_scalar(out=dstfn(par)[:, tc * 512:(tc + 1) * 512],
                                          in0=pt[:, :], scalar1=bcol[:, 0:1],
                                          scalar2=None, op0=ALU.add),
                          reads=[pk, bkey], writes=[dkey], tag=f"B.evv{h}.{tc}")
                    if tc == 0 and pend_norm[0] is not None:
                        pend_norm[0]()
                        pend_norm[0] = None
            if h == 3:
                emit_wdma(4)
                emit_wdma(5)
            if h == 7 and "D" in PHASES:
                for g in range(16):
                    DMA("gpsimd", lambda e, g=g: e.dma_start(
                        out=XBUF[:, g * 2048:(g + 1) * 2048],
                        in_=wo_d[g * 128:(g + 1) * 128, :]),
                        writes=[f"wob.{g}"] + [f"xT.{g}.{t}" for t in range(16)],
                        tag=f"D.wo{g}")

            for qg in range(4):
                pok = ("PO0", "PO1")[qg % 2]
                pot = (PO0, PO1)[qg % 2]
                nkc = qg * 4 + 4
                psr = ("PS0", "PS1", "PB0", "PB1")
                pst_r = (PS0, PS1, PB0, PB1)
                ptis = [(ptc[0] + kc) % 5 for kc in range(nkc)]
                ptc[0] += nkc

                def emit_smm(kc, qg=qg, par=par, psr=psr, pst_r=pst_r):
                    E("tensor", lambda e, kc=kc, qg=qg, par=par,
                      pst=pst_r[kc % 4]: e.matmul(
                        pst[:, :], KT(par)[:, kc * 128:(kc + 1) * 128],
                        QT(par)[:, qg * 512:(qg + 1) * 512],
                        start=True, stop=True),
                      reads=[f"KT.{par}", f"QT.{par}"], writes=[psr[kc % 4]],
                      tag=f"C.s{h}.{qg}.{kc}")

                for kk in range(min(3, nkc)):
                    emit_smm(kk)
                for kc in range(nkc):
                    if kc == 3 and pend_norm[0] is not None:
                        pend_norm[0]()
                        pend_norm[0] = None
                    pti = ptis[kc]
                    psk = psr[kc % 4]
                    pst = pst_r[kc % 4]
                    E("scalar", lambda e, pti=pti, pst=pst: e.activation(
                        out=pT[pti][:, :], in_=pst[:, :], func=AF.Exp, scale=CSOFT),
                      reads=[psk], writes=[f"pT.{pti}"], tag=f"C.e{h}.{qg}.{kc}")
                    if kc >= qg * 4:
                        j = kc - qg * 4
                        E("vector", lambda e, pti=pti, j=j: e.tensor_tensor(
                            out=pT[pti][:, :], in0=pT[pti][:, :],
                            in1=maskm[:, 512 - j * 128: 1024 - j * 128], op=ALU.mult),
                          reads=[f"pT.{pti}", "maskm"], writes=[f"pT.{pti}"],
                          tag=f"C.m{h}.{qg}.{kc}")
                    if kc + 3 < nkc:
                        emit_smm(kc + 3)
                    E("tensor", lambda e, kc=kc, h=h, pti=pti, pot=pot, nkc=nkc: e.matmul(
                        pot[:, :], VBUF[:, kc * 1024 + h * 128: kc * 1024 + (h + 1) * 128],
                        pT[pti][:, :], start=(kc == 0), stop=(kc == nkc - 1)),
                      reads=[f"V.{kc}.0", f"V.{kc}.1", f"pT.{pti}"], writes=[pok],
                      tag=f"C.pv{h}.{qg}.{kc}")
                    E("tensor", lambda e, kc=kc, pti=pti, nkc=nkc,
                      rp=rspar[0]: e.matmul(
                        P6[32 * rp:32 * rp + 1, :], onescolb[:, :], pT[pti][:, :],
                        start=(kc == 0), stop=(kc == nkc - 1)),
                      reads=["onescolb", f"pT.{pti}"], writes=[f"P6r.{rspar[0]}"],
                      tag=f"C.rs{h}.{qg}.{kc}")

                def norm(h=h, qg=qg, pok=pok, pot=pot, rp=None):
                    rp = rspar[0]
                    E("vector", lambda e, rp=rp: e.reciprocal(
                        out=rr_sb[:, :], in_=P6[32 * rp:32 * rp + 1, :]),
                      reads=[f"P6r.{rp}"], writes=["rr"], tag=f"C.rcp{h}.{qg}")
                    E("tensor", lambda e: e.matmul(
                        P7[:, :], onesrowf[:, :], rr_sb[:, :], start=True, stop=True),
                      reads=["onesrowf", "rr"], writes=["P7b"], tag=f"C.bc{h}.{qg}")
                    E("vector", lambda e: e.tensor_copy(out=bc_sb[:, :], in_=P7[:, :]),
                      reads=["P7b"], writes=["bcs"], tag=f"C.bce{h}.{qg}")
                    E("vector", lambda e, qg=qg, pot=pot: e.tensor_tensor(
                        out=otr[qg % 2][:, :], in0=pot[:, :], in1=bc_sb[:, :], op=ALU.mult),
                      reads=[pok, "bcs"], writes=[f"otr.{qg % 2}"], tag=f"C.ot{h}.{qg}")
                    cc_in = (ccA_in if h < 4 else ccB_in)
                    DMA("scalar", lambda e, h=h, qg=qg, cc_in=cc_in: e.dma_start(
                        out=cc_in.ap()[h % 4, :, qg * 512:(qg + 1) * 512],
                        in_=otr[qg % 2][:, :]),
                        reads=[f"otr.{qg % 2}"],
                        writes=[f"d:cc{'A' if h < 4 else 'B'}.{h % 4}.{qg}"],
                        tag=f"C.cw{h}.{qg}")

                def norm_fix(norm=norm, rp=rspar[0]):
                    def g():
                        sv = rspar[0]
                        rspar[0] = rp
                        norm()
                        rspar[0] = sv
                    return g
                pend_norm[0] = norm_fix()
                rspar[0] ^= 1
            if h == 3:
                pend_norm[0]()
                pend_norm[0] = None
            if h == 3:
                E("gpsimd", lambda e: e.collective_compute(
                    "AllGather", ALU.bypass, replica_groups=RG_PAIRS,
                    ins=[ccA_in.ap().opt()], outs=[ccA_out.ap().opt()]),
                    reads=[f"d:ccA.{i}.{q}" for i in range(4) for q in range(4)],
                    writes=["d:ccoA"], kind="cc", tag="CCA")
            if h == 7:
                pend_norm[0]()
                pend_norm[0] = None
                E("gpsimd", lambda e: e.collective_compute(
                    "AllGather", ALU.bypass, replica_groups=RG_PAIRS,
                    ins=[ccB_in.ap().opt()], outs=[ccB_out.ap().opt()]),
                    reads=[f"d:ccB.{i}.{q}" for i in range(4) for q in range(4)],
                    writes=["d:ccoB"], kind="cc", tag="CCB")

    def h1T_view():
        return HBUF

    def wob(g):
        return XBUF[:, g * 2048:(g + 1) * 2048]

    def olh(g):
        return VBUF[:, g * 1024:(g + 1) * 1024]

    def xresb(par):
        return STRM_F32[:, par * 2048:(par + 1) * 2048]

    if "D" in PHASES:
        sch.fence()
        DMA("sync", lambda e: e.dma_start(
            out=boutf32[:, :], in_=bo_d.rearrange("(p j) -> p j", p=1)),
            writes=["bout"], tag="D.bo")
        GORD = [0, 1, 2, 3, 8, 9, 10, 11, 4, 5, 6, 7, 12, 13, 14, 15]
        for g in GORD:
            blk, hh = g // 8, g % 8
            cco = ccA_out if hh < 4 else ccB_out
            DMA("sync", lambda e, blk=blk, hh=hh, cco=cco, g=g: e.dma_start(
                out=ccraw[g % 3][:, :], in_=cco.ap()[blk, hh % 4, :, :]),
                reads=["d:ccoA" if hh < 4 else "d:ccoB"],
                writes=[f"ccraw.{g % 3}"], tag=f"D.cr{g}")
            E("vector", lambda e, g=g: e.select(
                out=olh(g)[:, :], mask=msku8[:, :],
                on_true=ccraw[g % 3][:, 1024:2048],
                on_false=ccraw[g % 3][:, 0:1024]),
              reads=[f"ccraw.{g % 3}", "msku8"], writes=[f"olh.{g}"],
              tag=f"D.sel{g}")
        def emit_h1T(tc):
            for g4 in range(4):
                pk = ("PB0", "PB1")[g4 % 2]
                pt = (PB0, PB1)[g4 % 2]
                for k4 in range(4):
                    k = g4 * 4 + k4
                    E("tensor", lambda e, k=k, k4=k4, tc=tc, pt=pt: e.matmul(
                        pt[:, k4 * 128:(k4 + 1) * 128],
                        h1bf[tc % 2][:, k * 128:(k + 1) * 128], ident[:, :],
                        start=True, stop=True),
                      reads=[f"h1bfk.{tc % 2}", "ident"], writes=[pk],
                      tag=f"D.hT{tc}.{k}")
                for k4 in range(4):
                    k = g4 * 4 + k4
                    E("vector", lambda e, k=k, k4=k4, tc=tc, pt=pt: e.tensor_copy(
                        out=HBUF[:, k * 1024 + tc * 128: k * 1024 + (tc + 1) * 128],
                        in_=pt[:, k4 * 128:(k4 + 1) * 128]),
                      reads=[pk], writes=[f"h1T.{k}.{tc}"], tag=f"D.hE{tc}.{k}")

        for tc in range(8):
            par = tc % 2
            DMA("sync", lambda e, tc=tc, par=par: e.dma_start(
                out=xresb(par)[:, :], in_=xres_d[tc * 128:(tc + 1) * 128, :]),
                writes=[f"xresb.{par}"], tag=f"D.xr{tc}")
            for ng in range(4):
                pk = ("PS0", "PS1", "PO0", "PO1")[ng]
                pt = (PS0, PS1, PO0, PO1)[ng]
                for gx, g in enumerate(GORD):
                    E("tensor", lambda e, g=g, tc=tc, ng=ng, pt=pt, gx=gx: e.matmul(
                        pt[:, :], olh(g)[:, tc * 128:(tc + 1) * 128],
                        wob(g)[:, ng * 512:(ng + 1) * 512],
                        start=(gx == 0), stop=False),
                      reads=[f"olh.{g}", f"wob.{g}"], writes=[pk],
                      tag=f"D.mm{tc}.{ng}.{g}")
                E("tensor", lambda e, ng=ng, pt=pt: e.matmul(
                    pt[:, :], onesrowf[:, :], boutf32[:, ng * 512:(ng + 1) * 512],
                    start=False, stop=True),
                  reads=["onesrowf", "bout"], writes=[pk], tag=f"D.bo{tc}.{ng}")
                E("vector", lambda e, ng=ng, par=par, pt=pt: e.tensor_tensor(
                    out=SCR1[:, ng * 512:(ng + 1) * 512], in0=pt[:, :],
                    in1=xresb(par)[:, ng * 512:(ng + 1) * 512], op=ALU.add),
                  reads=[pk, f"xresb.{par}"], writes=[f"hpre.{ng}"],
                  tag=f"D.hp{tc}.{ng}")
            hprek = [f"hpre.{i}" for i in range(4)]
            _ln(sch, par, SCR1, SCR3, SCR2, hprek, ["sq1"], ["h1f"],
                c_s1, c_m, c_v, c_t, c_r, c_s2, f"D.ln{tc}", epscol)
            E("scalar", lambda e, tc=tc: e.copy(out=h1bf[tc % 2][:, :], in_=SCR2[:, :]),
              reads=["h1f"], writes=[f"h1bfk.{tc % 2}"], tag=f"D.h1b{tc}")
            DMA("scalar", lambda e, tc=tc: e.dma_start(
                out=h1s_d[tc * 128:(tc + 1) * 128, :], in_=SCR2[:, :]),
                reads=["h1f"], writes=[f"d:h1s.{tc}"], tag=f"D.h1s{tc}")
            if tc > 0:
                emit_h1T(tc - 1)
        emit_h1T(7)

    def wgnat(par):
        return STRM_BF[:, par * 2048:(par + 1) * 2048]

    def wunat(par):
        return STRM_BF[:, 4096 + par * 2048: 4096 + (par + 1) * 2048]

    def wgt_sb():
        return SCR3_BF[:, 0:2048]

    def wut_sb():
        return SCR3_BF[:, 2048:4096]

    def silu_scr():
        return SCR1[:, 0:1024]

    if "E" in PHASES:
        sch.fence()
        h1Tkeys = [f"h1T.{k}.{t}" for k in range(16) for t in range(8)] + ["reg:H"]

        def wgt_pb(par):
            return SCR3_BF[:, par * 2048:(par + 1) * 2048]

        def wut_pb(par):
            return SCR2_BF[:, par * 2048:(par + 1) * 2048]

        def e_dma(ic):
            par = ic % 2
            DMA("gpsimd", lambda e, ic=ic, par=par: e.dma_start(
                out=wgnat(par)[:, :], in_=gw_d[ic * 128:(ic + 1) * 128, :]),
                writes=[f"wgn.{par}"], tag=f"E.gw{ic}")
            DMA("gpsimd", lambda e, ic=ic, par=par: e.dma_start(
                out=wunat(par)[:, :], in_=uw_d[ic * 128:(ic + 1) * 128, :]),
                writes=[f"wun.{par}"], tag=f"E.uw{ic}")
            DMA("sync", lambda e, ic=ic, par=par: e.dma_start(
                out=gscol[par][:, :],
                in_=gs_d.rearrange("(p j) -> p j", j=1)[ic * 128:(ic + 1) * 128, :]),
                writes=[f"gsc.{par}"], tag=f"E.gs{ic}")
            DMA("sync", lambda e, ic=ic, par=par: e.dma_start(
                out=uscol[par][:, :],
                in_=us_d.rearrange("(p j) -> p j", j=1)[ic * 128:(ic + 1) * 128, :]),
                writes=[f"usc.{par}"], tag=f"E.us{ic}")

        def e_deq(ic):
            par = ic % 2
            deqeng = "gpsimd" if POOL_DEQ else "vector"
            E(deqeng, lambda e, par=par: e.tensor_scalar(
                out=wgnat(par)[:, :], in0=wgnat(par)[:, :], scalar1=128.0,
                scalar2=gscol[par][:, 0:1], op0=ALU.subtract, op1=ALU.mult),
              reads=[f"wgn.{par}", f"gsc.{par}"], writes=[f"wgn.{par}"],
              tag=f"E.dq{ic}")
            E(deqeng, lambda e, par=par: e.tensor_scalar(
                out=wunat(par)[:, :], in0=wunat(par)[:, :], scalar1=128.0,
                scalar2=uscol[par][:, 0:1], op0=ALU.subtract, op1=ALU.mult),
              reads=[f"wun.{par}", f"usc.{par}"], writes=[f"wun.{par}"],
              tag=f"E.du{ic}")

        def e_trans(ic):
            par = ic % 2
            gg = 0
            for (nat, natk, dst, dstk) in ((wgnat, f"wgn.{par}", wgt_pb, "wgt"),
                                           (wunat, f"wun.{par}", wut_pb, "wut")):
                for g4 in range(4):
                    pk = ("PB0", "PB1", "P6e", "P7e")[gg % 4]
                    pt = (PB0, PB1, P6, P7)[gg % 4]
                    for k4 in range(4):
                        k = g4 * 4 + k4
                        E("tensor", lambda e, k=k, k4=k4, par=par, nat=nat, pt=pt:
                          e.matmul(pt[:, k4 * 128:(k4 + 1) * 128],
                                   nat(par)[:, k * 128:(k + 1) * 128],
                                   ident[:, :], start=True, stop=True),
                          reads=[natk, "ident"], writes=[pk], tag=f"E.T{ic}.{k}")
                    if gg % 2 == 0:
                        E("vector", lambda e, g4=g4, dst=dst, par=par, pt=pt: e.tensor_copy(
                            out=dst(par)[:, g4 * 512:(g4 + 1) * 512], in_=pt[:, :]),
                          reads=[pk], writes=[f"{dstk}.{par}.{g4}"], tag=f"E.TE{ic}.{g4}")
                    else:
                        E("scalar", lambda e, g4=g4, dst=dst, par=par, pt=pt: e.copy(
                            out=dst(par)[:, g4 * 512:(g4 + 1) * 512], in_=pt[:, :]),
                          reads=[pk], writes=[f"{dstk}.{par}.{g4}"], tag=f"E.TEs{ic}.{g4}")
                    gg += 1

        e_dma(0)
        e_deq(0)
        e_trans(0)
        for ic in range(64):
            par = ic % 2
            if ic + 1 < 64:
                e_dma(ic + 1)
                e_deq(ic + 1)
                e_trans(ic + 1)
            wgtk = [f"wgt.{par}.{i}" for i in range(4)]
            wutk = [f"wut.{par}.{i}" for i in range(4)]
            for c in range(16):
                for th in range(2):
                    E("tensor", lambda e, c=c, th=th, par=par: e.matmul(
                        (PS0, PS1)[th][:, :], wgt_pb(par)[:, c * 128:(c + 1) * 128],
                        HBUF[:, c * 1024 + th * 512: c * 1024 + (th + 1) * 512],
                        start=(c == 0), stop=(c == 15)),
                      reads=wgtk + h1Tkeys, writes=[("PS0", "PS1")[th]],
                      tag=f"E.g{ic}.{c}.{th}")
                for th in range(2):
                    E("tensor", lambda e, c=c, th=th, par=par: e.matmul(
                        (PO0, PO1)[th][:, :], wut_pb(par)[:, c * 128:(c + 1) * 128],
                        HBUF[:, c * 1024 + th * 512: c * 1024 + (th + 1) * 512],
                        start=(c == 0), stop=(c == 15)),
                      reads=wutk + h1Tkeys, writes=[("PO0", "PO1")[th]],
                      tag=f"E.u{ic}.{c}.{th}")
            for th in range(2):
                E("scalar", lambda e, th=th: e.activation(
                    out=silu_scr()[:, th * 512:(th + 1) * 512],
                    in_=(PS0, PS1)[th][:, :], func=AF.Silu),
                  reads=[("PS0", "PS1")[th]], writes=[f"silu.{th}"],
                  tag=f"E.si{ic}.{th}")
                E("vector", lambda e, ic=ic, th=th: e.tensor_tensor(
                    out=aT_view(ic)[:, th * 512:(th + 1) * 512],
                    in0=silu_scr()[:, th * 512:(th + 1) * 512],
                    in1=(PO0, PO1)[th][:, :], op=ALU.mult),
                  reads=[f"silu.{th}", ("PO0", "PO1")[th]],
                  writes=[aT_key(ic)], tag=f"E.a{ic}.{th}")

    def wdnat(hslot, par):
        return STRM_BF[:, (hslot * 2 + par) * 2048:(hslot * 2 + par + 1) * 2048]

    def wdT(par):
        return SCR3_BF[:, par * 256:(par + 1) * 256]

    def ztile(par):
        return SCR2[:, par * 256:(par + 1) * 256]

    def h1sl(par):
        return SCR2[:, 1024 + par * 256: 1024 + (par + 1) * 256]

    def ztile4(i):
        return SCR2[:, i * 256:(i + 1) * 256]

    def h1sl4(i):
        return SCR2[:, 1024 + i * 256: 1024 + (i + 1) * 256]

    if "DOWN" in PHASES:
        sch.fence()
        if DBG:
            for ic in range(64):
                DMA("scalar", lambda e, ic=ic: e.dma_start(
                    out=dbg_at[ic * 128:(ic + 1) * 128, :], in_=aT_view(ic)[:, :]),
                    reads=[aT_key(ic)], writes=[f"d:dbgat.{ic}"], tag=f"dbgat{ic}")

        def w_dma(hc, iq):
            par = iq % 2
            for hs in range(2):
                H = hc * 2 + hs
                DMA("gpsimd", lambda e, H=H, iq=iq, hs=hs, par=par: e.dma_start(
                    out=wdnat(hs, par)[:, :],
                    in_=dw_d[H * 128:(H + 1) * 128, iq * 2048:(iq + 1) * 2048]),
                    writes=[f"wdn.{hs}.{par}"], tag=f"W.dw{hc}.{iq}.{hs}")

        def w_deq(hc, iq):
            par = iq % 2
            for hs in range(2):
                E("gpsimd" if POOL_DEQ else "vector", lambda e, hs=hs, par=par: e.tensor_scalar(
                    out=wdnat(hs, par)[:, :], in0=wdnat(hs, par)[:, :],
                    scalar1=128.0, scalar2=dscol[hs][:, 0:1],
                    op0=ALU.subtract, op1=ALU.mult),
                  reads=[f"wdn.{hs}.{par}", f"dsc.{hs}"],
                  writes=[f"wdn.{hs}.{par}"], tag=f"W.dq{hc}.{iq}.{hs}")

        def w_trans(hc, gi):
            iq, c = gi // 16, gi % 16
            par, cp = iq % 2, gi % 2
            pbt = (PB0, PB1)[cp]
            for hs in range(2):
                E("tensor", lambda e, c=c, hs=hs, par=par, pbt=pbt: e.matmul(
                    pbt[:, hs * 128:(hs + 1) * 128],
                    wdnat(hs, par)[:, c * 128:(c + 1) * 128],
                    ident[:, :], start=True, stop=True),
                  reads=[f"wdn.{hs}.{par}", "ident"],
                  writes=[f"PBd.{cp}"], tag=f"W.T{hc}.{gi}.{hs}")
            E("vector", lambda e, cp=cp, pbt=pbt: e.tensor_copy(
                out=wdT(cp)[:, :], in_=pbt[:, 0:256]),
              reads=[f"PBd.{cp}"], writes=[f"wdT.{cp}"], tag=f"W.TE{hc}.{gi}")

        for hc in range(8):
            for hs in range(2):
                H = hc * 2 + hs
                DMA("sync", lambda e, H=H, hs=hs: e.dma_start(
                    out=dscol[hs][:, :],
                    in_=ds_d.rearrange("(p j) -> p j", j=1)[H * 128:(H + 1) * 128, :]),
                    writes=[f"dsc.{hs}"], tag=f"W.ds{hc}.{hs}")
            if hc == 0:
                w_dma(hc, 0)
            w_deq(hc, 0)
            w_trans(hc, 0)
            for bi in range(4):
                pt = (PS0, PS1, PO0, PO1)[bi]
                pn = ("PS0", "PS1", "PO0", "PO1")[bi]
                E("tensor", lambda e, pt=pt: e.matmul(
                    pt[:, :], zerorow[:, :], maskm[0:1, 0:512],
                    start=True, stop=True),
                  reads=["zerorow", "maskm"],
                  writes=[pn + ".0", pn + ".1"], tag=f"W.z0{hc}.{bi}")
            for gi in range(64):
                iq, c = gi // 16, gi % 16
                cp = gi % 2
                if c == 0 and iq + 1 < 4:
                    w_dma(hc, iq + 1)
                if c == 0 and iq == 3 and hc + 1 < 8:
                    w_dma(hc + 1, 0)
                if gi + 1 < 64:
                    if (gi + 1) % 16 == 0:
                        w_deq(hc, iq + 1)
                    w_trans(hc, gi + 1)
                for tc in range(8):
                    pt = (PS0, PS1, PO0, PO1)[tc // 2]
                    pk = ("PS0", "PS1", "PO0", "PO1")[tc // 2] + f".{tc % 2}"
                    E("tensor", lambda e, gi=gi, tc=tc, cp=cp, pt=pt: e.matmul(
                        pt[:, (tc % 2) * 256:(tc % 2 + 1) * 256],
                        aT_view(gi)[:, tc * 128:(tc + 1) * 128],
                        wdT(cp)[:, :], start=False, stop=(gi == 63)),
                      reads=[aT_key(gi), f"wdT.{cp}"], writes=[pk],
                      tag=f"W.y{hc}.{gi}.{tc}")
            for tc in range(8):
                DMA("sync", lambda e, tc=tc, hc=hc: e.dma_start(
                    out=h1sl4(tc % 4)[:, :],
                    in_=h1s_d[tc * 128:(tc + 1) * 128, hc * 256:(hc + 1) * 256]),
                    writes=[f"h1sl.{tc % 4}"], tag=f"W.h1{hc}.{tc}")
                pt = (PS0, PS1, PO0, PO1)[tc // 2]
                pk = ("PS0", "PS1", "PO0", "PO1")[tc // 2] + f".{tc % 2}"
                E("vector", lambda e, tc=tc, pt=pt: e.tensor_tensor(
                    out=ztile4(tc % 4)[:, :], in0=pt[:, (tc % 2) * 256:(tc % 2 + 1) * 256],
                    in1=h1sl4(tc % 4)[:, :], op=ALU.add),
                  reads=[pk, f"h1sl.{tc % 4}"], writes=[f"zt.{tc % 4}"],
                  tag=f"W.z{hc}.{tc}")
                DMA("scalar", lambda e, tc=tc, hc=hc: e.dma_start(
                    out=zs_d[tc * 128:(tc + 1) * 128, hc * 256:(hc + 1) * 256],
                    in_=ztile4(tc % 4)[:, :]),
                    reads=[f"zt.{tc % 4}"], writes=[f"d:zs.{tc}.{hc}"],
                    tag=f"W.zs{hc}.{tc}")

    if DBG:
        dbg_at = nc.dram_tensor("dbg_at", [IFF, TOK], BF16, kind="ExternalOutput").ap()
        dbg_h1 = nc.dram_tensor("dbg_h1", [TOK, D], F32, kind="ExternalOutput").ap()
        dbg_z = nc.dram_tensor("dbg_z", [TOK, D], F32, kind="ExternalOutput").ap()
        dbg_ot = nc.dram_tensor("dbg_ot", [2, 4, 128, S], BF16, kind="ExternalOutput").ap()

    if "FIN" in PHASES:
        zrow_f = [HBUF_F32[:, i * 2048:(i + 1) * 2048] for i in range(2)]
        sq_f = HBUF_F32[:, 2 * 2048:3 * 2048]
        yrow_f = HBUF_F32[:, 3 * 2048:4 * 2048]
        for tc in range(8):
            par = tc % 2
            DMA("sync", lambda e, tc=tc, par=par: e.dma_start(
                out=zrow_f[par][:, :], in_=zs_d[tc * 128:(tc + 1) * 128, :]),
                reads=[f"d:zs.{tc}.{hcv}" for hcv in range(8)],
                writes=[f"zrow.{par}", "reg:H"], tag=f"F.z{tc}")
            _ln(sch, par, zrow_f[par], sq_f, yrow_f, [f"zrow.{par}"], ["sqf"], ["yrow"],
                c_s1, c_m, c_v, c_t, c_r, c_s2, f"F.ln{tc}", epscol)
            DMA("scalar", lambda e, tc=tc: e.dma_start(
                out=yout_d[tc * 128:(tc + 1) * 128, :], in_=yrow_f[:, :]),
                reads=["yrow"], writes=[f"d:y.{tc}"], tag=f"F.y{tc}")

    sch.finalize()

    import contextlib
    with contextlib.ExitStack() as ctx:
        sems = {}
        for name in sch.finals:
            sems[name] = ctx.enter_context(nc.semaphore(name))
        block = ctx.enter_context(nc.Block())
        vclocks = {e: {} for e in ENGS}

        @block.sync
        def _(eng):
            sch.replay("sync", eng, sems, vclocks["sync"])
            for name, val in sorted(sch.finals.items()):
                if vclocks["sync"].get(name, 0) < val:
                    eng.wait_ge(sems[name], val)

        @block.scalar
        def _(eng):
            sch.replay("scalar", eng, sems, vclocks["scalar"])

        @block.vector
        def _(eng):
            sch.replay("vector", eng, sems, vclocks["vector"])

        @block.tensor
        def _(eng):
            sch.replay("tensor", eng, sems, vclocks["tensor"])

        @block.gpsimd
        def _(eng):
            sch.replay("gpsimd", eng, sems, vclocks["gpsimd"])

    return nc


def _ln(sch, par, row, sq, out, rowkeys, sqkeys, outkeys,
        c_s1, c_m, c_v, c_t, c_r, c_s2, tag, epscol=None):
    E = sch.emit
    N = 2048
    E("vector", lambda e: e.reduce_sum(out=c_s1[par][:, :], in_=row[:, :], axis=AX),
      reads=rowkeys, writes=[f"cs1.{par}"], tag=tag + ".s1")
    E("vector", lambda e: e.tensor_scalar(
        out=c_m[par][:, :], in0=c_s1[par][:, :], scalar1=1.0 / N, scalar2=None,
        op0=ALU.mult),
      reads=[f"cs1.{par}"], writes=[f"cm.{par}"], tag=tag + ".m")
    E("vector", lambda e: e.tensor_tensor(
        out=sq[:, :], in0=row[:, :], in1=row[:, :], op=ALU.mult),
      reads=rowkeys, writes=sqkeys, tag=tag + ".sq")
    E("vector", lambda e: e.reduce_sum(out=c_s2[par][:, :], in_=sq[:, :], axis=AX),
      reads=sqkeys, writes=[f"cs2.{par}"], tag=tag + ".s2")
    E("vector", lambda e: e.tensor_scalar(
        out=c_v[par][:, :], in0=c_s2[par][:, :], scalar1=1.0 / N, scalar2=None,
        op0=ALU.mult),
      reads=[f"cs2.{par}"], writes=[f"cv.{par}"], tag=tag + ".v")
    E("vector", lambda e: e.tensor_tensor(
        out=c_t[par][:, :], in0=c_m[par][:, :], in1=c_m[par][:, :], op=ALU.mult),
      reads=[f"cm.{par}"], writes=[f"ct.{par}"], tag=tag + ".m2")
    E("vector", lambda e: e.tensor_tensor(
        out=c_v[par][:, :], in0=c_v[par][:, :], in1=c_t[par][:, :], op=ALU.subtract),
      reads=[f"cv.{par}", f"ct.{par}"], writes=[f"cv.{par}"], tag=tag + ".var")
    E("scalar", lambda e: e.activation(
        out=c_t[par][:, :], in_=c_v[par][:, :], func=AF.Ln,
        bias=epscol[:, 0:1]),
      reads=[f"cv.{par}", "epscol"], writes=[f"ct.{par}"], tag=tag + ".ln")
    E("scalar", lambda e: e.activation(
        out=c_r[par][:, :], in_=c_t[par][:, :], func=AF.Exp, scale=-0.5),
      reads=[f"ct.{par}"], writes=[f"cr.{par}"], tag=tag + ".ex")
    E("vector", lambda e: e.tensor_scalar(
        out=out[:, :], in0=row[:, :], scalar1=c_m[par][:, 0:1],
        scalar2=c_r[par][:, 0:1], op0=ALU.subtract, op1=ALU.mult),
      reads=rowkeys + [f"cm.{par}", f"cr.{par}"], writes=outkeys, tag=tag + ".o")


_NC_CACHE = [None]
_LAST_RES = [None]


def kernel(L0_attn_x, L0_attn_w_in, L0_attn_b_in, L0_attn_w_out, L0_attn_b_out,
           L0_ffn_gate_w, L0_ffn_gate_scale, L0_ffn_up_w, L0_ffn_up_scale,
           L0_ffn_down_w, L0_ffn_down_scale):
    x = np.ascontiguousarray(np.asarray(L0_attn_x, dtype=np.float32))
    w_in = np.asarray(L0_attn_w_in, dtype=np.float32)
    b_in = np.asarray(L0_attn_b_in, dtype=np.float32)
    w_out = np.ascontiguousarray(np.asarray(L0_attn_w_out, dtype=np.float32))
    b_out = np.ascontiguousarray(np.asarray(L0_attn_b_out, dtype=np.float32))
    gw = np.ascontiguousarray(np.asarray(L0_ffn_gate_w, dtype=np.int32))
    gs = np.ascontiguousarray(np.asarray(L0_ffn_gate_scale, dtype=np.float32))
    uw = np.ascontiguousarray(np.asarray(L0_ffn_up_w, dtype=np.int32))
    us = np.ascontiguousarray(np.asarray(L0_ffn_up_scale, dtype=np.float32))
    dw = np.ascontiguousarray(np.asarray(L0_ffn_down_w, dtype=np.int32))
    ds = np.ascontiguousarray(np.asarray(L0_ffn_down_scale, dtype=np.float32))

    ident = np.eye(128, dtype=np.float32).astype(BFNP)
    tt, pp = np.meshgrid(np.arange(1024), np.arange(128), indexing="xy")
    maskm = (np.arange(1024)[None, :] >= 512 + np.arange(128)[:, None]).astype(np.float32).astype(BFNP)
    onesrowf = np.ones((1, 128), np.float32)
    onesrowb = np.ones((1, 128), np.float32).astype(BFNP)

    in_maps = []
    for c in range(NCORE):
        b, hg = c // 2, c % 2
        cs = hg * 1024
        m = {
            "xb": x[b],
            "wq": np.ascontiguousarray(w_in[:, cs:cs + 1024]),
            "wk": np.ascontiguousarray(w_in[:, 2048 + cs:2048 + cs + 1024]),
            "wv": np.ascontiguousarray(w_in[:, 4096 + cs:4096 + cs + 1024]),
            "bq": np.ascontiguousarray(b_in[cs:cs + 1024]),
            "bk": np.ascontiguousarray(b_in[2048 + cs:2048 + cs + 1024]),
            "bv": np.ascontiguousarray(b_in[4096 + cs:4096 + cs + 1024]),
            "wo": w_out, "bo": b_out,
            "gw": gw, "gs": gs, "uw": uw, "us": us, "dw": dw, "ds": ds,
            "xres": np.ascontiguousarray(x[b, hg * 1024:(hg + 1) * 1024]),
            "ident": ident, "maskm": maskm,
            "msku8": np.full((128, 1024), hg, np.uint8),
            "onesrowf": onesrowf, "onesrowb": onesrowb,
            "epscol": np.full((128, 1), EPS, np.float32),
            "zerorow": np.zeros((1, 128), np.float32).astype(BFNP),
            "onescolb": np.ones((128, 1), np.float32).astype(BFNP),
        }
        in_maps.append(m)

    if _NC_CACHE[0] is None:
        _NC_CACHE[0] = build_nc()
    nc = _NC_CACHE[0]
    res = run_bass_kernel_spmd(nc, in_maps, core_ids=list(range(NCORE)))
    _LAST_RES[0] = res

    out = np.empty((NB, S, D), np.float32)
    for c in range(NCORE):
        b, hg = c // 2, c % 2
        out[b, hg * 1024:(hg + 1) * 1024] = res.results[c]["yout"]
    return out


# version 19
# speedup vs baseline: 1.1492x; 1.0410x over previous
import math
import numpy as np
import ml_dtypes
import concourse.bass as bass
import concourse.mybir as mybir
from concourse.bass_utils import run_bass_kernel_spmd

BFNP = ml_dtypes.bfloat16
F32 = mybir.dt.float32
BF16 = mybir.dt.bfloat16
I32 = mybir.dt.int32
U8 = mybir.dt.uint8
AF = mybir.ActivationFunctionType
ALU = mybir.AluOpType
AX = mybir.AxisListType.X

D = 2048
S = 2048
NB = 4
NH = 16
DH = 128
HPC = 8
TOK = 1024
IFF = 8192
NCORE = 8
CSOFT = 1.0 / math.sqrt(DH)
EPS = 1e-5

PHASES = ("A", "B0", "BC", "D", "E", "DOWN", "FIN")
DBG = False
POOL_RS = False
POOL_DEQ = False

RG_PAIRS = [[0, 1], [2, 3], [4, 5], [6, 7]]

ENGS = ("sync", "scalar", "vector", "tensor", "gpsimd")


class Sched:

    def __init__(self):
        self.ops = []
        self.keys = {}
        self.fence_deps = []

    def emit(self, eng, fn, reads=(), writes=(), kind="compute", tag=""):
        opid = len(self.ops)
        deps = set(self.fence_deps)
        for k in reads:
            e = self.keys.get(k)
            if e is not None and e["w"] is not None:
                deps.add(e["w"])
        for k in writes:
            e = self.keys.get(k)
            if e is not None:
                if e["w"] is not None:
                    deps.add(e["w"])
                deps.update(e["r"])
        deps.discard(opid)
        self.ops.append({"id": opid, "eng": eng, "fn": fn, "kind": kind,
                         "deps": deps, "tag": tag})
        for k in reads:
            self.keys.setdefault(k, {"w": None, "r": []})["r"].append(opid)
        for k in writes:
            self.keys[k] = {"w": opid, "r": []}
        return opid

    def dma(self, eng, fn, reads=(), writes=(), tag=""):
        return self.emit(eng, fn, reads, writes, kind="dma", tag=tag)

    def fence(self, exclude=()):
        last = {}
        for op in self.ops:
            if op["eng"] in exclude and op["kind"] == "cc":
                continue
            last[(op["eng"], op["kind"] == "cc")] = op["id"]
        self.fence_deps = list(last.values())

    def finalize(self):
        for op in self.ops:
            if op["eng"] == "tensor":
                op["deps"] = {d for d in op["deps"]
                              if self.ops[d]["eng"] != "tensor"}
        for op in self.ops:
            byeng = {}
            keep = set()
            for d in op["deps"]:
                dop = self.ops[d]
                if dop["kind"] == "compute":
                    k = dop["eng"]
                    if k not in byeng or byeng[k] < d:
                        byeng[k] = d
                else:
                    keep.add(d)
            op["deps"] = keep | set(byeng.values())
        need = set()
        for op in self.ops:
            need.update(op["deps"])
        NP = 8
        cnt = {e: 0 for e in ENGS}
        dmacnt = {e: 0 for e in ENGS}
        ccn = 0
        for op in self.ops:
            if op["kind"] == "dma":
                i = dmacnt[op["eng"]]
                dmacnt[op["eng"]] += 1
                op["tok"] = ("q%s%d" % (op["eng"], i % NP), 16 * (i // NP + 1))
                op["inc"] = 16
            elif op["kind"] == "cc":
                ccn += 1
                op["tok"] = ("cc", ccn)
                op["inc"] = 1
            else:
                if op["id"] in need:
                    cnt[op["eng"]] += 1
                    op["tok"] = ("c" + op["eng"], cnt[op["eng"]])
                    op["inc"] = 1
                else:
                    op["tok"] = None
                    op["inc"] = 0
        self.finals = {}
        for e in ENGS:
            for s in range(NP):
                n = (dmacnt[e] - s + NP - 1) // NP if dmacnt[e] > s else 0
                if n:
                    self.finals["q%s%d" % (e, s)] = 16 * n
            if cnt[e]:
                self.finals["c" + e] = cnt[e]
        if ccn:
            self.finals["cc"] = ccn

    def replay(self, engname, eng, sems, vclock):
        for op in self.ops:
            if op["eng"] != engname:
                continue
            waits = {}
            for d in sorted(op["deps"]):
                tok = self.ops[d]["tok"]
                assert tok is not None, (op["tag"], self.ops[d]["tag"])
                sem, val = tok
                if waits.get(sem, 0) < val:
                    waits[sem] = val
            for sem, val in waits.items():
                if vclock.get(sem, 0) < val:
                    eng.wait_ge(sems[sem], val)
                    vclock[sem] = val
            inst = op["fn"](eng)
            if op["inc"]:
                sem, _ = op["tok"]
                inst.then_inc(sems[sem], op["inc"])
                if sem == "c" + engname:
                    pass


def build_nc():
    nc = bass.Bass("TRN2", target_bir_lowering=False, debug=False)

    xb_d = nc.dram_tensor("xb", [S, D], F32, kind="ExternalInput").ap()
    wq_d = nc.dram_tensor("wq", [D, TOK], F32, kind="ExternalInput").ap()
    wk_d = nc.dram_tensor("wk", [D, TOK], F32, kind="ExternalInput").ap()
    wv_d = nc.dram_tensor("wv", [D, TOK], F32, kind="ExternalInput").ap()
    bq_d = nc.dram_tensor("bq", [TOK], F32, kind="ExternalInput").ap()
    bk_d = nc.dram_tensor("bk", [TOK], F32, kind="ExternalInput").ap()
    bv_d = nc.dram_tensor("bv", [TOK], F32, kind="ExternalInput").ap()
    wo_d = nc.dram_tensor("wo", [D, D], F32, kind="ExternalInput").ap()
    bo_d = nc.dram_tensor("bo", [D], F32, kind="ExternalInput").ap()
    gw_d = nc.dram_tensor("gw", [IFF, D], I32, kind="ExternalInput").ap()
    gs_d = nc.dram_tensor("gs", [IFF], F32, kind="ExternalInput").ap()
    uw_d = nc.dram_tensor("uw", [IFF, D], I32, kind="ExternalInput").ap()
    us_d = nc.dram_tensor("us", [IFF], F32, kind="ExternalInput").ap()
    dw_d = nc.dram_tensor("dw", [D, IFF], I32, kind="ExternalInput").ap()
    ds_d = nc.dram_tensor("ds", [D], F32, kind="ExternalInput").ap()
    xres_d = nc.dram_tensor("xres", [TOK, D], F32, kind="ExternalInput").ap()
    ident_d = nc.dram_tensor("ident", [128, 128], BF16, kind="ExternalInput").ap()
    maskm_d = nc.dram_tensor("maskm", [128, 1024], BF16, kind="ExternalInput").ap()
    msku8_d = nc.dram_tensor("msku8", [128, 1024], U8, kind="ExternalInput").ap()
    onesrowf_d = nc.dram_tensor("onesrowf", [1, 128], F32, kind="ExternalInput").ap()
    onesrowb_d = nc.dram_tensor("onesrowb", [1, 128], BF16, kind="ExternalInput").ap()
    epscol_d = nc.dram_tensor("epscol", [128, 1], F32, kind="ExternalInput").ap()
    zerorow_d = nc.dram_tensor("zerorow", [1, 128], BF16, kind="ExternalInput").ap()
    onescolb_d = nc.dram_tensor("onescolb", [128, 1], BF16, kind="ExternalInput").ap()

    yout_d = nc.dram_tensor("yout", [TOK, D], F32, kind="ExternalOutput").ap()

    h1s_d = nc.dram_tensor("h1s", [TOK, D], F32, kind="Internal").ap()
    zs_d = nc.dram_tensor("zs", [TOK, D], F32, kind="Internal").ap()
    ccA_in = nc.dram_tensor("ccA_in", [4, 128, S], BF16, kind="Internal")
    ccB_in = nc.dram_tensor("ccB_in", [4, 128, S], BF16, kind="Internal")
    ccA_out = nc.dram_tensor("ccA_out", [2, 4, 128, S], BF16, kind="Internal")
    ccB_out = nc.dram_tensor("ccB_out", [2, 4, 128, S], BF16, kind="Internal")

    SBB = 16512
    off = [SBB]

    def sb(name, shape, dtype, at=None):
        nonlocal_off = off[0] if at is None else at
        t = nc.alloc_sbuf_tensor_at(name, shape, dtype, offset=nonlocal_off)
        if at is None:
            nbytes = shape[1] * {F32: 4, BF16: 2, I32: 4, U8: 1}[dtype]
            off[0] = (nonlocal_off + nbytes + 31) & ~31
        return t

    ident = sb("ident", [128, 128], BF16)
    maskm = sb("maskm", [128, 1024], BF16)
    msku8 = sb("msku8", [128, 1024], U8)
    onesrowf = sb("onesrowf", [1, 128], F32)
    onesrowb = sb("onesrowb", [1, 128], BF16)
    epscol = sb("epscol", [128, 1], F32)
    zerorow = sb("zerorow", [1, 128], BF16)
    onescolb = sb("onescolb", [128, 1], BF16)
    bqcol = [sb(f"bqcol{i}", [128, 1], F32) for i in range(2)]
    bkcol = [sb(f"bkcol{i}", [128, 1], F32) for i in range(2)]
    gscol = [sb(f"gscol{i}", [128, 1], F32) for i in range(2)]
    uscol = [sb(f"uscol{i}", [128, 1], F32) for i in range(2)]
    dscol = [sb(f"dscol{i}", [128, 1], F32) for i in range(2)]
    bvhi = sb("bvhi", [1, 1024], BF16)
    bvlo = sb("bvlo", [1, 1024], BF16)
    c_s1 = [sb(f"cs1_{i}", [128, 1], F32) for i in range(2)]
    c_m = [sb(f"cm_{i}", [128, 1], F32) for i in range(2)]
    c_v = [sb(f"cv_{i}", [128, 1], F32) for i in range(2)]
    c_t = [sb(f"ct_{i}", [128, 1], F32) for i in range(2)]
    c_r = [sb(f"cr_{i}", [128, 1], F32) for i in range(2)]
    c_s2 = [sb(f"cs2_{i}", [128, 1], F32) for i in range(2)]
    assert off[0] <= SBB + 9216, off[0]

    SCR1 = sb("SCR1", [128, 2048], F32, at=SBB + 9216)
    SCR1_BF = nc.alloc_sbuf_tensor_at("SCR1bf", [128, 4096], BF16, offset=SBB + 9216)
    SCR2 = sb("SCR2", [128, 2048], F32, at=SBB + 17408)
    SCR2_BF = nc.alloc_sbuf_tensor_at("SCR2bf", [128, 4096], BF16, offset=SBB + 17408)
    SCR3 = sb("SCR3", [128, 2048], F32, at=SBB + 25600)
    SCR3_BF = nc.alloc_sbuf_tensor_at("SCR3bf", [128, 4096], BF16, offset=SBB + 25600)
    STRM_BF = nc.alloc_sbuf_tensor_at("STRMbf", [128, 8192], BF16, offset=SBB + 33792)
    STRM_F32 = nc.alloc_sbuf_tensor_at("STRMf32", [128, 4096], F32, offset=SBB + 33792)
    XBUF = nc.alloc_sbuf_tensor_at("XBUF", [128, 32768], BF16, offset=SBB + 50176)
    VBUF = nc.alloc_sbuf_tensor_at("VBUF", [128, 16384], BF16, offset=SBB + 115712)
    HBUF = nc.alloc_sbuf_tensor_at("HBUF", [128, 16384], BF16, offset=SBB + 148480)
    HBUF_F32 = nc.alloc_sbuf_tensor_at("HBUFf32", [128, 8192], F32, offset=SBB + 148480)
    yoff = [SBB + 181248]

    def yb(name, shape, dtype):
        t = nc.alloc_sbuf_tensor_at(name, shape, dtype, offset=yoff[0])
        nbytes = shape[1] * {F32: 4, BF16: 2}[dtype] * (1 if shape[0] > 1 else 1)
        yoff[0] = (yoff[0] + nbytes + 31) & ~31
        return t

    pT = [yb(f"pT{i}", [128, 512], BF16) for i in range(5)]
    rs_part = yb("rs", [128, 512], F32)
    bc_sb = yb("bc", [128, 512], F32)
    rr_sb = yb("rr", [1, 512], F32)
    otr = [yb(f"otr{i}", [128, 512], BF16) for i in range(2)]
    bvf32 = yb("bvf32", [1, 1024], F32)
    ccraw = [nc.alloc_sbuf_tensor_at(f"ccraw{i}", [128, 2048], BF16,
                                     offset=SBB + 181248 + i * 4096) for i in range(3)]
    boutf32 = nc.alloc_sbuf_tensor_at("boutf32", [1, 2048], F32, offset=SBB + 181248 + 12288)
    h1bf = [nc.alloc_sbuf_tensor_at(f"h1bf{i}", [128, 2048], BF16,
                                    offset=SBB + 181248 + 12288 + 8192 + i * 4096) for i in range(2)]
    YAT = nc.alloc_sbuf_tensor_at("YAT", [128, 14336], BF16, offset=SBB + 181248)

    P = [nc.alloc_psum_tensor(f"P{i}", [128, 512], F32) for i in range(8)]
    PB0, PB1, PS0, PS1, PO0, PO1, P6, P7 = P

    def aT_view(ic):
        if ic < 32:
            return XBUF[:, ic * 1024:(ic + 1) * 1024]
        if ic < 48:
            return VBUF[:, (ic - 32) * 1024:(ic - 31) * 1024]
        if ic < 62:
            return YAT[:, (ic - 48) * 1024:(ic - 47) * 1024]
        return SCR1_BF[:, 2048 + (ic - 62) * 1024: 2048 + (ic - 61) * 1024]

    def aT_key(ic):
        return f"aT.{ic}"

    sch = Sched()
    E = sch.emit
    DMA = sch.dma

    DMA("sync", lambda e: e.dma_start(out=ident[:, :], in_=ident_d[:, :]),
        writes=["ident"], tag="c.ident")
    DMA("sync", lambda e: e.dma_start(out=maskm[:, :], in_=maskm_d[:, :]),
        writes=["maskm"], tag="c.maskm")
    DMA("sync", lambda e: e.dma_start(out=msku8[:, :], in_=msku8_d[:, :]),
        writes=["msku8"], tag="c.msk")
    DMA("sync", lambda e: e.dma_start(out=onesrowf[:, :], in_=onesrowf_d[:, :]),
        writes=["onesrowf"], tag="c.orf")
    DMA("sync", lambda e: e.dma_start(out=onesrowb[:, :], in_=onesrowb_d[:, :]),
        writes=["onesrowb"], tag="c.orb")
    DMA("sync", lambda e: e.dma_start(out=epscol[:, :], in_=epscol_d[:, :]),
        writes=["epscol"], tag="c.eps")
    DMA("sync", lambda e: e.dma_start(out=zerorow[:, :], in_=zerorow_d[:, :]),
        writes=["zerorow"], tag="c.zr")
    DMA("sync", lambda e: e.dma_start(out=onescolb[:, :], in_=onescolb_d[:, :]),
        writes=["onescolb"], tag="c.ocb")

    def xrow(par):
        return STRM_BF[:, par * 2048:(par + 1) * 2048]

    if "A" in PHASES:
        for tt in range(16):
            par = tt % 2
            DMA("gpsimd",
                lambda e, par=par, tt=tt: e.dma_start(
                    out=xrow(par)[:, :], in_=xb_d[tt * 128:(tt + 1) * 128, :]),
                reads=[], writes=[f"xrow.{par}"], tag=f"A.dma{tt}")
            for g in range(4):
                pk = ("PB0", "PB1")[g % 2]
                pt = (PB0, PB1)[g % 2]
                for k4 in range(4):
                    k = g * 4 + k4
                    E("tensor",
                      lambda e, par=par, k=k, k4=k4, pt=pt: e.matmul(
                          pt[:, k4 * 128:(k4 + 1) * 128],
                          xrow(par)[:, k * 128:(k + 1) * 128],
                          ident[:, :], start=True, stop=True),
                      reads=[f"xrow.{par}", "ident"], writes=[pk],
                      tag=f"A.T{tt}.{k}")
                for k4 in range(4):
                    k = g * 4 + k4
                    E("vector",
                      lambda e, k=k, k4=k4, tt=tt, pt=pt: e.tensor_copy(
                          out=XBUF[:, k * 2048 + tt * 128: k * 2048 + (tt + 1) * 128],
                          in_=pt[:, k4 * 128:(k4 + 1) * 128]),
                      reads=[pk], writes=[f"xT.{k}.{tt}"], tag=f"A.E{tt}.{k}")

    def xT(k):
        return XBUF[:, k * 2048:(k + 1) * 2048]

    def xT_keys(rng=range(16)):
        return [f"xT.{k}.{t}" for k in rng for t in range(16)]

    if "B0" in PHASES:
        DMA("sync", lambda e: e.dma_start(
            out=bvf32[:, :], in_=bv_d.rearrange("(p j) -> p j", p=1)),
            writes=["bvf32"], tag="B0.bv")
        E("scalar", lambda e: e.copy(out=bvhi[:, :], in_=bvf32[:, :]),
          reads=["bvf32"], writes=["bvhi"], tag="B0.bvhi")
        E("vector", lambda e: e.tensor_tensor(
            out=bvf32[:, :], in0=bvf32[:, :], in1=bvhi[:, :], op=ALU.subtract),
          reads=["bvf32", "bvhi"], writes=["bvf32"], tag="B0.bvsub")
        E("scalar", lambda e: e.copy(out=bvlo[:, :], in_=bvf32[:, :]),
          reads=["bvf32"], writes=["bvlo"], tag="B0.bvlo")
        DMA("gpsimd", lambda e: e.dma_start(
            out=HBUF[:, :].rearrange("p (k m) -> p k m", k=16),
            in_=wv_d.rearrange("(k p) m -> p k m", p=128)),
            writes=["wvbf"], tag="B0.wv")
        for tt in range(16):
            for vc in range(2):
                pk = ("P6", "P7")[(tt * 2 + vc) % 2]
                pt = (P6, P7)[(tt * 2 + vc) % 2]
                for k in range(16):
                    E("tensor",
                      lambda e, k=k, tt=tt, vc=vc, pt=pt: e.matmul(
                          pt[:, :],
                          XBUF[:, k * 2048 + tt * 128: k * 2048 + (tt + 1) * 128],
                          HBUF[:, k * 1024 + vc * 512: k * 1024 + (vc + 1) * 512],
                          start=(k == 0), stop=False),
                      reads=[f"xT.{k}.{tt}", "wvbf"], writes=[pk],
                      tag=f"B0.mm{tt}.{vc}.{k}")
                E("tensor", lambda e, vc=vc, pt=pt: e.matmul(
                    pt[:, :], onesrowb[:, :],
                    bvhi[:, vc * 512:(vc + 1) * 512], start=False, stop=False),
                  reads=["onesrowb", "bvhi"], writes=[pk], tag=f"B0.bh{tt}.{vc}")
                E("tensor", lambda e, vc=vc, pt=pt: e.matmul(
                    pt[:, :], onesrowb[:, :],
                    bvlo[:, vc * 512:(vc + 1) * 512], start=False, stop=True),
                  reads=["onesrowb", "bvlo"], writes=[pk], tag=f"B0.bl{tt}.{vc}")
                E("vector", lambda e, tt=tt, vc=vc, pt=pt: e.tensor_copy(
                    out=VBUF[:, tt * 1024 + vc * 512: tt * 1024 + (vc + 1) * 512],
                    in_=pt[:, :]),
                  reads=[pk], writes=[f"V.{tt}.{vc}"], tag=f"B0.ev{tt}.{vc}")

    def QT(par):
        return HBUF[:, par * 2048:(par + 1) * 2048]

    def KT(par):
        return HBUF[:, 4096 + par * 2048: 4096 + (par + 1) * 2048]

    def wqh(par):
        return STRM_BF[:, par * 2048:(par + 1) * 2048]

    def wkh(par):
        return STRM_BF[:, 4096 + par * 2048: 4096 + (par + 1) * 2048]

    if "BC" in PHASES:
        sch.fence()
        ptc = [0]

        def emit_wdma(h):
            par = h % 2
            DMA("gpsimd", lambda e, h=h, par=par: e.dma_start(
                out=wqh(par)[:, :].rearrange("p (k j) -> p k j", k=16),
                in_=wq_d.rearrange("(k p) m -> p k m", p=128)[:, :, h * 128:(h + 1) * 128]),
                writes=[f"wqh.{par}"], tag=f"B.wq{h}")
            DMA("gpsimd", lambda e, h=h, par=par: e.dma_start(
                out=wkh(par)[:, :].rearrange("p (k j) -> p k j", k=16),
                in_=wk_d.rearrange("(k p) m -> p k m", p=128)[:, :, h * 128:(h + 1) * 128]),
                writes=[f"wkh.{par}"], tag=f"B.wk{h}")
            DMA("sync", lambda e, h=h, par=par: e.dma_start(
                out=bqcol[par][:, :],
                in_=bq_d.rearrange("(p j) -> p j", j=1)[h * 128:(h + 1) * 128, :]),
                writes=[f"bqc.{par}"], tag=f"B.bq{h}")
            DMA("sync", lambda e, h=h, par=par: e.dma_start(
                out=bkcol[par][:, :],
                in_=bk_d.rearrange("(p j) -> p j", j=1)[h * 128:(h + 1) * 128, :]),
                writes=[f"bkc.{par}"], tag=f"B.bk{h}")

        emit_wdma(0)
        pend_norm = [None]
        rspar = [0]
        for h in range(HPC):
            par = h % 2
            if h + 1 < HPC and h + 1 not in (4, 5):
                emit_wdma(h + 1)
            for (wfn, wkey, dstfn, dkey, bcol, bkey) in (
                    (wqh, f"wqh.{par}", QT, f"QT.{par}", bqcol[par], f"bqc.{par}"),
                    (wkh, f"wkh.{par}", KT, f"KT.{par}", bkcol[par], f"bkc.{par}")):
                for tc in range(4):
                    pk = ("PB0", "PB1")[tc % 2]
                    pt = (PB0, PB1)[tc % 2]
                    for k in range(16):
                        E("tensor", lambda e, k=k, tc=tc, par=par, wfn=wfn, pt=pt: e.matmul(
                            pt[:, :], wfn(par)[:, k * 128:(k + 1) * 128],
                            XBUF[:, k * 2048 + tc * 512: k * 2048 + (tc + 1) * 512],
                            start=(k == 0), stop=(k == 15)),
                          reads=[wkey, f"xT.{k}.{4 * tc}", f"xT.{k}.{4 * tc + 1}",
                                 f"xT.{k}.{4 * tc + 2}", f"xT.{k}.{4 * tc + 3}"],
                          writes=[pk], tag=f"B.qk{h}.{tc}.{k}")
                    if tc % 2 == 0:
                        E("scalar", lambda e, tc=tc, par=par, dstfn=dstfn, bcol=bcol, pt=pt:
                          e.activation(out=dstfn(par)[:, tc * 512:(tc + 1) * 512],
                                       in_=pt[:, :], func=AF.Identity,
                                       bias=bcol[:, 0:1], scale=1.0),
                          reads=[pk, bkey], writes=[dkey], tag=f"B.ev{h}.{tc}")
                    else:
                        E("vector", lambda e, tc=tc, par=par, dstfn=dstfn, bcol=bcol, pt=pt:
                          e.tensor_scalar(out=dstfn(par)[:, tc * 512:(tc + 1) * 512],
                                          in0=pt[:, :], scalar1=bcol[:, 0:1],
                                          scalar2=None, op0=ALU.add),
                          reads=[pk, bkey], writes=[dkey], tag=f"B.evv{h}.{tc}")
                    if tc == 0 and pend_norm[0] is not None:
                        pend_norm[0]()
                        pend_norm[0] = None
            if h == 3:
                emit_wdma(4)
                emit_wdma(5)
            if h == 7 and "D" in PHASES:
                for g in range(16):
                    DMA("gpsimd", lambda e, g=g: e.dma_start(
                        out=XBUF[:, g * 2048:(g + 1) * 2048],
                        in_=wo_d[g * 128:(g + 1) * 128, :]),
                        writes=[f"wob.{g}"] + [f"xT.{g}.{t}" for t in range(16)],
                        tag=f"D.wo{g}")

            for qg in range(4):
                pok = ("PO0", "PO1")[qg % 2]
                pot = (PO0, PO1)[qg % 2]
                nkc = qg * 4 + 4
                psr = ("PS0", "PS1", "PB0", "PB1")
                pst_r = (PS0, PS1, PB0, PB1)
                ptis = [(ptc[0] + kc) % 5 for kc in range(nkc)]
                ptc[0] += nkc

                def emit_smm(kc, qg=qg, par=par, psr=psr, pst_r=pst_r):
                    E("tensor", lambda e, kc=kc, qg=qg, par=par,
                      pst=pst_r[kc % 4]: e.matmul(
                        pst[:, :], KT(par)[:, kc * 128:(kc + 1) * 128],
                        QT(par)[:, qg * 512:(qg + 1) * 512],
                        start=True, stop=True),
                      reads=[f"KT.{par}", f"QT.{par}"], writes=[psr[kc % 4]],
                      tag=f"C.s{h}.{qg}.{kc}")

                for kk in range(min(3, nkc)):
                    emit_smm(kk)
                for kc in range(nkc):
                    if kc == 3 and pend_norm[0] is not None:
                        pend_norm[0]()
                        pend_norm[0] = None
                    pti = ptis[kc]
                    psk = psr[kc % 4]
                    pst = pst_r[kc % 4]
                    E("scalar", lambda e, pti=pti, pst=pst: e.activation(
                        out=pT[pti][:, :], in_=pst[:, :], func=AF.Exp, scale=CSOFT),
                      reads=[psk], writes=[f"pT.{pti}"], tag=f"C.e{h}.{qg}.{kc}")
                    if kc >= qg * 4:
                        j = kc - qg * 4
                        E("vector", lambda e, pti=pti, j=j: e.tensor_tensor(
                            out=pT[pti][:, :], in0=pT[pti][:, :],
                            in1=maskm[:, 512 - j * 128: 1024 - j * 128], op=ALU.mult),
                          reads=[f"pT.{pti}", "maskm"], writes=[f"pT.{pti}"],
                          tag=f"C.m{h}.{qg}.{kc}")
                    if kc + 3 < nkc:
                        emit_smm(kc + 3)
                    E("tensor", lambda e, kc=kc, h=h, pti=pti, pot=pot, nkc=nkc: e.matmul(
                        pot[:, :], VBUF[:, kc * 1024 + h * 128: kc * 1024 + (h + 1) * 128],
                        pT[pti][:, :], start=(kc == 0), stop=(kc == nkc - 1)),
                      reads=[f"V.{kc}.0", f"V.{kc}.1", f"pT.{pti}"], writes=[pok],
                      tag=f"C.pv{h}.{qg}.{kc}")
                    E("tensor", lambda e, kc=kc, pti=pti, nkc=nkc,
                      rp=rspar[0]: e.matmul(
                        P6[32 * rp:32 * rp + 1, :], onescolb[:, :], pT[pti][:, :],
                        start=(kc == 0), stop=(kc == nkc - 1)),
                      reads=["onescolb", f"pT.{pti}"], writes=[f"P6r.{rspar[0]}"],
                      tag=f"C.rs{h}.{qg}.{kc}")

                def norm(h=h, qg=qg, pok=pok, pot=pot, rp=None):
                    rp = rspar[0]
                    E("vector", lambda e, rp=rp: e.reciprocal(
                        out=rr_sb[:, :], in_=P6[32 * rp:32 * rp + 1, :]),
                      reads=[f"P6r.{rp}"], writes=["rr"], tag=f"C.rcp{h}.{qg}")
                    E("tensor", lambda e: e.matmul(
                        P7[:, :], onesrowf[:, :], rr_sb[:, :], start=True, stop=True),
                      reads=["onesrowf", "rr"], writes=["P7b"], tag=f"C.bc{h}.{qg}")
                    E("vector", lambda e: e.tensor_copy(out=bc_sb[:, :], in_=P7[:, :]),
                      reads=["P7b"], writes=["bcs"], tag=f"C.bce{h}.{qg}")
                    E("vector", lambda e, qg=qg, pot=pot: e.tensor_tensor(
                        out=otr[qg % 2][:, :], in0=pot[:, :], in1=bc_sb[:, :], op=ALU.mult),
                      reads=[pok, "bcs"], writes=[f"otr.{qg % 2}"], tag=f"C.ot{h}.{qg}")
                    cc_in = (ccA_in if h < 4 else ccB_in)
                    DMA("scalar", lambda e, h=h, qg=qg, cc_in=cc_in: e.dma_start(
                        out=cc_in.ap()[h % 4, :, qg * 512:(qg + 1) * 512],
                        in_=otr[qg % 2][:, :]),
                        reads=[f"otr.{qg % 2}"],
                        writes=[f"d:cc{'A' if h < 4 else 'B'}.{h % 4}.{qg}"],
                        tag=f"C.cw{h}.{qg}")

                def norm_fix(norm=norm, rp=rspar[0]):
                    def g():
                        sv = rspar[0]
                        rspar[0] = rp
                        norm()
                        rspar[0] = sv
                    return g
                pend_norm[0] = norm_fix()
                rspar[0] ^= 1
            if h == 3:
                pend_norm[0]()
                pend_norm[0] = None
            if h == 3:
                E("gpsimd", lambda e: e.collective_compute(
                    "AllGather", ALU.bypass, replica_groups=RG_PAIRS,
                    ins=[ccA_in.ap().opt()], outs=[ccA_out.ap().opt()]),
                    reads=[f"d:ccA.{i}.{q}" for i in range(4) for q in range(4)],
                    writes=["d:ccoA"], kind="cc", tag="CCA")
            if h == 7:
                pend_norm[0]()
                pend_norm[0] = None
                E("gpsimd", lambda e: e.collective_compute(
                    "AllGather", ALU.bypass, replica_groups=RG_PAIRS,
                    ins=[ccB_in.ap().opt()], outs=[ccB_out.ap().opt()]),
                    reads=[f"d:ccB.{i}.{q}" for i in range(4) for q in range(4)],
                    writes=["d:ccoB"], kind="cc", tag="CCB")

    def h1T_view():
        return HBUF

    def wob(g):
        return XBUF[:, g * 2048:(g + 1) * 2048]

    def olh(g):
        return VBUF[:, g * 1024:(g + 1) * 1024]

    def xresb(par):
        return STRM_F32[:, par * 2048:(par + 1) * 2048]

    if "D" in PHASES:
        sch.fence()
        DMA("sync", lambda e: e.dma_start(
            out=boutf32[:, :], in_=bo_d.rearrange("(p j) -> p j", p=1)),
            writes=["bout"], tag="D.bo")
        GORD = [0, 1, 2, 3, 8, 9, 10, 11, 4, 5, 6, 7, 12, 13, 14, 15]
        for g in GORD:
            blk, hh = g // 8, g % 8
            cco = ccA_out if hh < 4 else ccB_out
            DMA("sync", lambda e, blk=blk, hh=hh, cco=cco, g=g: e.dma_start(
                out=ccraw[g % 3][:, :], in_=cco.ap()[blk, hh % 4, :, :]),
                reads=["d:ccoA" if hh < 4 else "d:ccoB"],
                writes=[f"ccraw.{g % 3}"], tag=f"D.cr{g}")
            E("vector", lambda e, g=g: e.select(
                out=olh(g)[:, :], mask=msku8[:, :],
                on_true=ccraw[g % 3][:, 1024:2048],
                on_false=ccraw[g % 3][:, 0:1024]),
              reads=[f"ccraw.{g % 3}", "msku8"], writes=[f"olh.{g}"],
              tag=f"D.sel{g}")
        def emit_h1T(tc):
            for g4 in range(4):
                pk = ("PB0", "PB1")[g4 % 2]
                pt = (PB0, PB1)[g4 % 2]
                for k4 in range(4):
                    k = g4 * 4 + k4
                    E("tensor", lambda e, k=k, k4=k4, tc=tc, pt=pt: e.matmul(
                        pt[:, k4 * 128:(k4 + 1) * 128],
                        h1bf[tc % 2][:, k * 128:(k + 1) * 128], ident[:, :],
                        start=True, stop=True),
                      reads=[f"h1bfk.{tc % 2}", "ident"], writes=[pk],
                      tag=f"D.hT{tc}.{k}")
                for k4 in range(4):
                    k = g4 * 4 + k4
                    E("vector", lambda e, k=k, k4=k4, tc=tc, pt=pt: e.tensor_copy(
                        out=HBUF[:, k * 1024 + tc * 128: k * 1024 + (tc + 1) * 128],
                        in_=pt[:, k4 * 128:(k4 + 1) * 128]),
                      reads=[pk], writes=[f"h1T.{k}.{tc}"], tag=f"D.hE{tc}.{k}")

        for tc in range(8):
            par = tc % 2
            DMA("sync", lambda e, tc=tc, par=par: e.dma_start(
                out=xresb(par)[:, :], in_=xres_d[tc * 128:(tc + 1) * 128, :]),
                writes=[f"xresb.{par}"], tag=f"D.xr{tc}")
            for ng in range(4):
                pk = ("PS0", "PS1", "PO0", "PO1")[ng]
                pt = (PS0, PS1, PO0, PO1)[ng]
                for gx, g in enumerate(GORD):
                    E("tensor", lambda e, g=g, tc=tc, ng=ng, pt=pt, gx=gx: e.matmul(
                        pt[:, :], olh(g)[:, tc * 128:(tc + 1) * 128],
                        wob(g)[:, ng * 512:(ng + 1) * 512],
                        start=(gx == 0), stop=False),
                      reads=[f"olh.{g}", f"wob.{g}"], writes=[pk],
                      tag=f"D.mm{tc}.{ng}.{g}")
                E("tensor", lambda e, ng=ng, pt=pt: e.matmul(
                    pt[:, :], onesrowf[:, :], boutf32[:, ng * 512:(ng + 1) * 512],
                    start=False, stop=True),
                  reads=["onesrowf", "bout"], writes=[pk], tag=f"D.bo{tc}.{ng}")
                E("vector", lambda e, ng=ng, par=par, pt=pt: e.tensor_tensor(
                    out=SCR1[:, ng * 512:(ng + 1) * 512], in0=pt[:, :],
                    in1=xresb(par)[:, ng * 512:(ng + 1) * 512], op=ALU.add),
                  reads=[pk, f"xresb.{par}"], writes=[f"hpre.{ng}"],
                  tag=f"D.hp{tc}.{ng}")
            hprek = [f"hpre.{i}" for i in range(4)]
            _ln(sch, par, SCR1, SCR3, SCR2, hprek, ["sq1"], ["h1f"],
                c_s1, c_m, c_v, c_t, c_r, c_s2, f"D.ln{tc}", epscol)
            E("scalar", lambda e, tc=tc: e.copy(out=h1bf[tc % 2][:, :], in_=SCR2[:, :]),
              reads=["h1f"], writes=[f"h1bfk.{tc % 2}"], tag=f"D.h1b{tc}")
            DMA("scalar", lambda e, tc=tc: e.dma_start(
                out=h1s_d[tc * 128:(tc + 1) * 128, :], in_=SCR2[:, :]),
                reads=["h1f"], writes=[f"d:h1s.{tc}"], tag=f"D.h1s{tc}")
            if tc > 0:
                emit_h1T(tc - 1)
        emit_h1T(7)

    def wgnat(par):
        return STRM_BF[:, par * 2048:(par + 1) * 2048]

    def wunat(par):
        return STRM_BF[:, 4096 + par * 2048: 4096 + (par + 1) * 2048]

    def wgt_sb():
        return SCR3_BF[:, 0:2048]

    def wut_sb():
        return SCR3_BF[:, 2048:4096]

    def silu_scr():
        return SCR1[:, 0:1024]

    if "E" in PHASES:
        sch.fence()
        h1Tkeys = [f"h1T.{k}.{t}" for k in range(16) for t in range(8)] + ["reg:H"]

        def wgt_pb(par):
            return SCR3_BF[:, par * 2048:(par + 1) * 2048]

        def wut_pb(par):
            return SCR2_BF[:, par * 2048:(par + 1) * 2048]

        def e_dma(ic):
            par = ic % 2
            DMA("gpsimd", lambda e, ic=ic, par=par: e.dma_start(
                out=wgnat(par)[:, :], in_=gw_d[ic * 128:(ic + 1) * 128, :]),
                writes=[f"wgn.{par}"], tag=f"E.gw{ic}")
            DMA("gpsimd", lambda e, ic=ic, par=par: e.dma_start(
                out=wunat(par)[:, :], in_=uw_d[ic * 128:(ic + 1) * 128, :]),
                writes=[f"wun.{par}"], tag=f"E.uw{ic}")
            DMA("sync", lambda e, ic=ic, par=par: e.dma_start(
                out=gscol[par][:, :],
                in_=gs_d.rearrange("(p j) -> p j", j=1)[ic * 128:(ic + 1) * 128, :]),
                writes=[f"gsc.{par}"], tag=f"E.gs{ic}")
            DMA("sync", lambda e, ic=ic, par=par: e.dma_start(
                out=uscol[par][:, :],
                in_=us_d.rearrange("(p j) -> p j", j=1)[ic * 128:(ic + 1) * 128, :]),
                writes=[f"usc.{par}"], tag=f"E.us{ic}")

        def e_deq(ic):
            par = ic % 2
            deqeng = "gpsimd" if POOL_DEQ else "vector"
            E(deqeng, lambda e, par=par: e.tensor_scalar(
                out=wgnat(par)[:, :], in0=wgnat(par)[:, :], scalar1=128.0,
                scalar2=gscol[par][:, 0:1], op0=ALU.subtract, op1=ALU.mult),
              reads=[f"wgn.{par}", f"gsc.{par}"], writes=[f"wgn.{par}"],
              tag=f"E.dq{ic}")
            E(deqeng, lambda e, par=par: e.tensor_scalar(
                out=wunat(par)[:, :], in0=wunat(par)[:, :], scalar1=128.0,
                scalar2=uscol[par][:, 0:1], op0=ALU.subtract, op1=ALU.mult),
              reads=[f"wun.{par}", f"usc.{par}"], writes=[f"wun.{par}"],
              tag=f"E.du{ic}")

        def e_trans(ic):
            par = ic % 2
            gg = 0
            for (nat, natk, dst, dstk) in ((wgnat, f"wgn.{par}", wgt_pb, "wgt"),
                                           (wunat, f"wun.{par}", wut_pb, "wut")):
                for g4 in range(4):
                    pk = ("PB0", "PB1", "P6e", "P7e")[gg % 4]
                    pt = (PB0, PB1, P6, P7)[gg % 4]
                    for k4 in range(4):
                        k = g4 * 4 + k4
                        E("tensor", lambda e, k=k, k4=k4, par=par, nat=nat, pt=pt:
                          e.matmul(pt[:, k4 * 128:(k4 + 1) * 128],
                                   nat(par)[:, k * 128:(k + 1) * 128],
                                   ident[:, :], start=True, stop=True),
                          reads=[natk, "ident"], writes=[pk], tag=f"E.T{ic}.{k}")
                    if gg % 2 == 0:
                        E("vector", lambda e, g4=g4, dst=dst, par=par, pt=pt: e.tensor_copy(
                            out=dst(par)[:, g4 * 512:(g4 + 1) * 512], in_=pt[:, :]),
                          reads=[pk], writes=[f"{dstk}.{par}.{g4}"], tag=f"E.TE{ic}.{g4}")
                    else:
                        E("scalar", lambda e, g4=g4, dst=dst, par=par, pt=pt: e.copy(
                            out=dst(par)[:, g4 * 512:(g4 + 1) * 512], in_=pt[:, :]),
                          reads=[pk], writes=[f"{dstk}.{par}.{g4}"], tag=f"E.TEs{ic}.{g4}")
                    gg += 1

        e_dma(0)
        e_deq(0)
        e_trans(0)
        for ic in range(64):
            par = ic % 2
            if ic + 1 < 64:
                e_dma(ic + 1)
                e_deq(ic + 1)
            wgtk = [f"wgt.{par}.{i}" for i in range(4)]
            wutk = [f"wut.{par}.{i}" for i in range(4)]
            for c in range(16):
                for th in range(2):
                    E("tensor", lambda e, c=c, th=th, par=par: e.matmul(
                        (PS0, PS1)[th][:, :], wgt_pb(par)[:, c * 128:(c + 1) * 128],
                        HBUF[:, c * 1024 + th * 512: c * 1024 + (th + 1) * 512],
                        start=(c == 0), stop=(c == 15)),
                      reads=wgtk + h1Tkeys, writes=[("PS0", "PS1")[th]],
                      tag=f"E.g{ic}.{c}.{th}")
            if ic + 1 < 64:
                e_trans(ic + 1)
            for c in range(16):
                for th in range(2):
                    E("tensor", lambda e, c=c, th=th, par=par: e.matmul(
                        (PO0, PO1)[th][:, :], wut_pb(par)[:, c * 128:(c + 1) * 128],
                        HBUF[:, c * 1024 + th * 512: c * 1024 + (th + 1) * 512],
                        start=(c == 0), stop=(c == 15)),
                      reads=wutk + h1Tkeys, writes=[("PO0", "PO1")[th]],
                      tag=f"E.u{ic}.{c}.{th}")
            for th in range(2):
                E("scalar", lambda e, th=th: e.activation(
                    out=silu_scr()[:, th * 512:(th + 1) * 512],
                    in_=(PS0, PS1)[th][:, :], func=AF.Silu),
                  reads=[("PS0", "PS1")[th]], writes=[f"silu.{th}"],
                  tag=f"E.si{ic}.{th}")
                E("vector", lambda e, ic=ic, th=th: e.tensor_tensor(
                    out=aT_view(ic)[:, th * 512:(th + 1) * 512],
                    in0=silu_scr()[:, th * 512:(th + 1) * 512],
                    in1=(PO0, PO1)[th][:, :], op=ALU.mult),
                  reads=[f"silu.{th}", ("PO0", "PO1")[th]],
                  writes=[aT_key(ic)], tag=f"E.a{ic}.{th}")

    def wdnat(hslot, par):
        return STRM_BF[:, (hslot * 2 + par) * 2048:(hslot * 2 + par + 1) * 2048]

    def wdT(par):
        return SCR3_BF[:, par * 256:(par + 1) * 256]

    def ztile(par):
        return SCR2[:, par * 256:(par + 1) * 256]

    def h1sl(par):
        return SCR2[:, 1024 + par * 256: 1024 + (par + 1) * 256]

    def ztile4(i):
        return SCR2[:, i * 256:(i + 1) * 256]

    def h1sl4(i):
        return SCR2[:, 1024 + i * 256: 1024 + (i + 1) * 256]

    if "DOWN" in PHASES:
        sch.fence()
        if DBG:
            for ic in range(64):
                DMA("scalar", lambda e, ic=ic: e.dma_start(
                    out=dbg_at[ic * 128:(ic + 1) * 128, :], in_=aT_view(ic)[:, :]),
                    reads=[aT_key(ic)], writes=[f"d:dbgat.{ic}"], tag=f"dbgat{ic}")

        def w_dma(hc, iq):
            par = iq % 2
            for hs in range(2):
                H = hc * 2 + hs
                DMA("gpsimd", lambda e, H=H, iq=iq, hs=hs, par=par: e.dma_start(
                    out=wdnat(hs, par)[:, :],
                    in_=dw_d[H * 128:(H + 1) * 128, iq * 2048:(iq + 1) * 2048]),
                    writes=[f"wdn.{hs}.{par}"], tag=f"W.dw{hc}.{iq}.{hs}")

        def w_deq(hc, iq):
            par = iq % 2
            for hs in range(2):
                E("gpsimd" if POOL_DEQ else "vector", lambda e, hs=hs, par=par: e.tensor_scalar(
                    out=wdnat(hs, par)[:, :], in0=wdnat(hs, par)[:, :],
                    scalar1=128.0, scalar2=dscol[hs][:, 0:1],
                    op0=ALU.subtract, op1=ALU.mult),
                  reads=[f"wdn.{hs}.{par}", f"dsc.{hs}"],
                  writes=[f"wdn.{hs}.{par}"], tag=f"W.dq{hc}.{iq}.{hs}")

        def w_trans(hc, gi):
            iq, c = gi // 16, gi % 16
            par, cp = iq % 2, gi % 2
            pbt = (PB0, PB1)[cp]
            for hs in range(2):
                E("tensor", lambda e, c=c, hs=hs, par=par, pbt=pbt: e.matmul(
                    pbt[:, hs * 128:(hs + 1) * 128],
                    wdnat(hs, par)[:, c * 128:(c + 1) * 128],
                    ident[:, :], start=True, stop=True),
                  reads=[f"wdn.{hs}.{par}", "ident"],
                  writes=[f"PBd.{cp}"], tag=f"W.T{hc}.{gi}.{hs}")
            E("vector", lambda e, cp=cp, pbt=pbt: e.tensor_copy(
                out=wdT(cp)[:, :], in_=pbt[:, 0:256]),
              reads=[f"PBd.{cp}"], writes=[f"wdT.{cp}"], tag=f"W.TE{hc}.{gi}")

        for hc in range(8):
            for hs in range(2):
                H = hc * 2 + hs
                DMA("sync", lambda e, H=H, hs=hs: e.dma_start(
                    out=dscol[hs][:, :],
                    in_=ds_d.rearrange("(p j) -> p j", j=1)[H * 128:(H + 1) * 128, :]),
                    writes=[f"dsc.{hs}"], tag=f"W.ds{hc}.{hs}")
            if hc == 0:
                w_dma(hc, 0)
            w_deq(hc, 0)
            w_trans(hc, 0)
            for bi in range(4):
                pt = (PS0, PS1, PO0, PO1)[bi]
                pn = ("PS0", "PS1", "PO0", "PO1")[bi]
                E("tensor", lambda e, pt=pt: e.matmul(
                    pt[:, :], zerorow[:, :], maskm[0:1, 0:512],
                    start=True, stop=True),
                  reads=["zerorow", "maskm"],
                  writes=[pn + ".0", pn + ".1"], tag=f"W.z0{hc}.{bi}")
            for gi in range(64):
                iq, c = gi // 16, gi % 16
                cp = gi % 2
                if c == 0 and iq + 1 < 4:
                    w_dma(hc, iq + 1)
                if c == 0 and iq == 3 and hc + 1 < 8:
                    w_dma(hc + 1, 0)
                if gi + 1 < 64:
                    if (gi + 1) % 16 == 0:
                        w_deq(hc, iq + 1)
                    w_trans(hc, gi + 1)
                for tc in range(8):
                    pt = (PS0, PS1, PO0, PO1)[tc // 2]
                    pk = ("PS0", "PS1", "PO0", "PO1")[tc // 2] + f".{tc % 2}"
                    E("tensor", lambda e, gi=gi, tc=tc, cp=cp, pt=pt: e.matmul(
                        pt[:, (tc % 2) * 256:(tc % 2 + 1) * 256],
                        aT_view(gi)[:, tc * 128:(tc + 1) * 128],
                        wdT(cp)[:, :], start=False, stop=(gi == 63)),
                      reads=[aT_key(gi), f"wdT.{cp}"], writes=[pk],
                      tag=f"W.y{hc}.{gi}.{tc}")
            for tc in range(8):
                DMA("sync", lambda e, tc=tc, hc=hc: e.dma_start(
                    out=h1sl4(tc % 4)[:, :],
                    in_=h1s_d[tc * 128:(tc + 1) * 128, hc * 256:(hc + 1) * 256]),
                    writes=[f"h1sl.{tc % 4}"], tag=f"W.h1{hc}.{tc}")
                pt = (PS0, PS1, PO0, PO1)[tc // 2]
                pk = ("PS0", "PS1", "PO0", "PO1")[tc // 2] + f".{tc % 2}"
                E("vector", lambda e, tc=tc, pt=pt: e.tensor_tensor(
                    out=ztile4(tc % 4)[:, :], in0=pt[:, (tc % 2) * 256:(tc % 2 + 1) * 256],
                    in1=h1sl4(tc % 4)[:, :], op=ALU.add),
                  reads=[pk, f"h1sl.{tc % 4}"], writes=[f"zt.{tc % 4}"],
                  tag=f"W.z{hc}.{tc}")
                DMA("scalar", lambda e, tc=tc, hc=hc: e.dma_start(
                    out=zs_d[tc * 128:(tc + 1) * 128, hc * 256:(hc + 1) * 256],
                    in_=ztile4(tc % 4)[:, :]),
                    reads=[f"zt.{tc % 4}"], writes=[f"d:zs.{tc}.{hc}"],
                    tag=f"W.zs{hc}.{tc}")

    if DBG:
        dbg_at = nc.dram_tensor("dbg_at", [IFF, TOK], BF16, kind="ExternalOutput").ap()
        dbg_h1 = nc.dram_tensor("dbg_h1", [TOK, D], F32, kind="ExternalOutput").ap()
        dbg_z = nc.dram_tensor("dbg_z", [TOK, D], F32, kind="ExternalOutput").ap()
        dbg_ot = nc.dram_tensor("dbg_ot", [2, 4, 128, S], BF16, kind="ExternalOutput").ap()

    if "FIN" in PHASES:
        zrow_f = [HBUF_F32[:, i * 2048:(i + 1) * 2048] for i in range(2)]
        sq_f = HBUF_F32[:, 2 * 2048:3 * 2048]
        yrow_f = HBUF_F32[:, 3 * 2048:4 * 2048]
        for tc in range(8):
            par = tc % 2
            DMA("sync", lambda e, tc=tc, par=par: e.dma_start(
                out=zrow_f[par][:, :], in_=zs_d[tc * 128:(tc + 1) * 128, :]),
                reads=[f"d:zs.{tc}.{hcv}" for hcv in range(8)],
                writes=[f"zrow.{par}", "reg:H"], tag=f"F.z{tc}")
            _ln(sch, par, zrow_f[par], sq_f, yrow_f, [f"zrow.{par}"], ["sqf"], ["yrow"],
                c_s1, c_m, c_v, c_t, c_r, c_s2, f"F.ln{tc}", epscol)
            DMA("scalar", lambda e, tc=tc: e.dma_start(
                out=yout_d[tc * 128:(tc + 1) * 128, :], in_=yrow_f[:, :]),
                reads=["yrow"], writes=[f"d:y.{tc}"], tag=f"F.y{tc}")

    sch.finalize()

    import contextlib
    with contextlib.ExitStack() as ctx:
        sems = {}
        for name in sch.finals:
            sems[name] = ctx.enter_context(nc.semaphore(name))
        block = ctx.enter_context(nc.Block())
        vclocks = {e: {} for e in ENGS}

        @block.sync
        def _(eng):
            sch.replay("sync", eng, sems, vclocks["sync"])
            for name, val in sorted(sch.finals.items()):
                if vclocks["sync"].get(name, 0) < val:
                    eng.wait_ge(sems[name], val)

        @block.scalar
        def _(eng):
            sch.replay("scalar", eng, sems, vclocks["scalar"])

        @block.vector
        def _(eng):
            sch.replay("vector", eng, sems, vclocks["vector"])

        @block.tensor
        def _(eng):
            sch.replay("tensor", eng, sems, vclocks["tensor"])

        @block.gpsimd
        def _(eng):
            sch.replay("gpsimd", eng, sems, vclocks["gpsimd"])

    return nc


def _ln(sch, par, row, sq, out, rowkeys, sqkeys, outkeys,
        c_s1, c_m, c_v, c_t, c_r, c_s2, tag, epscol=None):
    E = sch.emit
    N = 2048
    E("vector", lambda e: e.reduce_sum(out=c_s1[par][:, :], in_=row[:, :], axis=AX),
      reads=rowkeys, writes=[f"cs1.{par}"], tag=tag + ".s1")
    E("vector", lambda e: e.tensor_scalar(
        out=c_m[par][:, :], in0=c_s1[par][:, :], scalar1=1.0 / N, scalar2=None,
        op0=ALU.mult),
      reads=[f"cs1.{par}"], writes=[f"cm.{par}"], tag=tag + ".m")
    E("vector", lambda e: e.tensor_tensor(
        out=sq[:, :], in0=row[:, :], in1=row[:, :], op=ALU.mult),
      reads=rowkeys, writes=sqkeys, tag=tag + ".sq")
    E("vector", lambda e: e.reduce_sum(out=c_s2[par][:, :], in_=sq[:, :], axis=AX),
      reads=sqkeys, writes=[f"cs2.{par}"], tag=tag + ".s2")
    E("vector", lambda e: e.tensor_scalar(
        out=c_v[par][:, :], in0=c_s2[par][:, :], scalar1=1.0 / N, scalar2=None,
        op0=ALU.mult),
      reads=[f"cs2.{par}"], writes=[f"cv.{par}"], tag=tag + ".v")
    E("vector", lambda e: e.tensor_tensor(
        out=c_t[par][:, :], in0=c_m[par][:, :], in1=c_m[par][:, :], op=ALU.mult),
      reads=[f"cm.{par}"], writes=[f"ct.{par}"], tag=tag + ".m2")
    E("vector", lambda e: e.tensor_tensor(
        out=c_v[par][:, :], in0=c_v[par][:, :], in1=c_t[par][:, :], op=ALU.subtract),
      reads=[f"cv.{par}", f"ct.{par}"], writes=[f"cv.{par}"], tag=tag + ".var")
    E("scalar", lambda e: e.activation(
        out=c_t[par][:, :], in_=c_v[par][:, :], func=AF.Ln,
        bias=epscol[:, 0:1]),
      reads=[f"cv.{par}", "epscol"], writes=[f"ct.{par}"], tag=tag + ".ln")
    E("scalar", lambda e: e.activation(
        out=c_r[par][:, :], in_=c_t[par][:, :], func=AF.Exp, scale=-0.5),
      reads=[f"ct.{par}"], writes=[f"cr.{par}"], tag=tag + ".ex")
    E("vector", lambda e: e.tensor_scalar(
        out=out[:, :], in0=row[:, :], scalar1=c_m[par][:, 0:1],
        scalar2=c_r[par][:, 0:1], op0=ALU.subtract, op1=ALU.mult),
      reads=rowkeys + [f"cm.{par}", f"cr.{par}"], writes=outkeys, tag=tag + ".o")


_NC_CACHE = [None]
_LAST_RES = [None]


def kernel(L0_attn_x, L0_attn_w_in, L0_attn_b_in, L0_attn_w_out, L0_attn_b_out,
           L0_ffn_gate_w, L0_ffn_gate_scale, L0_ffn_up_w, L0_ffn_up_scale,
           L0_ffn_down_w, L0_ffn_down_scale):
    x = np.ascontiguousarray(np.asarray(L0_attn_x, dtype=np.float32))
    w_in = np.asarray(L0_attn_w_in, dtype=np.float32)
    b_in = np.asarray(L0_attn_b_in, dtype=np.float32)
    w_out = np.ascontiguousarray(np.asarray(L0_attn_w_out, dtype=np.float32))
    b_out = np.ascontiguousarray(np.asarray(L0_attn_b_out, dtype=np.float32))
    gw = np.ascontiguousarray(np.asarray(L0_ffn_gate_w, dtype=np.int32))
    gs = np.ascontiguousarray(np.asarray(L0_ffn_gate_scale, dtype=np.float32))
    uw = np.ascontiguousarray(np.asarray(L0_ffn_up_w, dtype=np.int32))
    us = np.ascontiguousarray(np.asarray(L0_ffn_up_scale, dtype=np.float32))
    dw = np.ascontiguousarray(np.asarray(L0_ffn_down_w, dtype=np.int32))
    ds = np.ascontiguousarray(np.asarray(L0_ffn_down_scale, dtype=np.float32))

    ident = np.eye(128, dtype=np.float32).astype(BFNP)
    tt, pp = np.meshgrid(np.arange(1024), np.arange(128), indexing="xy")
    maskm = (np.arange(1024)[None, :] >= 512 + np.arange(128)[:, None]).astype(np.float32).astype(BFNP)
    onesrowf = np.ones((1, 128), np.float32)
    onesrowb = np.ones((1, 128), np.float32).astype(BFNP)

    in_maps = []
    for c in range(NCORE):
        b, hg = c // 2, c % 2
        cs = hg * 1024
        m = {
            "xb": x[b],
            "wq": np.ascontiguousarray(w_in[:, cs:cs + 1024]),
            "wk": np.ascontiguousarray(w_in[:, 2048 + cs:2048 + cs + 1024]),
            "wv": np.ascontiguousarray(w_in[:, 4096 + cs:4096 + cs + 1024]),
            "bq": np.ascontiguousarray(b_in[cs:cs + 1024]),
            "bk": np.ascontiguousarray(b_in[2048 + cs:2048 + cs + 1024]),
            "bv": np.ascontiguousarray(b_in[4096 + cs:4096 + cs + 1024]),
            "wo": w_out, "bo": b_out,
            "gw": gw, "gs": gs, "uw": uw, "us": us, "dw": dw, "ds": ds,
            "xres": np.ascontiguousarray(x[b, hg * 1024:(hg + 1) * 1024]),
            "ident": ident, "maskm": maskm,
            "msku8": np.full((128, 1024), hg, np.uint8),
            "onesrowf": onesrowf, "onesrowb": onesrowb,
            "epscol": np.full((128, 1), EPS, np.float32),
            "zerorow": np.zeros((1, 128), np.float32).astype(BFNP),
            "onescolb": np.ones((128, 1), np.float32).astype(BFNP),
        }
        in_maps.append(m)

    if _NC_CACHE[0] is None:
        _NC_CACHE[0] = build_nc()
    nc = _NC_CACHE[0]
    res = run_bass_kernel_spmd(nc, in_maps, core_ids=list(range(NCORE)))
    _LAST_RES[0] = res

    out = np.empty((NB, S, D), np.float32)
    for c in range(NCORE):
        b, hg = c // 2, c % 2
        out[b, hg * 1024:(hg + 1) * 1024] = res.results[c]["yout"]
    return out


# version 20
# speedup vs baseline: 1.2257x; 1.0666x over previous
import math
import numpy as np
import ml_dtypes
import concourse.bass as bass
import concourse.mybir as mybir
from concourse.bass_utils import run_bass_kernel_spmd

BFNP = ml_dtypes.bfloat16
F32 = mybir.dt.float32
BF16 = mybir.dt.bfloat16
I32 = mybir.dt.int32
U8 = mybir.dt.uint8
AF = mybir.ActivationFunctionType
ALU = mybir.AluOpType
AX = mybir.AxisListType.X

D = 2048
S = 2048
NB = 4
NH = 16
DH = 128
HPC = 8
TOK = 1024
IFF = 8192
NCORE = 8
CSOFT = 1.0 / math.sqrt(DH)
EPS = 1e-5

PHASES = ("A", "B0", "BC", "D", "E", "DOWN", "FIN")
DBG = False
POOL_RS = False
POOL_DEQ = False

RG_PAIRS = [[0, 1], [2, 3], [4, 5], [6, 7]]

ENGS = ("sync", "scalar", "vector", "tensor", "gpsimd")


class Sched:

    def __init__(self):
        self.ops = []
        self.keys = {}
        self.fence_deps = []

    def emit(self, eng, fn, reads=(), writes=(), kind="compute", tag=""):
        opid = len(self.ops)
        deps = set(self.fence_deps)
        for k in reads:
            e = self.keys.get(k)
            if e is not None and e["w"] is not None:
                deps.add(e["w"])
        for k in writes:
            e = self.keys.get(k)
            if e is not None:
                if e["w"] is not None:
                    deps.add(e["w"])
                deps.update(e["r"])
        deps.discard(opid)
        self.ops.append({"id": opid, "eng": eng, "fn": fn, "kind": kind,
                         "deps": deps, "tag": tag})
        for k in reads:
            self.keys.setdefault(k, {"w": None, "r": []})["r"].append(opid)
        for k in writes:
            self.keys[k] = {"w": opid, "r": []}
        return opid

    def dma(self, eng, fn, reads=(), writes=(), tag=""):
        return self.emit(eng, fn, reads, writes, kind="dma", tag=tag)

    def fence(self, exclude=()):
        last = {}
        for op in self.ops:
            if op["eng"] in exclude and op["kind"] == "cc":
                continue
            last[(op["eng"], op["kind"] == "cc")] = op["id"]
        self.fence_deps = list(last.values())

    def finalize(self):
        for op in self.ops:
            if op["eng"] == "tensor":
                op["deps"] = {d for d in op["deps"]
                              if self.ops[d]["eng"] != "tensor"}
        for op in self.ops:
            byeng = {}
            keep = set()
            for d in op["deps"]:
                dop = self.ops[d]
                if dop["kind"] == "compute":
                    k = dop["eng"]
                    if k not in byeng or byeng[k] < d:
                        byeng[k] = d
                else:
                    keep.add(d)
            op["deps"] = keep | set(byeng.values())
        need = set()
        for op in self.ops:
            need.update(op["deps"])
        NP = 8
        cnt = {e: 0 for e in ENGS}
        dmacnt = {e: 0 for e in ENGS}
        ccn = 0
        for op in self.ops:
            if op["kind"] == "dma":
                i = dmacnt[op["eng"]]
                dmacnt[op["eng"]] += 1
                op["tok"] = ("q%s%d" % (op["eng"], i % NP), 16 * (i // NP + 1))
                op["inc"] = 16
            elif op["kind"] == "cc":
                ccn += 1
                op["tok"] = ("cc", ccn)
                op["inc"] = 1
            else:
                if op["id"] in need:
                    cnt[op["eng"]] += 1
                    op["tok"] = ("c" + op["eng"], cnt[op["eng"]])
                    op["inc"] = 1
                else:
                    op["tok"] = None
                    op["inc"] = 0
        self.finals = {}
        for e in ENGS:
            for s in range(NP):
                n = (dmacnt[e] - s + NP - 1) // NP if dmacnt[e] > s else 0
                if n:
                    self.finals["q%s%d" % (e, s)] = 16 * n
            if cnt[e]:
                self.finals["c" + e] = cnt[e]
        if ccn:
            self.finals["cc"] = ccn

    def replay(self, engname, eng, sems, vclock):
        for op in self.ops:
            if op["eng"] != engname:
                continue
            waits = {}
            for d in sorted(op["deps"]):
                tok = self.ops[d]["tok"]
                assert tok is not None, (op["tag"], self.ops[d]["tag"])
                sem, val = tok
                if waits.get(sem, 0) < val:
                    waits[sem] = val
            for sem, val in waits.items():
                if vclock.get(sem, 0) < val:
                    eng.wait_ge(sems[sem], val)
                    vclock[sem] = val
            inst = op["fn"](eng)
            if op["inc"]:
                sem, _ = op["tok"]
                inst.then_inc(sems[sem], op["inc"])
                if sem == "c" + engname:
                    pass


def build_nc():
    nc = bass.Bass("TRN2", target_bir_lowering=False, debug=False)

    xb_d = nc.dram_tensor("xb", [S, D], F32, kind="ExternalInput").ap()
    wq_d = nc.dram_tensor("wq", [D, TOK], F32, kind="ExternalInput").ap()
    wk_d = nc.dram_tensor("wk", [D, TOK], F32, kind="ExternalInput").ap()
    wv_d = nc.dram_tensor("wv", [D, TOK], F32, kind="ExternalInput").ap()
    bq_d = nc.dram_tensor("bq", [TOK], F32, kind="ExternalInput").ap()
    bk_d = nc.dram_tensor("bk", [TOK], F32, kind="ExternalInput").ap()
    bv_d = nc.dram_tensor("bv", [TOK], F32, kind="ExternalInput").ap()
    wo_d = nc.dram_tensor("wo", [D, D], F32, kind="ExternalInput").ap()
    bo_d = nc.dram_tensor("bo", [D], F32, kind="ExternalInput").ap()
    gw_d = nc.dram_tensor("gw", [IFF, D], I32, kind="ExternalInput").ap()
    gs_d = nc.dram_tensor("gs", [IFF], F32, kind="ExternalInput").ap()
    uw_d = nc.dram_tensor("uw", [IFF, D], I32, kind="ExternalInput").ap()
    us_d = nc.dram_tensor("us", [IFF], F32, kind="ExternalInput").ap()
    dw_d = nc.dram_tensor("dw", [D, IFF], I32, kind="ExternalInput").ap()
    ds_d = nc.dram_tensor("ds", [D], F32, kind="ExternalInput").ap()
    xres_d = nc.dram_tensor("xres", [TOK, D], F32, kind="ExternalInput").ap()
    ident_d = nc.dram_tensor("ident", [128, 128], BF16, kind="ExternalInput").ap()
    maskm_d = nc.dram_tensor("maskm", [128, 1024], BF16, kind="ExternalInput").ap()
    msku8_d = nc.dram_tensor("msku8", [128, 1024], U8, kind="ExternalInput").ap()
    onesrowf_d = nc.dram_tensor("onesrowf", [1, 128], F32, kind="ExternalInput").ap()
    onesrowb_d = nc.dram_tensor("onesrowb", [1, 128], BF16, kind="ExternalInput").ap()
    epscol_d = nc.dram_tensor("epscol", [128, 1], F32, kind="ExternalInput").ap()
    zerorow_d = nc.dram_tensor("zerorow", [1, 128], BF16, kind="ExternalInput").ap()
    onescolb_d = nc.dram_tensor("onescolb", [128, 1], BF16, kind="ExternalInput").ap()

    yout_d = nc.dram_tensor("yout", [TOK, D], F32, kind="ExternalOutput").ap()

    h1s_d = nc.dram_tensor("h1s", [TOK, D], F32, kind="Internal").ap()
    zs_d = nc.dram_tensor("zs", [TOK, D], F32, kind="Internal").ap()
    ccA_in = nc.dram_tensor("ccA_in", [4, 128, S], BF16, kind="Internal")
    ccB_in = nc.dram_tensor("ccB_in", [4, 128, S], BF16, kind="Internal")
    ccA_out = nc.dram_tensor("ccA_out", [2, 4, 128, S], BF16, kind="Internal")
    ccB_out = nc.dram_tensor("ccB_out", [2, 4, 128, S], BF16, kind="Internal")

    SBB = 16512
    off = [SBB]

    def sb(name, shape, dtype, at=None):
        nonlocal_off = off[0] if at is None else at
        t = nc.alloc_sbuf_tensor_at(name, shape, dtype, offset=nonlocal_off)
        if at is None:
            nbytes = shape[1] * {F32: 4, BF16: 2, I32: 4, U8: 1}[dtype]
            off[0] = (nonlocal_off + nbytes + 31) & ~31
        return t

    ident = sb("ident", [128, 128], BF16)
    maskm = sb("maskm", [128, 1024], BF16)
    msku8 = sb("msku8", [128, 1024], U8)
    onesrowf = sb("onesrowf", [1, 128], F32)
    onesrowb = sb("onesrowb", [1, 128], BF16)
    epscol = sb("epscol", [128, 1], F32)
    zerorow = sb("zerorow", [1, 128], BF16)
    onescolb = sb("onescolb", [128, 1], BF16)
    bqcol = [sb(f"bqcol{i}", [128, 1], F32) for i in range(2)]
    bkcol = [sb(f"bkcol{i}", [128, 1], F32) for i in range(2)]
    gscol = [sb(f"gscol{i}", [128, 1], F32) for i in range(2)]
    uscol = [sb(f"uscol{i}", [128, 1], F32) for i in range(2)]
    dscol = [sb(f"dscol{i}", [128, 1], F32) for i in range(2)]
    bvhi = sb("bvhi", [1, 1024], BF16)
    bvlo = sb("bvlo", [1, 1024], BF16)
    c_s1 = [sb(f"cs1_{i}", [128, 1], F32) for i in range(2)]
    c_m = [sb(f"cm_{i}", [128, 1], F32) for i in range(2)]
    c_v = [sb(f"cv_{i}", [128, 1], F32) for i in range(2)]
    c_t = [sb(f"ct_{i}", [128, 1], F32) for i in range(2)]
    c_r = [sb(f"cr_{i}", [128, 1], F32) for i in range(2)]
    c_s2 = [sb(f"cs2_{i}", [128, 1], F32) for i in range(2)]
    assert off[0] <= SBB + 9216, off[0]

    SCR1 = sb("SCR1", [128, 2048], F32, at=SBB + 9216)
    SCR1_BF = nc.alloc_sbuf_tensor_at("SCR1bf", [128, 4096], BF16, offset=SBB + 9216)
    SCR2 = sb("SCR2", [128, 2048], F32, at=SBB + 17408)
    SCR2_BF = nc.alloc_sbuf_tensor_at("SCR2bf", [128, 4096], BF16, offset=SBB + 17408)
    SCR3 = sb("SCR3", [128, 2048], F32, at=SBB + 25600)
    SCR3_BF = nc.alloc_sbuf_tensor_at("SCR3bf", [128, 4096], BF16, offset=SBB + 25600)
    STRM_BF = nc.alloc_sbuf_tensor_at("STRMbf", [128, 8192], BF16, offset=SBB + 33792)
    STRM_F32 = nc.alloc_sbuf_tensor_at("STRMf32", [128, 4096], F32, offset=SBB + 33792)
    XBUF = nc.alloc_sbuf_tensor_at("XBUF", [128, 32768], BF16, offset=SBB + 50176)
    VBUF = nc.alloc_sbuf_tensor_at("VBUF", [128, 16384], BF16, offset=SBB + 115712)
    HBUF = nc.alloc_sbuf_tensor_at("HBUF", [128, 16384], BF16, offset=SBB + 148480)
    HBUF_F32 = nc.alloc_sbuf_tensor_at("HBUFf32", [128, 8192], F32, offset=SBB + 148480)
    yoff = [SBB + 181248]

    def yb(name, shape, dtype):
        t = nc.alloc_sbuf_tensor_at(name, shape, dtype, offset=yoff[0])
        nbytes = shape[1] * {F32: 4, BF16: 2}[dtype] * (1 if shape[0] > 1 else 1)
        yoff[0] = (yoff[0] + nbytes + 31) & ~31
        return t

    pT = [yb(f"pT{i}", [128, 512], BF16) for i in range(5)]
    rs_part = yb("rs", [128, 512], F32)
    bc_sb = yb("bc", [128, 512], F32)
    rr_sb = yb("rr", [1, 512], F32)
    otr = [yb(f"otr{i}", [128, 512], BF16) for i in range(2)]
    bvf32 = yb("bvf32", [1, 1024], F32)
    ccraw = [nc.alloc_sbuf_tensor_at(f"ccraw{i}", [128, 2048], BF16,
                                     offset=SBB + 181248 + i * 4096) for i in range(3)]
    boutf32 = nc.alloc_sbuf_tensor_at("boutf32", [1, 2048], F32, offset=SBB + 181248 + 12288)
    h1bf = [nc.alloc_sbuf_tensor_at(f"h1bf{i}", [128, 2048], BF16,
                                    offset=SBB + 181248 + 12288 + 8192 + i * 4096) for i in range(2)]
    YAT = nc.alloc_sbuf_tensor_at("YAT", [128, 14336], BF16, offset=SBB + 181248)

    P = [nc.alloc_psum_tensor(f"P{i}", [128, 512], F32) for i in range(8)]
    PB0, PB1, PS0, PS1, PO0, PO1, P6, P7 = P

    def aT_view(ic):
        if ic < 32:
            return XBUF[:, ic * 1024:(ic + 1) * 1024]
        if ic < 48:
            return VBUF[:, (ic - 32) * 1024:(ic - 31) * 1024]
        if ic < 62:
            return YAT[:, (ic - 48) * 1024:(ic - 47) * 1024]
        return SCR1_BF[:, 2048 + (ic - 62) * 1024: 2048 + (ic - 61) * 1024]

    def aT_key(ic):
        return f"aT.{ic}"

    sch = Sched()
    E = sch.emit
    DMA = sch.dma

    DMA("sync", lambda e: e.dma_start(out=ident[:, :], in_=ident_d[:, :]),
        writes=["ident"], tag="c.ident")
    DMA("sync", lambda e: e.dma_start(out=maskm[:, :], in_=maskm_d[:, :]),
        writes=["maskm"], tag="c.maskm")
    DMA("sync", lambda e: e.dma_start(out=msku8[:, :], in_=msku8_d[:, :]),
        writes=["msku8"], tag="c.msk")
    DMA("sync", lambda e: e.dma_start(out=onesrowf[:, :], in_=onesrowf_d[:, :]),
        writes=["onesrowf"], tag="c.orf")
    DMA("sync", lambda e: e.dma_start(out=onesrowb[:, :], in_=onesrowb_d[:, :]),
        writes=["onesrowb"], tag="c.orb")
    DMA("sync", lambda e: e.dma_start(out=epscol[:, :], in_=epscol_d[:, :]),
        writes=["epscol"], tag="c.eps")
    DMA("sync", lambda e: e.dma_start(out=zerorow[:, :], in_=zerorow_d[:, :]),
        writes=["zerorow"], tag="c.zr")
    DMA("sync", lambda e: e.dma_start(out=onescolb[:, :], in_=onescolb_d[:, :]),
        writes=["onescolb"], tag="c.ocb")

    def xrow(par):
        return STRM_BF[:, par * 2048:(par + 1) * 2048]

    if "A" in PHASES:
        for tt in range(16):
            par = tt % 2
            DMA("gpsimd",
                lambda e, par=par, tt=tt: e.dma_start(
                    out=xrow(par)[:, :], in_=xb_d[tt * 128:(tt + 1) * 128, :]),
                reads=[], writes=[f"xrow.{par}"], tag=f"A.dma{tt}")
            for g in range(4):
                pk = ("PB0", "PB1")[g % 2]
                pt = (PB0, PB1)[g % 2]
                for k4 in range(4):
                    k = g * 4 + k4
                    E("tensor",
                      lambda e, par=par, k=k, k4=k4, pt=pt: e.matmul(
                          pt[:, k4 * 128:(k4 + 1) * 128],
                          xrow(par)[:, k * 128:(k + 1) * 128],
                          ident[:, :], start=True, stop=True),
                      reads=[f"xrow.{par}", "ident"], writes=[pk],
                      tag=f"A.T{tt}.{k}")
                for k4 in range(4):
                    k = g * 4 + k4
                    E("vector",
                      lambda e, k=k, k4=k4, tt=tt, pt=pt: e.tensor_copy(
                          out=XBUF[:, k * 2048 + tt * 128: k * 2048 + (tt + 1) * 128],
                          in_=pt[:, k4 * 128:(k4 + 1) * 128]),
                      reads=[pk], writes=[f"xT.{k}.{tt}"], tag=f"A.E{tt}.{k}")

    def xT(k):
        return XBUF[:, k * 2048:(k + 1) * 2048]

    def xT_keys(rng=range(16)):
        return [f"xT.{k}.{t}" for k in rng for t in range(16)]

    if "B0" in PHASES:
        DMA("sync", lambda e: e.dma_start(
            out=bvf32[:, :], in_=bv_d.rearrange("(p j) -> p j", p=1)),
            writes=["bvf32"], tag="B0.bv")
        E("scalar", lambda e: e.copy(out=bvhi[:, :], in_=bvf32[:, :]),
          reads=["bvf32"], writes=["bvhi"], tag="B0.bvhi")
        E("vector", lambda e: e.tensor_tensor(
            out=bvf32[:, :], in0=bvf32[:, :], in1=bvhi[:, :], op=ALU.subtract),
          reads=["bvf32", "bvhi"], writes=["bvf32"], tag="B0.bvsub")
        E("scalar", lambda e: e.copy(out=bvlo[:, :], in_=bvf32[:, :]),
          reads=["bvf32"], writes=["bvlo"], tag="B0.bvlo")
        DMA("gpsimd", lambda e: e.dma_start(
            out=HBUF[:, :].rearrange("p (k m) -> p k m", k=16),
            in_=wv_d.rearrange("(k p) m -> p k m", p=128)),
            writes=["wvbf"], tag="B0.wv")
        for tt in range(16):
            for vc in range(2):
                pk = ("P6", "P7")[(tt * 2 + vc) % 2]
                pt = (P6, P7)[(tt * 2 + vc) % 2]
                for k in range(16):
                    E("tensor",
                      lambda e, k=k, tt=tt, vc=vc, pt=pt: e.matmul(
                          pt[:, :],
                          XBUF[:, k * 2048 + tt * 128: k * 2048 + (tt + 1) * 128],
                          HBUF[:, k * 1024 + vc * 512: k * 1024 + (vc + 1) * 512],
                          start=(k == 0), stop=False),
                      reads=[f"xT.{k}.{tt}", "wvbf"], writes=[pk],
                      tag=f"B0.mm{tt}.{vc}.{k}")
                E("tensor", lambda e, vc=vc, pt=pt: e.matmul(
                    pt[:, :], onesrowb[:, :],
                    bvhi[:, vc * 512:(vc + 1) * 512], start=False, stop=False),
                  reads=["onesrowb", "bvhi"], writes=[pk], tag=f"B0.bh{tt}.{vc}")
                E("tensor", lambda e, vc=vc, pt=pt: e.matmul(
                    pt[:, :], onesrowb[:, :],
                    bvlo[:, vc * 512:(vc + 1) * 512], start=False, stop=True),
                  reads=["onesrowb", "bvlo"], writes=[pk], tag=f"B0.bl{tt}.{vc}")
                E("vector", lambda e, tt=tt, vc=vc, pt=pt: e.tensor_copy(
                    out=VBUF[:, tt * 1024 + vc * 512: tt * 1024 + (vc + 1) * 512],
                    in_=pt[:, :]),
                  reads=[pk], writes=[f"V.{tt}.{vc}"], tag=f"B0.ev{tt}.{vc}")

    def QT(par):
        return HBUF[:, par * 2048:(par + 1) * 2048]

    def KT(par):
        return HBUF[:, 4096 + par * 2048: 4096 + (par + 1) * 2048]

    def wqh(par):
        return STRM_BF[:, par * 2048:(par + 1) * 2048]

    def wkh(par):
        return STRM_BF[:, 4096 + par * 2048: 4096 + (par + 1) * 2048]

    if "BC" in PHASES:
        sch.fence()
        ptc = [0]

        def emit_wdma(h):
            par = h % 2
            DMA("gpsimd", lambda e, h=h, par=par: e.dma_start(
                out=wqh(par)[:, :].rearrange("p (k j) -> p k j", k=16),
                in_=wq_d.rearrange("(k p) m -> p k m", p=128)[:, :, h * 128:(h + 1) * 128]),
                writes=[f"wqh.{par}"], tag=f"B.wq{h}")
            DMA("gpsimd", lambda e, h=h, par=par: e.dma_start(
                out=wkh(par)[:, :].rearrange("p (k j) -> p k j", k=16),
                in_=wk_d.rearrange("(k p) m -> p k m", p=128)[:, :, h * 128:(h + 1) * 128]),
                writes=[f"wkh.{par}"], tag=f"B.wk{h}")
            DMA("sync", lambda e, h=h, par=par: e.dma_start(
                out=bqcol[par][:, :],
                in_=bq_d.rearrange("(p j) -> p j", j=1)[h * 128:(h + 1) * 128, :]),
                writes=[f"bqc.{par}"], tag=f"B.bq{h}")
            DMA("sync", lambda e, h=h, par=par: e.dma_start(
                out=bkcol[par][:, :],
                in_=bk_d.rearrange("(p j) -> p j", j=1)[h * 128:(h + 1) * 128, :]),
                writes=[f"bkc.{par}"], tag=f"B.bk{h}")

        emit_wdma(0)
        pend_norm = [None]
        rspar = [0]
        for h in range(HPC):
            par = h % 2
            if h + 1 < HPC and h + 1 not in (4, 5):
                emit_wdma(h + 1)
            for (wfn, wkey, dstfn, dkey, bcol, bkey) in (
                    (wqh, f"wqh.{par}", QT, f"QT.{par}", bqcol[par], f"bqc.{par}"),
                    (wkh, f"wkh.{par}", KT, f"KT.{par}", bkcol[par], f"bkc.{par}")):
                for tc in range(4):
                    pk = ("PB0", "PB1")[tc % 2]
                    pt = (PB0, PB1)[tc % 2]
                    for k in range(16):
                        E("tensor", lambda e, k=k, tc=tc, par=par, wfn=wfn, pt=pt: e.matmul(
                            pt[:, :], wfn(par)[:, k * 128:(k + 1) * 128],
                            XBUF[:, k * 2048 + tc * 512: k * 2048 + (tc + 1) * 512],
                            start=(k == 0), stop=(k == 15)),
                          reads=[wkey, f"xT.{k}.{4 * tc}", f"xT.{k}.{4 * tc + 1}",
                                 f"xT.{k}.{4 * tc + 2}", f"xT.{k}.{4 * tc + 3}"],
                          writes=[pk], tag=f"B.qk{h}.{tc}.{k}")
                    if tc % 2 == 0:
                        E("scalar", lambda e, tc=tc, par=par, dstfn=dstfn, bcol=bcol, pt=pt:
                          e.activation(out=dstfn(par)[:, tc * 512:(tc + 1) * 512],
                                       in_=pt[:, :], func=AF.Identity,
                                       bias=bcol[:, 0:1], scale=1.0),
                          reads=[pk, bkey], writes=[dkey], tag=f"B.ev{h}.{tc}")
                    else:
                        E("vector", lambda e, tc=tc, par=par, dstfn=dstfn, bcol=bcol, pt=pt:
                          e.tensor_scalar(out=dstfn(par)[:, tc * 512:(tc + 1) * 512],
                                          in0=pt[:, :], scalar1=bcol[:, 0:1],
                                          scalar2=None, op0=ALU.add),
                          reads=[pk, bkey], writes=[dkey], tag=f"B.evv{h}.{tc}")
                    if tc == 0 and pend_norm[0] is not None:
                        pend_norm[0]()
                        pend_norm[0] = None
            if h == 3:
                emit_wdma(4)
                emit_wdma(5)
            if h == 7 and "D" in PHASES:
                for g in range(16):
                    DMA("gpsimd", lambda e, g=g: e.dma_start(
                        out=XBUF[:, g * 2048:(g + 1) * 2048],
                        in_=wo_d[g * 128:(g + 1) * 128, :]),
                        writes=[f"wob.{g}"] + [f"xT.{g}.{t}" for t in range(16)],
                        tag=f"D.wo{g}")

            for qg in range(4):
                pok = ("PO0", "PO1")[qg % 2]
                pot = (PO0, PO1)[qg % 2]
                nkc = qg * 4 + 4
                psr = ("PS0", "PS1", "PB0", "PB1")
                pst_r = (PS0, PS1, PB0, PB1)
                ptis = [(ptc[0] + kc) % 5 for kc in range(nkc)]
                ptc[0] += nkc

                def emit_smm(kc, qg=qg, par=par, psr=psr, pst_r=pst_r):
                    E("tensor", lambda e, kc=kc, qg=qg, par=par,
                      pst=pst_r[kc % 4]: e.matmul(
                        pst[:, :], KT(par)[:, kc * 128:(kc + 1) * 128],
                        QT(par)[:, qg * 512:(qg + 1) * 512],
                        start=True, stop=True),
                      reads=[f"KT.{par}", f"QT.{par}"], writes=[psr[kc % 4]],
                      tag=f"C.s{h}.{qg}.{kc}")

                for kk in range(min(3, nkc)):
                    emit_smm(kk)
                for kc in range(nkc):
                    if kc == 3 and pend_norm[0] is not None:
                        pend_norm[0]()
                        pend_norm[0] = None
                    pti = ptis[kc]
                    psk = psr[kc % 4]
                    pst = pst_r[kc % 4]
                    E("scalar", lambda e, pti=pti, pst=pst: e.activation(
                        out=pT[pti][:, :], in_=pst[:, :], func=AF.Exp, scale=CSOFT),
                      reads=[psk], writes=[f"pT.{pti}"], tag=f"C.e{h}.{qg}.{kc}")
                    if kc >= qg * 4:
                        j = kc - qg * 4
                        E("vector", lambda e, pti=pti, j=j: e.tensor_tensor(
                            out=pT[pti][:, :], in0=pT[pti][:, :],
                            in1=maskm[:, 512 - j * 128: 1024 - j * 128], op=ALU.mult),
                          reads=[f"pT.{pti}", "maskm"], writes=[f"pT.{pti}"],
                          tag=f"C.m{h}.{qg}.{kc}")
                    if kc + 3 < nkc:
                        emit_smm(kc + 3)
                    E("tensor", lambda e, kc=kc, h=h, pti=pti, pot=pot, nkc=nkc: e.matmul(
                        pot[:, :], VBUF[:, kc * 1024 + h * 128: kc * 1024 + (h + 1) * 128],
                        pT[pti][:, :], start=(kc == 0), stop=(kc == nkc - 1)),
                      reads=[f"V.{kc}.0", f"V.{kc}.1", f"pT.{pti}"], writes=[pok],
                      tag=f"C.pv{h}.{qg}.{kc}")
                    E("tensor", lambda e, kc=kc, pti=pti, nkc=nkc,
                      rp=rspar[0]: e.matmul(
                        P6[32 * rp:32 * rp + 1, :], onescolb[:, :], pT[pti][:, :],
                        start=(kc == 0), stop=(kc == nkc - 1)),
                      reads=["onescolb", f"pT.{pti}"], writes=[f"P6r.{rspar[0]}"],
                      tag=f"C.rs{h}.{qg}.{kc}")

                def norm(h=h, qg=qg, pok=pok, pot=pot, rp=None):
                    rp = rspar[0]
                    E("vector", lambda e, rp=rp: e.reciprocal(
                        out=rr_sb[:, :], in_=P6[32 * rp:32 * rp + 1, :]),
                      reads=[f"P6r.{rp}"], writes=["rr"], tag=f"C.rcp{h}.{qg}")
                    E("tensor", lambda e: e.matmul(
                        P7[:, :], onesrowf[:, :], rr_sb[:, :], start=True, stop=True),
                      reads=["onesrowf", "rr"], writes=["P7b"], tag=f"C.bc{h}.{qg}")
                    E("vector", lambda e: e.tensor_copy(out=bc_sb[:, :], in_=P7[:, :]),
                      reads=["P7b"], writes=["bcs"], tag=f"C.bce{h}.{qg}")
                    E("vector", lambda e, qg=qg, pot=pot: e.tensor_tensor(
                        out=otr[qg % 2][:, :], in0=pot[:, :], in1=bc_sb[:, :], op=ALU.mult),
                      reads=[pok, "bcs"], writes=[f"otr.{qg % 2}"], tag=f"C.ot{h}.{qg}")
                    cc_in = (ccA_in if h < 4 else ccB_in)
                    DMA("scalar", lambda e, h=h, qg=qg, cc_in=cc_in: e.dma_start(
                        out=cc_in.ap()[h % 4, :, qg * 512:(qg + 1) * 512],
                        in_=otr[qg % 2][:, :]),
                        reads=[f"otr.{qg % 2}"],
                        writes=[f"d:cc{'A' if h < 4 else 'B'}.{h % 4}.{qg}"],
                        tag=f"C.cw{h}.{qg}")

                def norm_fix(norm=norm, rp=rspar[0]):
                    def g():
                        sv = rspar[0]
                        rspar[0] = rp
                        norm()
                        rspar[0] = sv
                    return g
                pend_norm[0] = norm_fix()
                rspar[0] ^= 1
            if h == 3:
                pend_norm[0]()
                pend_norm[0] = None
            if h == 3:
                E("gpsimd", lambda e: e.collective_compute(
                    "AllGather", ALU.bypass, replica_groups=RG_PAIRS,
                    ins=[ccA_in.ap().opt()], outs=[ccA_out.ap().opt()]),
                    reads=[f"d:ccA.{i}.{q}" for i in range(4) for q in range(4)],
                    writes=["d:ccoA"], kind="cc", tag="CCA")
            if h == 7:
                pend_norm[0]()
                pend_norm[0] = None
                E("gpsimd", lambda e: e.collective_compute(
                    "AllGather", ALU.bypass, replica_groups=RG_PAIRS,
                    ins=[ccB_in.ap().opt()], outs=[ccB_out.ap().opt()]),
                    reads=[f"d:ccB.{i}.{q}" for i in range(4) for q in range(4)],
                    writes=["d:ccoB"], kind="cc", tag="CCB")

    def h1T_view():
        return HBUF

    def wob(g):
        return XBUF[:, g * 2048:(g + 1) * 2048]

    def olh(g):
        return VBUF[:, g * 1024:(g + 1) * 1024]

    def xresb(par):
        return STRM_F32[:, par * 2048:(par + 1) * 2048]

    if "D" in PHASES:
        sch.fence()
        DMA("sync", lambda e: e.dma_start(
            out=boutf32[:, :], in_=bo_d.rearrange("(p j) -> p j", p=1)),
            writes=["bout"], tag="D.bo")
        GORD = [0, 1, 2, 3, 8, 9, 10, 11, 4, 5, 6, 7, 12, 13, 14, 15]
        for g in GORD:
            blk, hh = g // 8, g % 8
            cco = ccA_out if hh < 4 else ccB_out
            DMA("sync", lambda e, blk=blk, hh=hh, cco=cco, g=g: e.dma_start(
                out=ccraw[g % 3][:, :], in_=cco.ap()[blk, hh % 4, :, :]),
                reads=["d:ccoA" if hh < 4 else "d:ccoB"],
                writes=[f"ccraw.{g % 3}"], tag=f"D.cr{g}")
            E("vector", lambda e, g=g: e.select(
                out=olh(g)[:, :], mask=msku8[:, :],
                on_true=ccraw[g % 3][:, 1024:2048],
                on_false=ccraw[g % 3][:, 0:1024]),
              reads=[f"ccraw.{g % 3}", "msku8"], writes=[f"olh.{g}"],
              tag=f"D.sel{g}")
        def emit_h1T(tc):
            for g4 in range(4):
                pk = ("PB0", "PB1")[g4 % 2]
                pt = (PB0, PB1)[g4 % 2]
                for k4 in range(4):
                    k = g4 * 4 + k4
                    E("tensor", lambda e, k=k, k4=k4, tc=tc, pt=pt: e.matmul(
                        pt[:, k4 * 128:(k4 + 1) * 128],
                        h1bf[tc % 2][:, k * 128:(k + 1) * 128], ident[:, :],
                        start=True, stop=True),
                      reads=[f"h1bfk.{tc % 2}", "ident"], writes=[pk],
                      tag=f"D.hT{tc}.{k}")
                for k4 in range(4):
                    k = g4 * 4 + k4
                    E("vector", lambda e, k=k, k4=k4, tc=tc, pt=pt: e.tensor_copy(
                        out=HBUF[:, k * 1024 + tc * 128: k * 1024 + (tc + 1) * 128],
                        in_=pt[:, k4 * 128:(k4 + 1) * 128]),
                      reads=[pk], writes=[f"h1T.{k}.{tc}"], tag=f"D.hE{tc}.{k}")

        for tc in range(8):
            par = tc % 2
            DMA("sync", lambda e, tc=tc, par=par: e.dma_start(
                out=xresb(par)[:, :], in_=xres_d[tc * 128:(tc + 1) * 128, :]),
                writes=[f"xresb.{par}"], tag=f"D.xr{tc}")
            for ng in range(4):
                pk = ("PS0", "PS1", "PO0", "PO1")[ng]
                pt = (PS0, PS1, PO0, PO1)[ng]
                for gx, g in enumerate(GORD):
                    E("tensor", lambda e, g=g, tc=tc, ng=ng, pt=pt, gx=gx: e.matmul(
                        pt[:, :], olh(g)[:, tc * 128:(tc + 1) * 128],
                        wob(g)[:, ng * 512:(ng + 1) * 512],
                        start=(gx == 0), stop=False),
                      reads=[f"olh.{g}", f"wob.{g}"], writes=[pk],
                      tag=f"D.mm{tc}.{ng}.{g}")
                E("tensor", lambda e, ng=ng, pt=pt: e.matmul(
                    pt[:, :], onesrowf[:, :], boutf32[:, ng * 512:(ng + 1) * 512],
                    start=False, stop=True),
                  reads=["onesrowf", "bout"], writes=[pk], tag=f"D.bo{tc}.{ng}")
                E("vector", lambda e, ng=ng, par=par, pt=pt: e.tensor_tensor(
                    out=SCR1[:, ng * 512:(ng + 1) * 512], in0=pt[:, :],
                    in1=xresb(par)[:, ng * 512:(ng + 1) * 512], op=ALU.add),
                  reads=[pk, f"xresb.{par}"], writes=[f"hpre.{ng}"],
                  tag=f"D.hp{tc}.{ng}")
            hprek = [f"hpre.{i}" for i in range(4)]
            _ln(sch, par, SCR1, SCR3, SCR2, hprek, ["sq1"], ["h1f"],
                c_s1, c_m, c_v, c_t, c_r, c_s2, f"D.ln{tc}", epscol)
            if tc > 1:
                emit_h1T(tc - 2)
            E("scalar", lambda e, tc=tc: e.copy(out=h1bf[tc % 2][:, :], in_=SCR2[:, :]),
              reads=["h1f"], writes=[f"h1bfk.{tc % 2}"], tag=f"D.h1b{tc}")
            DMA("scalar", lambda e, tc=tc: e.dma_start(
                out=h1s_d[tc * 128:(tc + 1) * 128, :], in_=SCR2[:, :]),
                reads=["h1f"], writes=[f"d:h1s.{tc}"], tag=f"D.h1s{tc}")
        emit_h1T(6)
        emit_h1T(7)

    def wgnat(par):
        return STRM_BF[:, par * 2048:(par + 1) * 2048]

    def wunat(par):
        return STRM_BF[:, 4096 + par * 2048: 4096 + (par + 1) * 2048]

    def wgt_sb():
        return SCR3_BF[:, 0:2048]

    def wut_sb():
        return SCR3_BF[:, 2048:4096]

    def silu_scr():
        return SCR1[:, 0:1024]

    if "E" in PHASES:
        sch.fence()
        h1Tkeys = [f"h1T.{k}.{t}" for k in range(16) for t in range(8)] + ["reg:H"]

        def wgt_pb(par):
            return SCR3_BF[:, par * 2048:(par + 1) * 2048]

        def wut_pb(par):
            return SCR2_BF[:, par * 2048:(par + 1) * 2048]

        def e_dma(ic):
            par = ic % 2
            DMA("gpsimd", lambda e, ic=ic, par=par: e.dma_start(
                out=wgnat(par)[:, :], in_=gw_d[ic * 128:(ic + 1) * 128, :]),
                writes=[f"wgn.{par}"], tag=f"E.gw{ic}")
            DMA("gpsimd", lambda e, ic=ic, par=par: e.dma_start(
                out=wunat(par)[:, :], in_=uw_d[ic * 128:(ic + 1) * 128, :]),
                writes=[f"wun.{par}"], tag=f"E.uw{ic}")
            DMA("sync", lambda e, ic=ic, par=par: e.dma_start(
                out=gscol[par][:, :],
                in_=gs_d.rearrange("(p j) -> p j", j=1)[ic * 128:(ic + 1) * 128, :]),
                writes=[f"gsc.{par}"], tag=f"E.gs{ic}")
            DMA("sync", lambda e, ic=ic, par=par: e.dma_start(
                out=uscol[par][:, :],
                in_=us_d.rearrange("(p j) -> p j", j=1)[ic * 128:(ic + 1) * 128, :]),
                writes=[f"usc.{par}"], tag=f"E.us{ic}")

        def e_deq(ic):
            par = ic % 2
            deqeng = "gpsimd" if POOL_DEQ else "vector"
            E(deqeng, lambda e, par=par: e.tensor_scalar(
                out=wgnat(par)[:, :], in0=wgnat(par)[:, :], scalar1=128.0,
                scalar2=gscol[par][:, 0:1], op0=ALU.subtract, op1=ALU.mult),
              reads=[f"wgn.{par}", f"gsc.{par}"], writes=[f"wgn.{par}"],
              tag=f"E.dq{ic}")
            E(deqeng, lambda e, par=par: e.tensor_scalar(
                out=wunat(par)[:, :], in0=wunat(par)[:, :], scalar1=128.0,
                scalar2=uscol[par][:, 0:1], op0=ALU.subtract, op1=ALU.mult),
              reads=[f"wun.{par}", f"usc.{par}"], writes=[f"wun.{par}"],
              tag=f"E.du{ic}")

        def e_trans(ic):
            par = ic % 2
            gg = 0
            for (nat, natk, dst, dstk) in ((wgnat, f"wgn.{par}", wgt_pb, "wgt"),
                                           (wunat, f"wun.{par}", wut_pb, "wut")):
                for g4 in range(4):
                    pk = ("PB0", "PB1", "P6e", "P7e")[gg % 4]
                    pt = (PB0, PB1, P6, P7)[gg % 4]
                    for k4 in range(4):
                        k = g4 * 4 + k4
                        E("tensor", lambda e, k=k, k4=k4, par=par, nat=nat, pt=pt:
                          e.matmul(pt[:, k4 * 128:(k4 + 1) * 128],
                                   nat(par)[:, k * 128:(k + 1) * 128],
                                   ident[:, :], start=True, stop=True),
                          reads=[natk, "ident"], writes=[pk], tag=f"E.T{ic}.{k}")
                    if gg % 2 == 0:
                        E("vector", lambda e, g4=g4, dst=dst, par=par, pt=pt: e.tensor_copy(
                            out=dst(par)[:, g4 * 512:(g4 + 1) * 512], in_=pt[:, :]),
                          reads=[pk], writes=[f"{dstk}.{par}.{g4}"], tag=f"E.TE{ic}.{g4}")
                    else:
                        E("scalar", lambda e, g4=g4, dst=dst, par=par, pt=pt: e.copy(
                            out=dst(par)[:, g4 * 512:(g4 + 1) * 512], in_=pt[:, :]),
                          reads=[pk], writes=[f"{dstk}.{par}.{g4}"], tag=f"E.TEs{ic}.{g4}")
                    gg += 1

        e_dma(0)
        e_deq(0)
        e_trans(0)
        for ic in range(64):
            par = ic % 2
            if ic + 1 < 64:
                e_dma(ic + 1)
                e_deq(ic + 1)
            wgtk = [f"wgt.{par}.{i}" for i in range(4)]
            wutk = [f"wut.{par}.{i}" for i in range(4)]
            for c in range(16):
                for th in range(2):
                    E("tensor", lambda e, c=c, th=th, par=par: e.matmul(
                        (PS0, PS1)[th][:, :], wgt_pb(par)[:, c * 128:(c + 1) * 128],
                        HBUF[:, c * 1024 + th * 512: c * 1024 + (th + 1) * 512],
                        start=(c == 0), stop=(c == 15)),
                      reads=wgtk + h1Tkeys, writes=[("PS0", "PS1")[th]],
                      tag=f"E.g{ic}.{c}.{th}")
            if ic + 1 < 64:
                e_trans(ic + 1)
            for c in range(16):
                for th in range(2):
                    E("tensor", lambda e, c=c, th=th, par=par: e.matmul(
                        (PO0, PO1)[th][:, :], wut_pb(par)[:, c * 128:(c + 1) * 128],
                        HBUF[:, c * 1024 + th * 512: c * 1024 + (th + 1) * 512],
                        start=(c == 0), stop=(c == 15)),
                      reads=wutk + h1Tkeys, writes=[("PO0", "PO1")[th]],
                      tag=f"E.u{ic}.{c}.{th}")
            for th in range(2):
                E("scalar", lambda e, th=th: e.activation(
                    out=silu_scr()[:, th * 512:(th + 1) * 512],
                    in_=(PS0, PS1)[th][:, :], func=AF.Silu),
                  reads=[("PS0", "PS1")[th]], writes=[f"silu.{th}"],
                  tag=f"E.si{ic}.{th}")
                E("vector", lambda e, ic=ic, th=th: e.tensor_tensor(
                    out=aT_view(ic)[:, th * 512:(th + 1) * 512],
                    in0=silu_scr()[:, th * 512:(th + 1) * 512],
                    in1=(PO0, PO1)[th][:, :], op=ALU.mult),
                  reads=[f"silu.{th}", ("PO0", "PO1")[th]],
                  writes=[aT_key(ic)], tag=f"E.a{ic}.{th}")

    def wdnat(hslot, par):
        return STRM_BF[:, (hslot * 2 + par) * 2048:(hslot * 2 + par + 1) * 2048]

    def wdT(par):
        return SCR3_BF[:, par * 256:(par + 1) * 256]

    def ztile(par):
        return SCR2[:, par * 256:(par + 1) * 256]

    def h1sl(par):
        return SCR2[:, 1024 + par * 256: 1024 + (par + 1) * 256]

    def ztile4(i):
        return SCR2[:, i * 256:(i + 1) * 256]

    def h1sl4(i):
        return SCR2[:, 1024 + i * 256: 1024 + (i + 1) * 256]

    if "DOWN" in PHASES:
        sch.fence()
        if DBG:
            for ic in range(64):
                DMA("scalar", lambda e, ic=ic: e.dma_start(
                    out=dbg_at[ic * 128:(ic + 1) * 128, :], in_=aT_view(ic)[:, :]),
                    reads=[aT_key(ic)], writes=[f"d:dbgat.{ic}"], tag=f"dbgat{ic}")

        def w_dma(hc, iq):
            par = iq % 2
            for hs in range(2):
                H = hc * 2 + hs
                DMA("gpsimd", lambda e, H=H, iq=iq, hs=hs, par=par: e.dma_start(
                    out=wdnat(hs, par)[:, :],
                    in_=dw_d[H * 128:(H + 1) * 128, iq * 2048:(iq + 1) * 2048]),
                    writes=[f"wdn.{hs}.{par}"], tag=f"W.dw{hc}.{iq}.{hs}")

        def w_deq(hc, iq):
            par = iq % 2
            for hs in range(2):
                E("gpsimd" if POOL_DEQ else "vector", lambda e, hs=hs, par=par: e.tensor_scalar(
                    out=wdnat(hs, par)[:, :], in0=wdnat(hs, par)[:, :],
                    scalar1=128.0, scalar2=dscol[hs][:, 0:1],
                    op0=ALU.subtract, op1=ALU.mult),
                  reads=[f"wdn.{hs}.{par}", f"dsc.{hs}"],
                  writes=[f"wdn.{hs}.{par}"], tag=f"W.dq{hc}.{iq}.{hs}")

        def w_trans(hc, gi):
            iq, c = gi // 16, gi % 16
            par, cp = iq % 2, gi % 2
            pbt = (PB0, PB1)[cp]
            for hs in range(2):
                E("tensor", lambda e, c=c, hs=hs, par=par, pbt=pbt: e.matmul(
                    pbt[:, hs * 128:(hs + 1) * 128],
                    wdnat(hs, par)[:, c * 128:(c + 1) * 128],
                    ident[:, :], start=True, stop=True),
                  reads=[f"wdn.{hs}.{par}", "ident"],
                  writes=[f"PBd.{cp}"], tag=f"W.T{hc}.{gi}.{hs}")
            E("vector", lambda e, cp=cp, pbt=pbt: e.tensor_copy(
                out=wdT(cp)[:, :], in_=pbt[:, 0:256]),
              reads=[f"PBd.{cp}"], writes=[f"wdT.{cp}"], tag=f"W.TE{hc}.{gi}")

        for hc in range(8):
            for hs in range(2):
                H = hc * 2 + hs
                DMA("sync", lambda e, H=H, hs=hs: e.dma_start(
                    out=dscol[hs][:, :],
                    in_=ds_d.rearrange("(p j) -> p j", j=1)[H * 128:(H + 1) * 128, :]),
                    writes=[f"dsc.{hs}"], tag=f"W.ds{hc}.{hs}")
            if hc == 0:
                w_dma(hc, 0)
            w_deq(hc, 0)
            w_trans(hc, 0)
            for bi in range(4):
                pt = (PS0, PS1, PO0, PO1)[bi]
                pn = ("PS0", "PS1", "PO0", "PO1")[bi]
                E("tensor", lambda e, pt=pt: e.matmul(
                    pt[:, :], zerorow[:, :], maskm[0:1, 0:512],
                    start=True, stop=True),
                  reads=["zerorow", "maskm"],
                  writes=[pn + ".0", pn + ".1"], tag=f"W.z0{hc}.{bi}")
            for gi in range(64):
                iq, c = gi // 16, gi % 16
                cp = gi % 2
                if c == 0 and iq + 1 < 4:
                    w_dma(hc, iq + 1)
                if c == 0 and iq == 3 and hc + 1 < 8:
                    w_dma(hc + 1, 0)
                if (gi + 9) % 16 == 0 and iq + 1 < 4:
                    w_deq(hc, iq + 1)
                if gi + 1 < 64:
                    w_trans(hc, gi + 1)
                for tc in range(8):
                    pt = (PS0, PS1, PO0, PO1)[tc // 2]
                    pk = ("PS0", "PS1", "PO0", "PO1")[tc // 2] + f".{tc % 2}"
                    E("tensor", lambda e, gi=gi, tc=tc, cp=cp, pt=pt: e.matmul(
                        pt[:, (tc % 2) * 256:(tc % 2 + 1) * 256],
                        aT_view(gi)[:, tc * 128:(tc + 1) * 128],
                        wdT(cp)[:, :], start=False, stop=(gi == 63)),
                      reads=[aT_key(gi), f"wdT.{cp}"], writes=[pk],
                      tag=f"W.y{hc}.{gi}.{tc}")
            for tc in range(8):
                DMA("sync", lambda e, tc=tc, hc=hc: e.dma_start(
                    out=h1sl4(tc % 4)[:, :],
                    in_=h1s_d[tc * 128:(tc + 1) * 128, hc * 256:(hc + 1) * 256]),
                    writes=[f"h1sl.{tc % 4}"], tag=f"W.h1{hc}.{tc}")
                pt = (PS0, PS1, PO0, PO1)[tc // 2]
                pk = ("PS0", "PS1", "PO0", "PO1")[tc // 2] + f".{tc % 2}"
                E("vector", lambda e, tc=tc, pt=pt: e.tensor_tensor(
                    out=ztile4(tc % 4)[:, :], in0=pt[:, (tc % 2) * 256:(tc % 2 + 1) * 256],
                    in1=h1sl4(tc % 4)[:, :], op=ALU.add),
                  reads=[pk, f"h1sl.{tc % 4}"], writes=[f"zt.{tc % 4}"],
                  tag=f"W.z{hc}.{tc}")
                DMA("scalar", lambda e, tc=tc, hc=hc: e.dma_start(
                    out=zs_d[tc * 128:(tc + 1) * 128, hc * 256:(hc + 1) * 256],
                    in_=ztile4(tc % 4)[:, :]),
                    reads=[f"zt.{tc % 4}"], writes=[f"d:zs.{tc}.{hc}"],
                    tag=f"W.zs{hc}.{tc}")

    if DBG:
        dbg_at = nc.dram_tensor("dbg_at", [IFF, TOK], BF16, kind="ExternalOutput").ap()
        dbg_h1 = nc.dram_tensor("dbg_h1", [TOK, D], F32, kind="ExternalOutput").ap()
        dbg_z = nc.dram_tensor("dbg_z", [TOK, D], F32, kind="ExternalOutput").ap()
        dbg_ot = nc.dram_tensor("dbg_ot", [2, 4, 128, S], BF16, kind="ExternalOutput").ap()

    if "FIN" in PHASES:
        zrow_f = [HBUF_F32[:, i * 2048:(i + 1) * 2048] for i in range(2)]
        sq_f = HBUF_F32[:, 2 * 2048:3 * 2048]
        yrow_f = HBUF_F32[:, 3 * 2048:4 * 2048]
        for tc in range(8):
            par = tc % 2
            DMA("sync", lambda e, tc=tc, par=par: e.dma_start(
                out=zrow_f[par][:, :], in_=zs_d[tc * 128:(tc + 1) * 128, :]),
                reads=[f"d:zs.{tc}.{hcv}" for hcv in range(8)],
                writes=[f"zrow.{par}", "reg:H"], tag=f"F.z{tc}")
            _ln(sch, par, zrow_f[par], sq_f, yrow_f, [f"zrow.{par}"], ["sqf"], ["yrow"],
                c_s1, c_m, c_v, c_t, c_r, c_s2, f"F.ln{tc}", epscol)
            DMA("scalar", lambda e, tc=tc: e.dma_start(
                out=yout_d[tc * 128:(tc + 1) * 128, :], in_=yrow_f[:, :]),
                reads=["yrow"], writes=[f"d:y.{tc}"], tag=f"F.y{tc}")

    sch.finalize()

    import contextlib
    with contextlib.ExitStack() as ctx:
        sems = {}
        for name in sch.finals:
            sems[name] = ctx.enter_context(nc.semaphore(name))
        block = ctx.enter_context(nc.Block())
        vclocks = {e: {} for e in ENGS}

        @block.sync
        def _(eng):
            sch.replay("sync", eng, sems, vclocks["sync"])
            for name, val in sorted(sch.finals.items()):
                if vclocks["sync"].get(name, 0) < val:
                    eng.wait_ge(sems[name], val)

        @block.scalar
        def _(eng):
            sch.replay("scalar", eng, sems, vclocks["scalar"])

        @block.vector
        def _(eng):
            sch.replay("vector", eng, sems, vclocks["vector"])

        @block.tensor
        def _(eng):
            sch.replay("tensor", eng, sems, vclocks["tensor"])

        @block.gpsimd
        def _(eng):
            sch.replay("gpsimd", eng, sems, vclocks["gpsimd"])

    return nc


def _ln(sch, par, row, sq, out, rowkeys, sqkeys, outkeys,
        c_s1, c_m, c_v, c_t, c_r, c_s2, tag, epscol=None):
    E = sch.emit
    N = 2048
    E("vector", lambda e: e.reduce_sum(out=c_s1[par][:, :], in_=row[:, :], axis=AX),
      reads=rowkeys, writes=[f"cs1.{par}"], tag=tag + ".s1")
    E("vector", lambda e: e.tensor_scalar(
        out=c_m[par][:, :], in0=c_s1[par][:, :], scalar1=1.0 / N, scalar2=None,
        op0=ALU.mult),
      reads=[f"cs1.{par}"], writes=[f"cm.{par}"], tag=tag + ".m")
    E("scalar", lambda e: e.activation(
        out=sq[:, :], in_=row[:, :], func=AF.Square),
      reads=rowkeys, writes=sqkeys, tag=tag + ".sq")
    E("vector", lambda e: e.reduce_sum(out=c_s2[par][:, :], in_=sq[:, :], axis=AX),
      reads=sqkeys, writes=[f"cs2.{par}"], tag=tag + ".s2")
    E("vector", lambda e: e.tensor_scalar(
        out=c_v[par][:, :], in0=c_s2[par][:, :], scalar1=1.0 / N, scalar2=None,
        op0=ALU.mult),
      reads=[f"cs2.{par}"], writes=[f"cv.{par}"], tag=tag + ".v")
    E("vector", lambda e: e.tensor_tensor(
        out=c_t[par][:, :], in0=c_m[par][:, :], in1=c_m[par][:, :], op=ALU.mult),
      reads=[f"cm.{par}"], writes=[f"ct.{par}"], tag=tag + ".m2")
    E("vector", lambda e: e.tensor_tensor(
        out=c_v[par][:, :], in0=c_v[par][:, :], in1=c_t[par][:, :], op=ALU.subtract),
      reads=[f"cv.{par}", f"ct.{par}"], writes=[f"cv.{par}"], tag=tag + ".var")
    E("scalar", lambda e: e.activation(
        out=c_t[par][:, :], in_=c_v[par][:, :], func=AF.Ln,
        bias=epscol[:, 0:1]),
      reads=[f"cv.{par}", "epscol"], writes=[f"ct.{par}"], tag=tag + ".ln")
    E("scalar", lambda e: e.activation(
        out=c_r[par][:, :], in_=c_t[par][:, :], func=AF.Exp, scale=-0.5),
      reads=[f"ct.{par}"], writes=[f"cr.{par}"], tag=tag + ".ex")
    E("vector", lambda e: e.tensor_scalar(
        out=out[:, :], in0=row[:, :], scalar1=c_m[par][:, 0:1],
        scalar2=c_r[par][:, 0:1], op0=ALU.subtract, op1=ALU.mult),
      reads=rowkeys + [f"cm.{par}", f"cr.{par}"], writes=outkeys, tag=tag + ".o")


_NC_CACHE = [None]
_LAST_RES = [None]


def kernel(L0_attn_x, L0_attn_w_in, L0_attn_b_in, L0_attn_w_out, L0_attn_b_out,
           L0_ffn_gate_w, L0_ffn_gate_scale, L0_ffn_up_w, L0_ffn_up_scale,
           L0_ffn_down_w, L0_ffn_down_scale):
    x = np.ascontiguousarray(np.asarray(L0_attn_x, dtype=np.float32))
    w_in = np.asarray(L0_attn_w_in, dtype=np.float32)
    b_in = np.asarray(L0_attn_b_in, dtype=np.float32)
    w_out = np.ascontiguousarray(np.asarray(L0_attn_w_out, dtype=np.float32))
    b_out = np.ascontiguousarray(np.asarray(L0_attn_b_out, dtype=np.float32))
    gw = np.ascontiguousarray(np.asarray(L0_ffn_gate_w, dtype=np.int32))
    gs = np.ascontiguousarray(np.asarray(L0_ffn_gate_scale, dtype=np.float32))
    uw = np.ascontiguousarray(np.asarray(L0_ffn_up_w, dtype=np.int32))
    us = np.ascontiguousarray(np.asarray(L0_ffn_up_scale, dtype=np.float32))
    dw = np.ascontiguousarray(np.asarray(L0_ffn_down_w, dtype=np.int32))
    ds = np.ascontiguousarray(np.asarray(L0_ffn_down_scale, dtype=np.float32))

    ident = np.eye(128, dtype=np.float32).astype(BFNP)
    tt, pp = np.meshgrid(np.arange(1024), np.arange(128), indexing="xy")
    maskm = (np.arange(1024)[None, :] >= 512 + np.arange(128)[:, None]).astype(np.float32).astype(BFNP)
    onesrowf = np.ones((1, 128), np.float32)
    onesrowb = np.ones((1, 128), np.float32).astype(BFNP)

    in_maps = []
    for c in range(NCORE):
        b, hg = c // 2, c % 2
        cs = hg * 1024
        m = {
            "xb": x[b],
            "wq": np.ascontiguousarray(w_in[:, cs:cs + 1024]),
            "wk": np.ascontiguousarray(w_in[:, 2048 + cs:2048 + cs + 1024]),
            "wv": np.ascontiguousarray(w_in[:, 4096 + cs:4096 + cs + 1024]),
            "bq": np.ascontiguousarray(b_in[cs:cs + 1024]),
            "bk": np.ascontiguousarray(b_in[2048 + cs:2048 + cs + 1024]),
            "bv": np.ascontiguousarray(b_in[4096 + cs:4096 + cs + 1024]),
            "wo": w_out, "bo": b_out,
            "gw": gw, "gs": gs, "uw": uw, "us": us, "dw": dw, "ds": ds,
            "xres": np.ascontiguousarray(x[b, hg * 1024:(hg + 1) * 1024]),
            "ident": ident, "maskm": maskm,
            "msku8": np.full((128, 1024), hg, np.uint8),
            "onesrowf": onesrowf, "onesrowb": onesrowb,
            "epscol": np.full((128, 1), EPS, np.float32),
            "zerorow": np.zeros((1, 128), np.float32).astype(BFNP),
            "onescolb": np.ones((128, 1), np.float32).astype(BFNP),
        }
        in_maps.append(m)

    if _NC_CACHE[0] is None:
        _NC_CACHE[0] = build_nc()
    nc = _NC_CACHE[0]
    res = run_bass_kernel_spmd(nc, in_maps, core_ids=list(range(NCORE)))
    _LAST_RES[0] = res

    out = np.empty((NB, S, D), np.float32)
    for c in range(NCORE):
        b, hg = c // 2, c % 2
        out[b, hg * 1024:(hg + 1) * 1024] = res.results[c]["yout"]
    return out
